# version 04
# baseline (speedup 1.0000x reference)
import numpy as np
import ml_dtypes
from contextlib import ExitStack

import concourse.bass as bass
import concourse.mybir as mybir
from concourse.bass_utils import run_bass_kernel_spmd

F32 = mybir.dt.float32
BF16 = mybir.dt.bfloat16
AF = mybir.ActivationFunctionType
ALU = mybir.AluOpType
AX = mybir.AxisListType
BFNP = ml_dtypes.bfloat16

D = 1024
TCORE = 2048
NT = TCORE // 128
NE = D // 128
GROUPS = [[0, 1, 2, 3], [4, 5, 6, 7]]

QC = 128
UB = 512
NUB = TCORE // UB
NQC = UB // QC


class V:

    def __init__(self, ap):
        self._ap = ap

    def ap(self):
        return self._ap

    def __getitem__(self, k):
        return self._ap[k]


class Arena:

    def __init__(self, nc, name, nelem):
        self.t = nc.alloc_sbuf_tensor(name, [128, nelem], BF16)
        self.nelem = nelem
        self.cur = 0
        self.hi = 0

    def reset(self):
        self.cur = 0

    def take(self, shape, dtype):
        per = 1
        for s in shape[1:]:
            per *= s
        elems = per * (2 if dtype == F32 else 1)
        off = self.cur
        self.cur = (off + elems + 31) & ~31
        self.hi = max(self.hi, self.cur)
        assert self.cur <= self.nelem, f"arena overflow {self.cur} > {self.nelem}"
        ap = self.t.ap()[0:shape[0], off:off + elems]
        if dtype == F32:
            ap = ap.bitcast(F32)
        if len(shape) == 3:
            ap = ap.rearrange("p (a b) -> p a b", b=shape[2])
        elif len(shape) == 4:
            ap = ap.rearrange("p (a b c) -> p a b c", b=shape[2], c=shape[3])
        return V(ap)


def bcast_np(row, p=128):
    return np.ascontiguousarray(np.broadcast_to(np.asarray(row, np.float32), (p, len(row))))


class Ctr:
    def __init__(self, nc, name):
        self.sem = nc.alloc_semaphore(name=name)
        self.n = 0

    def now(self):
        return (self, self.n)


class PB:
    ENGINES = ("tensor", "vector", "scalar", "gpsimd", "sync")

    def __init__(self, nc):
        self.nc = nc
        self.seq = 0
        self.limit = None
        self.ops = {e: [] for e in self.ENGINES}
        self.pe = Ctr(nc, "c_pe")
        self.dve = Ctr(nc, "c_dve")
        self.act = Ctr(nc, "c_act")
        self.pool = Ctr(nc, "c_pool")
        self.dma = Ctr(nc, "c_dma")
        self.dmag = Ctr(nc, "c_dmag")
        self.cc = Ctr(nc, "c_cc")
        self._waited = {e: {} for e in self.ENGINES}

    def op(self, engine, fn, wait=(), inc=None, inck=1):
        selfc = {"vector": self.dve, "scalar": self.act, "gpsimd": self.pool}.get(engine)
        if selfc is not None and fn is not None:
            wait = list(wait) + [(selfc, selfc.n)]
        waits = []
        for w in wait:
            if isinstance(w, Ctr):
                w = (w, w.n)
            c, v = w
            if v > self._waited[engine].get(id(c), -1):
                waits.append((c, v))
                self._waited[engine][id(c)] = v
        self.seq += 1
        if self.limit is not None and self.seq > self.limit:
            return
        if inc is not None and fn is not None:
            inc.n += (1 if inck is None else inck)
        self.ops[engine].append((fn, waits, inc, inck))

    def barrier(self):
        wl = [c.now() for c in (self.pe, self.dve, self.act, self.pool,
                                self.dma, self.dmag, self.cc)]
        for e in self.ENGINES:
            self.op(e, None, wait=wl)

    def dmaop(self, engine, fn, wait=(), ctr=None):
        if ctr is None:
            ctr = self.dma if engine == "sync" else self.dmag
        self.op(engine, fn, wait=list(wait) + [ctr.now()], inc=ctr, inck=16)

    def replay(self, block):
        for engine in self.ENGINES:
            ops = self.ops[engine]
            if not ops:
                continue

            def body(eng, ops=ops):
                for fn, waits, inc, inck in ops:
                    for c, v in waits:
                        eng.wait_ge(c.sem, v)
                    if fn is not None:
                        instr = fn(eng)
                        if inc is not None:
                            if inck is None:
                                instr.then_inc(inc.sem)
                            else:
                                instr.then_inc(inc.sem, inck)
            getattr(block, engine)(body)


CP_EPS, CP_NEGH, CP_NEG30, CP_ONE = 0, 1, 2, 3
CPOOL_VALS = [1e-5, -0.5, -30.0, 1.0, 0.0, 0.0, 0.0, 0.0]


def emit_ln_tile(pb, src, dst, st, wait, dst2=None):
    cp = pb.cpool
    pb.op("vector", lambda v: v.bn_stats(st[:, 0:6], src[:, 0:512]), wait=wait, inc=pb.dve)
    pb.op("vector", lambda v: v.bn_stats(st[:, 6:12], src[:, 512:1024]), inc=pb.dve)
    pb.op("vector", lambda v: v.bn_aggr(st[:, 12:14], st[:, 0:12]),
          wait=[pb.dve.now()], inc=pb.dve)
    pb.op("scalar", lambda s: s.activation(st[:, 14:15], st[:, 13:14], AF.Ln,
                                           bias=cp[:, CP_EPS:CP_EPS + 1]),
          wait=[pb.dve.now()], inc=pb.act)
    pb.op("scalar", lambda s: s.activation(st[:, 14:15], st[:, 14:15], AF.Exp,
                                           scale=cp[:, CP_NEGH:CP_NEGH + 1]),
          wait=[pb.act.now()], inc=pb.act)
    pb.op("vector", lambda v: v.tensor_scalar(dst, src, st[:, 12:13], st[:, 14:15],
                                              ALU.subtract, ALU.mult),
          wait=[pb.act.now()], inc=pb.dve)
    if dst2 is not None:
        pb.op("vector", lambda v: v.tensor_scalar(dst2, src, st[:, 12:13], st[:, 14:15],
                                                  ALU.subtract, ALU.mult), inc=pb.dve)


def emit_mm_tiled(pb, out_ps, lhsT_fn, rhs_fn, nk, nn, wait, evict=None,
                  evict_engine="scalar", n_width=512):
    for n in range(nn):
        for i in range(nk):
            pb.op("tensor", lambda t, n=n, i=i: t.matmul(
                out_ps[:, n * n_width:(n + 1) * n_width], lhsT_fn(i), rhs_fn(i, n),
                start=(i == 0), stop=(i == nk - 1)),
                wait=(wait if (n == 0 and i == 0) else ()), inc=pb.pe)
    if evict is not None:
        pb.op(evict_engine, evict, wait=[pb.pe.now()],
              inc=(pb.act if evict_engine == "scalar" else pb.dve))


def emit_phase1(pb, T):
    nc = pb.nc
    XL = T["xln"]
    xln = XL.ap().rearrange("p (j f) -> p j f", f=D)
    readsT = XL.ap().rearrange("p (i t) -> p i t", t=TCORE)
    reads = T["reads"].ap().rearrange("p (j f) -> p j f", f=D)
    cp0 = T["cpool"]

    pb.cpool = T["cpool"]
    pb.dmaop("sync", lambda s: s.dma_start(out=T["cpool"][:, :], in_=T["cpool_c"]))
    pb.dmaop("sync", lambda s: s.dma_start(out=T["halo"][0:1, 0:D], in_=T["xhalo"]))
    pb.dmaop("sync", lambda s: s.dma_start(out=T["id_sb"][:, :], in_=T["ident"]))
    pb.dmaop("sync", lambda s: s.dma_start(out=T["idf_sb"][:, :], in_=T["identf"]))
    pb.dmaop("sync", lambda s: s.dma_start(out=T["gl"][:, :], in_=T["dec_sc"]))
    pb.dmaop("sync", lambda s: s.dma_start(out=T["al"][:, :], in_=T["la_sc"]))
    pb.dmaop("sync", lambda s: s.dma_start(out=T["g_upd"][:, :], in_=T["e_upd"]))
    pb.dmaop("sync", lambda s: s.dma_start(out=T["g_qb"][:, :], in_=T["e_qb"]))
    pb.dmaop("sync", lambda s: s.dma_start(out=T["g_glob"][:, :], in_=T["e_glob"]))
    pb.dmaop("sync", lambda s: s.dma_start(out=T["g_pw"][:, :], in_=T["pexp"]))
    pb.dmaop("sync", lambda s: s.dma_start(out=T["m_full"][:, 0, :], in_=T["m_tri"]))
    pb.dmaop("sync", lambda s: s.dma_start(out=T["m_full"][:, 1, :], in_=T["e_m1"]))
    pb.dmaop("sync", lambda s: s.dma_start(out=T["pw_mask"][:, :], in_=T["pmask"]))
    ld0 = [pb.dma.now()]

    pb.op("scalar", lambda s: s.activation(T["gl"][:, :], T["gl"][:, :], AF.Sigmoid),
          wait=ld0, inc=pb.act)
    pb.op("scalar", lambda s: s.activation(T["gl"][:, :], T["gl"][:, :], AF.Ln),
          wait=[pb.act.now()], inc=pb.act)
    pb.op("scalar", lambda s: s.activation(T["al"][:, :], T["al"][:, :], AF.Exp),
          inc=pb.act)
    wgl = [pb.act.now(), pb.dma.now()]
    for name in ("g_upd", "g_qb", "g_glob", "g_pw"):
        pb.op("vector", lambda v, name=name: v.tensor_scalar_mul(
            T[name][:, :], T[name][:, :], T["gl"][:, 0:1]), wait=wgl, inc=pb.dve)
    pb.op("vector", lambda v: v.tensor_scalar_mul(
        T["m_full"][:, 1, :], T["m_full"][:, 1, :], T["gl"][:, 0:1]), wait=wgl, inc=pb.dve)
    wv = [pb.dve.now()]
    for name in ("g_upd", "g_qb", "g_glob", "g_pw"):
        pb.op("scalar", lambda s, name=name: s.activation(
            T[name][:, :], T[name][:, :], AF.Exp), wait=wv, inc=pb.act)
    pb.op("scalar", lambda s: s.activation(T["m_full"][:, 1, :], T["m_full"][:, 1, :],
                                           AF.Exp), wait=wv, inc=pb.act)
    wa = [pb.act.now()]
    pb.op("vector", lambda v: v.tensor_mul(T["g_pw"][:, :], T["g_pw"][:, :],
                                           T["pw_mask"][:, :]), wait=wa, inc=pb.dve)
    pb.op("vector", lambda v: v.tensor_mul(T["m_full"][:, 0, :], T["m_full"][:, 0, :],
                                           T["m_full"][:, 1, :]), wait=wa, inc=pb.dve)
    pb.op("vector", lambda v: v.tensor_scalar_mul(T["gcol"][:, 0:1], T["gl"][:, 0:1],
                                                  128.0), wait=wgl, inc=pb.dve)
    pb.op("scalar", lambda s: s.activation(T["gcol"][:, 0:1], T["gcol"][:, 0:1], AF.Exp),
          wait=[pb.dve.now()], inc=pb.act)
    pb.op("vector", lambda v: v.tensor_mul(T["gcol"][:, 1:2], T["gcol"][:, 0:1],
                                           T["gcol"][:, 0:1]), wait=[pb.act.now()], inc=pb.dve)
    pb.op("vector", lambda v: v.tensor_mul(T["gcol"][:, 2:3], T["gcol"][:, 1:2],
                                           T["gcol"][:, 1:2]), wait=[pb.dve.now()], inc=pb.dve)
    pb.op("vector", lambda v: v.tensor_scalar_mul(T["m_full"][:, 1, :], T["m_full"][:, 1, :],
                                                  T["gcol"][:, 0:1]), wait=[pb.dve.now()],
          inc=pb.dve)
    pb.op("vector", lambda v: v.tensor_scalar_mul(T["m_full"][:, 2, :], T["m_full"][:, 1, :],
                                                  T["gcol"][:, 0:1]), wait=[pb.dve.now()],
          inc=pb.dve)
    pb.op("vector", lambda v: v.tensor_scalar_mul(T["m_full"][:, 3, :], T["m_full"][:, 1, :],
                                                  T["gcol"][:, 1:2]), inc=pb.dve)
    gamma_ready = [pb.dve.now(), pb.act.now()]

    for j in range(NT):
        pb.dmaop("sync", lambda s, j=j: s.dma_start(
            out=T["xt"][:, :], in_=T["x_in"][j * 128:(j + 1) * 128, :]),
            wait=[pb.dve.now()])
        emit_ln_tile(pb, T["xt"][:, :], xln[:, j, :], T["stats"][:, j, :],
                     [pb.dma.now()])
    xln_done = [pb.dve.now()]

    H = T["halo"]
    pb.op("vector", lambda v: v.bn_stats(H[0:1, D:D + 6], H[0:1, 0:512]), wait=ld0,
          inc=pb.dve)
    pb.op("vector", lambda v: v.bn_stats(H[0:1, D + 6:D + 12], H[0:1, 512:1024]),
          inc=pb.dve)
    pb.op("vector", lambda v: v.bn_aggr(H[0:1, D + 16:D + 18], H[0:1, D:D + 12]),
          wait=[pb.dve.now()], inc=pb.dve)
    pb.op("scalar", lambda s: s.activation(H[0:1, D + 18:D + 19], H[0:1, D + 17:D + 18],
                                           AF.Ln, bias=cp0[0:1, CP_EPS:CP_EPS + 1]),
          wait=[pb.dve.now()], inc=pb.act)
    pb.op("scalar", lambda s: s.activation(H[0:1, D + 18:D + 19], H[0:1, D + 18:D + 19],
                                           AF.Exp, scale=cp0[0:1, CP_NEGH:CP_NEGH + 1]),
          wait=[pb.act.now()], inc=pb.act)
    pb.op("vector", lambda v: v.tensor_scalar(H[0:1, 0:D], H[0:1, 0:D],
                                              H[0:1, D + 16:D + 17], H[0:1, D + 18:D + 19],
                                              ALU.subtract, ALU.mult),
          wait=[pb.act.now()], inc=pb.dve)
    halo_done = [pb.dve.now()]

    pb.op("scalar", lambda s: s.copy(T["zl"][0:1, :], H[0:1, 0:D]), wait=halo_done,
          inc=pb.act)
    pb.dmaop("gpsimd", lambda g: g.dma_start(out=T["wdram"].ap()[0:1, :], in_=H[0:1, 0:D]),
             wait=halo_done)
    for i in range(NE):
        pb.op("tensor", lambda t, i=i: t.transpose(
            T["ps_tr"][:, 0:1], T["zl"][0:1, i * 128:(i + 1) * 128],
            T["id_sb"][0:1, 0:1]), wait=[pb.act.now()], inc=pb.pe)
        pb.op("scalar", lambda s, i=i: s.copy(T["xlnT"][:, i, 0:1], T["ps_tr"][:, 0:1]),
              wait=[pb.pe.now()], inc=pb.act)
    pb.dmaop("gpsimd", lambda g: g.dma_start(
        out=T["wdram"].ap()[1:TCORE + 1, :].rearrange("(j p) f -> p j f", p=128),
        in_=xln), wait=xln_done)
    wdram_done = [pb.dmag.now()]

    for idx, (j, i) in enumerate([(j, i) for j in range(NT) for i in range(NE)]):
        pb.op("tensor", lambda t, j=j, i=i: t.transpose(
            T["ps_tr"][:, :], xln[:, j, i * 128:(i + 1) * 128], T["id_sb"][:, :]),
            wait=(xln_done + [pb.act.now()] if idx == 0 else [pb.act.now()]), inc=pb.pe)
        pb.op("scalar", lambda s, j=j, i=i: s.copy(
            T["xlnT"][:, i, 1 + j * 128:1 + (j + 1) * 128], T["ps_tr"][:, :]),
            wait=[pb.pe.now()], inc=pb.act)
    xlnT_done = [pb.act.now(), pb.dmag.now()]

    pb.dmaop("gpsimd", lambda g: g.dma_start(
        out=T["wbuf"][:, :, :], in_=T["WwT"].rearrange("(i p) f -> p i f", p=128)))
    wwt = [pb.dmag.now()]
    for j in range(NT):
        emit_mm_tiled(pb, T["ps_c"],
                      lambda i, j=j: T["xlnT"][:, i, 1 + j * 128:1 + (j + 1) * 128],
                      lambda i, n: T["wbuf"][:, i, n * 512:(n + 1) * 512],
                      NE, 2, xlnT_done + wwt + [pb.act.now()],
                      evict=lambda s, j=j: s.copy(T["v_sb"][:, j, :], T["ps_c"][:, :]))
    v_done = [pb.act.now()]

    pb.op("vector", lambda v: v.memset(T["wt_sb"][:, :, :], 0.0), inc=pb.dve)
    for ub in range(NUB):
        for qc in range(NQC):
            jt = ub * NQC + qc
            col0 = 1 + jt * 128
            emit_mm_tiled(pb, T["ps_b"],
                          lambda i, col0=col0: T["xlnT"][:, i, col0:col0 + 128],
                          lambda i, n: T["wt_sb"][:, i, n * 512:(n + 1) * 512],
                          NE, 2,
                          xlnT_done + v_done + gamma_ready +
                          [pb.dve.now(), pb.act.now()])
            inter_mm = pb.pe.now()
            for kc in range(qc + 1):
                kcol0 = 1 + (ub * NQC + kc) * 128
                for i in range(NE):
                    pb.op("tensor", lambda t, kcol0=kcol0, col0=col0, i=i: t.matmul(
                        T["ps_s"][:, :],
                        T["xlnT"][:, i, kcol0 - 1:kcol0 + 127],
                        T["xlnT"][:, i, col0:col0 + 128],
                        start=(i == 0), stop=(i == NE - 1)),
                        wait=([pb.dve.now()] if i == 0 else ()), inc=pb.pe)
                pb.op("vector", lambda v, d=qc - kc: v.tensor_tensor(
                    T["smt"][:, :], T["ps_s"][:, :], T["m_full"][:, d, :], ALU.mult),
                    wait=[pb.pe.now()], inc=pb.dve)
                kt = ub * NQC + kc
                for n in range(2):
                    pb.op("tensor", lambda t, kt=kt, n=n, kc=kc, qc=qc: t.matmul(
                        T["ps_a"][:, n * 512:(n + 1) * 512],
                        T["smt"][:, :],
                        T["v_sb"][:, kt, n * 512:(n + 1) * 512],
                        start=(kc == 0), stop=(kc == qc)),
                        wait=([pb.dve.now()] if n == 0 else ()), inc=pb.pe)
            pb.op("scalar", lambda s, qc=qc: s.activation(
                T["ev1"][:, :], T["ps_b"][:, :], AF.Copy,
                scale=T["g_qb"][:, qc:qc + 1]), wait=[pb.pe.now()], inc=pb.act)
            pb.op("vector", lambda v, jt=jt: v.tensor_add(
                reads[:, jt, :], T["ev1"][:, :], T["ps_a"][:, :]),
                wait=[pb.act.now()], inc=pb.dve)
        blk0 = ub * UB
        pb.dmaop("sync", lambda s, blk0=blk0: s.dma_start(
            out=T["wbuf"][:, 0:NQC, :],
            in_=T["wdram"].ap()[blk0:blk0 + UB, :].rearrange("(c p) f -> p c f", p=128)),
            wait=wdram_done + v_done + [pb.dve.now()])
        for c in range(NQC):
            pb.op("vector", lambda v, c=c: v.tensor_scalar_mul(
                T["wbuf"][:, c, :], T["wbuf"][:, c, :], T["g_upd"][:, c:c + 1]),
                wait=[pb.dma.now()], inc=pb.dve)
        wsc = [pb.dve.now()]
        for et in range(NE):
            for n in range(2):
                for c in range(NQC):
                    pb.op("tensor", lambda t, et=et, n=n, c=c, ub=ub: t.matmul(
                        T["ps_c"][:, n * 512:(n + 1) * 512],
                        T["wbuf"][:, c, et * 128:(et + 1) * 128],
                        T["v_sb"][:, ub * NQC + c, n * 512:(n + 1) * 512],
                        start=(c == 0), stop=(c == NQC - 1)),
                        wait=(wsc + [pb.dve.now()] if (n == 0 and c == 0) else ()),
                        inc=pb.pe)
            pb.op("vector", lambda v, et=et: v.scalar_tensor_tensor(
                T["wt_sb"][:, et, :], T["wt_sb"][:, et, :], T["gcol"][:, 2:3],
                T["ps_c"][:, :], ALU.mult, ALU.add), wait=[pb.pe.now()], inc=pb.dve)
    scan_done = [pb.dve.now()]

    pb.dmaop("gpsimd", lambda g: g.dma_start(
        out=T["ccA_in"].ap().rearrange("(i p) f -> p i f", p=128),
        in_=T["wt_sb"][:, :, :]), wait=scan_done)
    pb.op("gpsimd", lambda g: g.collective_compute(
        "AllGather", ALU.bypass, replica_groups=GROUPS,
        ins=[T["ccA_in"].ap().opt()], outs=[T["ccA_out"].ap().opt()]),
        wait=[pb.dmag.now()], inc=pb.cc, inck=None)
    win = T["wt_sb"]
    for j in range(4):
        pb.dmaop("sync", lambda s, j=j: s.dma_start(
            out=T["wbuf"][:, :, :],
            in_=T["ccA_out"].ap()[j * D:(j + 1) * D, :].rearrange("(i p) f -> p i f", p=128)),
            wait=[pb.cc.now(), pb.dve.now()])
        if j == 0:
            pb.op("vector", lambda v: v.tensor_scalar_mul(
                win[:, :, :], T["wbuf"][:, :, :], T["g_pw"][:, 0:1]),
                wait=[pb.dma.now()], inc=pb.dve)
        else:
            pb.op("vector", lambda v, j=j: v.scalar_tensor_tensor(
                win[:, :, :], T["wbuf"][:, :, :], T["g_pw"][:, j:j + 1],
                win[:, :, :], ALU.mult, ALU.add), wait=[pb.dma.now()], inc=pb.dve)
    win_done = [pb.dve.now()]

    for j in range(NT):
        emit_mm_tiled(pb, T["ps_c"],
                      lambda i, j=j: T["xlnT"][:, i, 1 + j * 128:1 + (j + 1) * 128],
                      lambda i, n: win[:, i, n * 512:(n + 1) * 512],
                      NE, 2, win_done + [pb.dve.now()])
        pb.op("vector", lambda v, j=j: v.scalar_tensor_tensor(
            reads[:, j, :], T["ps_c"][:, :], T["g_glob"][:, j:j + 1],
            reads[:, j, :], ALU.mult, ALU.add), wait=[pb.pe.now()], inc=pb.dve)
    ginter_done = [pb.dve.now()]

    for idx, (j, i) in enumerate([(j, i) for j in range(NT) for i in range(NE)]):
        pb.op("tensor", lambda t, j=j, i=i: t.transpose(
            T["ps_tr"][:, :], reads[:, j, i * 128:(i + 1) * 128], T["id_sb"][:, :]),
            wait=(ginter_done + [pb.act.now()] if idx == 0 else [pb.act.now()]), inc=pb.pe)
        pb.op("scalar", lambda s, j=j, i=i: s.copy(
            readsT[:, i, j * 128:(j + 1) * 128], T["ps_tr"][:, :]),
            wait=[pb.pe.now()], inc=pb.act)
    readsT_done = [pb.act.now()]

    pb.dmaop("gpsimd", lambda g: g.dma_start(
        out=T["wbuf"][:, :, :], in_=T["WrT"].rearrange("(i p) f -> p i f", p=128)),
        wait=win_done)
    wrt = [pb.dmag.now()]
    for j in range(NT):
        pb.dmaop("sync", lambda s, j=j: s.dma_start(
            out=T["xt"][:, :], in_=T["x_in"][j * 128:(j + 1) * 128, :]),
            wait=[pb.dve.now()])
        pb.dmaop("sync", lambda s, j=j: s.dma_start(
            out=T["zl"][:, :], in_=T["wdram"].ap()[1 + j * 128:1 + (j + 1) * 128, :]))
        emit_mm_tiled(pb, T["ps_c"],
                      lambda i, j=j: readsT[:, i, j * 128:(j + 1) * 128],
                      lambda i, n: T["wbuf"][:, i, n * 512:(n + 1) * 512],
                      NE, 2, readsT_done + wrt + [pb.dve.now()])
        pb.op("vector", lambda v: v.scalar_tensor_tensor(
            T["ev1"][:, :], T["ps_c"][:, :], T["al"][:, 0:1], T["zl"][:, :],
            ALU.mult, ALU.add), wait=[pb.pe.now(), pb.dma.now()], inc=pb.dve)
        emit_ln_tile(pb, T["ev1"][:, :], T["ev1"][:, :], T["stats"][:, j, :],
                     [pb.dve.now()])
        pb.op("vector", lambda v: v.tensor_add(T["xt"][:, :], T["xt"][:, :],
                                               T["ev1"][:, :]),
              wait=[pb.dve.now()], inc=pb.dve)
        pb.dmaop("sync", lambda s, j=j: s.dma_start(
            out=T["x_out"][j * 128:(j + 1) * 128, :], in_=T["xt"][:, :]),
            wait=[pb.dve.now()])
    pb.op("sync", None, wait=[pb.dma.now()])


ARENA_N = 106000


def psum_std(nc, T, split_ab, ps=None):
    if ps is None:
        ps = dict(ab=nc.alloc_psum_tensor("ps_ab", [128, 2048], F32),
                  c=nc.alloc_psum_tensor("ps_c", [128, D], F32),
                  s=nc.alloc_psum_tensor("ps_s", [128, 128], F32),
                  tr=nc.alloc_psum_tensor("ps_tr", [128, 128], BF16))
    if split_ab:
        T["ps_a"] = V(ps["ab"].ap()[:, 0:1024])
        T["ps_b"] = V(ps["ab"].ap()[:, 1024:2048])
    else:
        T["ps_ab"] = V(ps["ab"].ap())
    T["ps_c"] = V(ps["c"].ap())
    T["ps_s"] = V(ps["s"].ap())
    T["ps_tr"] = V(ps["tr"].ap())
    return ps


def decl_shared(nc, T):
    if "ident" in T:
        return
    T["ident"] = nc.inline_tensor(np.eye(128, dtype=BFNP), name="ident").ap()
    T["ones_c"] = nc.inline_tensor(np.ones((128, 512), dtype=BFNP),
                                   name="ones_c").ap()
    T["cpool_c"] = nc.inline_tensor(
        np.broadcast_to(np.asarray(CPOOL_VALS, np.float32), (128, 8)).copy(),
        name="cpool_c").ap()


HW = 514
NTX = 20
HIDT = 22


def p2_views(T):
    V = {}
    V["uT"] = T["uT"]
    av = T["av"].ap()
    V["vn"] = av.rearrange("p (a b) -> p a b", b=512)
    V["actT"] = V["vn"]
    qk = T["qk"].ap()
    V["qn"] = qk[:, 0:16 * 512].rearrange("p (a b) -> p a b", b=512)
    V["kn"] = qk[:, 16 * 512:36 * 512].rearrange("p (a b) -> p a b", b=512)
    V["o_n"] = V["qn"]
    V["oT"] = qk[:, 16 * 512:16 * 512 + 4 * 2048].rearrange("p (a b) -> p a b", b=2048)
    tq = T["tq"].ap()
    V["kT"] = tq[:, 0:4 * 2560].rearrange("p (a b) -> p a b", b=2560)
    V["qT"] = tq[:, 4 * 2560:4 * 2560 + 4 * 2048].rearrange("p (a b) -> p a b", b=2048)
    V["hln"] = tq[:, 0:8 * 2048].rearrange("p (a b) -> p a b", b=2048)
    wz = T["wz"].ap()
    V["wq3"] = wz[:, 0:8 * 512].rearrange("p (a b) -> p a b", b=512)
    V["wo_sb"] = wz[:, 0:4 * 1024].rearrange("p (a b) -> p a b", b=1024)
    V["hbuf"] = wz[:, 0:8 * 514].rearrange("p (a b) -> p a b", b=514)
    V["lastrow"] = wz[0:1, 0:1024]
    V["selrow"] = wz[0:1, 1024:1024 + 4 * 1024].rearrange("o (r f) -> o r f", f=D)
    V["w1d"] = wz[:, 0:2 * 2 * 8 * 128].rearrange("p (d ag i f) -> p d ag i f",
                                                  d=2, ag=2, i=8)
    off = 2 * 2 * 8 * 128
    V["w2d"] = wz[:, off:off + 2 * HIDT * 128].rearrange("p (d k f) -> p d k f",
                                                         d=2, k=HIDT)
    V["tbl"] = T["tbl"]
    return V


def emit_halo_ag(pb, T, first):
    w = [pb.dve.now(), pb.act.now()]
    pb.dmaop("gpsimd", lambda g: g.dma_start(
        out=T["ccH_in"].ap().rearrange("(i p) t -> p i t", p=128),
        in_=T["uT"][:, :, HW + TCORE - HW:HW + TCORE]), wait=w)
    pb.op("gpsimd", lambda g: g.collective_compute(
        "AllGather", ALU.bypass, replica_groups=GROUPS,
        ins=[T["ccH_in"].ap().opt()], outs=[T["ccH_out"].ap().opt()]),
        wait=[pb.dmag.now()], inc=pb.cc, inck=None)
    for r in range(4):
        pb.dmaop("sync", lambda s, r=r: s.dma_start(
            out=T["hbuf_v"],
            in_=T["ccH_out"].ap()[r * D:(r + 1) * D, :].rearrange("(i p) t -> p i t", p=128)),
            wait=[pb.cc.now()] + ([pb.dve.now()] if r > 0 else []))
        if r == 0:
            pb.op("vector", lambda v: v.tensor_scalar_mul(
                T["uT"][:, :, 0:HW], T["hbuf_v"], T["selprev"][:, 0:1]),
                wait=[pb.dma.now()], inc=pb.dve)
        else:
            pb.op("vector", lambda v, r=r: v.scalar_tensor_tensor(
                T["uT"][:, :, 0:HW], T["hbuf_v"], T["selprev"][:, r:r + 1],
                T["uT"][:, :, 0:HW], ALU.mult, ALU.add),
                wait=[pb.dma.now()], inc=pb.dve)


def emit_mid_halo(pb, T, V):
    for i in range(NE):
        pb.op("tensor", lambda t, i=i: t.transpose(
            T["ps_tr"][:, 0:1] if False else T["ps_tr"][0:1, 0:128],
            T["uT"][:, i, HW + TCORE - 1:HW + TCORE], T["id_sb"][:, :]),
            wait=[pb.dve.now(), pb.act.now()] if i == 0 else [pb.act.now()], inc=pb.pe)
        pb.op("scalar", lambda s, i=i: s.copy(V["lastrow"][0:1, i * 128:(i + 1) * 128],
                                              T["ps_tr"][0:1, 0:128]),
              wait=[pb.pe.now()], inc=pb.act)
    pb.dmaop("gpsimd", lambda g: g.dma_start(out=T["ccM_in"].ap(),
                                             in_=V["lastrow"][0:1, :]),
             wait=[pb.act.now()])
    pb.op("gpsimd", lambda g: g.collective_compute(
        "AllGather", ALU.bypass, replica_groups=GROUPS,
        ins=[T["ccM_in"].ap().opt()], outs=[T["ccM_out"].ap().opt()]),
        wait=[pb.dmag.now()], inc=pb.cc, inck=None)
    pb.dmaop("sync", lambda s: s.dma_start(out=V["selrow"][0:1, :, :],
                                           in_=T["ccM_out"].ap()), wait=[pb.cc.now()])
    pb.op("vector", lambda v: v.tensor_scalar_mul(
        V["lastrow"][0:1, :], V["selrow"][0:1, 0, :], T["selprev1"][0:1, 0:1]),
        wait=[pb.dma.now()], inc=pb.dve)
    for r in range(1, 4):
        pb.op("vector", lambda v, r=r: v.scalar_tensor_tensor(
            V["lastrow"][0:1, :], V["selrow"][0:1, r, :], T["selprev1"][0:1, r:r + 1],
            V["lastrow"][0:1, :], ALU.mult, ALU.add), inc=pb.dve)
    for i in range(NE):
        pb.op("tensor", lambda t, i=i: t.transpose(
            T["ps_tr"][:, 0:1], V["lastrow"][0:1, i * 128:(i + 1) * 128],
            T["id_sb"][0:1, 0:1]),
            wait=[pb.dve.now()] if i == 0 else [pb.act.now()], inc=pb.pe)
        pb.op("scalar", lambda s, i=i: s.copy(T["uT"][:, i, HW - 1:HW],
                                              T["ps_tr"][:, 0:1]),
              wait=[pb.pe.now()], inc=pb.act)


def emit_inner_layer(pb, T, l):
    V = p2_views(T)
    uT, vn, qn, kn = V["uT"], V["vn"], V["qn"], V["kn"]
    cp = pb.cpool

    emit_halo_ag(pb, T, l == 0)
    halo_done = [pb.dve.now()]

    for nm in ("qs", "ks", "lg", "lb", "b1a", "b1g", "b2"):
        pb.dmaop("sync", lambda s, nm=nm: s.dma_start(
            out=T[nm + "_sb"][:, :], in_=T[f"{nm}{l}"]))
    prm = [pb.dma.now()]

    for n3 in range(3):
        pb.dmaop("gpsimd", lambda g, l=l, n3=n3: g.dma_start(
            out=V["wq3"][:, :, :],
            in_=T[f"wqkv{l}"][:, n3 * 512:(n3 + 1) * 512]
            .rearrange("(i p) f -> p i f", p=128)),
            wait=[pb.pe.now()] + halo_done)
        wq = [pb.dmag.now()]
        dstbuf = (qn, kn, vn)[n3]
        for jx in range(NTX):
            c0 = 2 + jx * 128
            if n3 == 0 and jx < 4:
                continue
            ps = T["ps_a"][:, 0:512] if jx % 2 == 0 else T["ps_a"][:, 512:1024]
            w0 = halo_done + wq + [pb.act.now()]
            for i in range(NE):
                pb.op("tensor", lambda t, c0=c0, i=i, ps=ps: t.matmul(
                    ps, T["uT"][:, i, c0 - (0 if i < 4 else 1):
                                c0 - (0 if i < 4 else 1) + 128],
                    V["wq3"][:, i, :],
                    start=(i == 0), stop=(i == NE - 1)),
                    wait=(w0 if i == 0 else ()), inc=pb.pe)
            pb.op("scalar", lambda s, jx=jx, n3=n3, ps=ps, dstbuf=dstbuf: s.copy(
                dstbuf[:, jx - (4 if n3 == 0 else 0), :], ps),
                wait=[pb.pe.now()], inc=pb.act)
    qkv_done = [pb.act.now()]

    def norm_rot(buf, jx, tblc, tbls, gain, wait0):
        b3 = buf[:, jx, :].rearrange("p (h d) -> p h d", d=64)
        pb.op("vector", lambda v: v.tensor_mul(T["sq"][:, :], buf[:, jx, :], buf[:, jx, :]),
              wait=wait0, inc=pb.dve)
        pb.op("vector", lambda v: v.tensor_reduce(
            T["nrm"][:, 0:8].rearrange("p (h o) -> p h o", o=1),
            T["sq"][:, :].rearrange("p (h d) -> p h d", d=64), AX.X, ALU.add),
            wait=[pb.dve.now()], inc=pb.dve)
        pb.op("scalar", lambda s: s.activation(T["nrm"][:, 0:8], T["nrm"][:, 0:8], AF.Sqrt),
              wait=[pb.dve.now()], inc=pb.act)
        pb.op("vector", lambda v: v.tensor_scalar_max(T["nrm"][:, 0:8], T["nrm"][:, 0:8],
                                                      1e-12), wait=[pb.act.now()], inc=pb.dve)
        pb.op("vector", lambda v: v.reciprocal(T["nrm"][:, 0:8], T["nrm"][:, 0:8]),
              wait=[pb.dve.now()], inc=pb.dve)
        pb.op("vector", lambda v: v.tensor_tensor(
            b3, b3, T["nrm"][:, 0:8].rearrange("p (h o) -> p h o", o=1)
            .to_broadcast([128, 8, 64]), ALU.mult), wait=[pb.dve.now()], inc=pb.dve)
        pb.op("vector", lambda v: v.tensor_tensor(
            b3, b3, gain.unsqueeze(1).to_broadcast([128, 8, 64]), ALU.mult),
            wait=[pb.dve.now()], inc=pb.dve)
        pb.op("vector", lambda v: v.tensor_tensor(
            T["t1"][:, :].rearrange("p (h d) -> p h d", d=64), b3,
            tblc.unsqueeze(1).to_broadcast([128, 8, 64]), ALU.mult),
            wait=[pb.dve.now()], inc=pb.dve)
        pb.op("vector", lambda v: v.tensor_tensor(
            T["t2"][:, :].rearrange("p (h d) -> p h d", d=64), b3,
            tbls.unsqueeze(1).to_broadcast([128, 8, 64]), ALU.mult), inc=pb.dve)
        t1v = T["t1"][:, :].rearrange("p (h d) -> p h d", d=64)
        t2v = T["t2"][:, :].rearrange("p (h d) -> p h d", d=64)
        pb.op("vector", lambda v: v.tensor_sub(b3[:, :, 0:32], t1v[:, :, 0:32],
                                               t2v[:, :, 32:64]),
              wait=[pb.dve.now()], inc=pb.dve)
        pb.op("vector", lambda v: v.tensor_add(b3[:, :, 32:64], t1v[:, :, 32:64],
                                               t2v[:, :, 0:32]), inc=pb.dve)

    for jx in range(NTX):
        norm_rot(kn, jx, T["tbl"][:, 2, jx, :], T["tbl"][:, 3, jx, :],
                 T["ks_sb"][:, :], qkv_done + prm)
    for jq in range(NT):
        norm_rot(qn, jq, T["tbl"][:, 0, jq + 4, :], T["tbl"][:, 1, jq + 4, :],
                 T["qs_sb"][:, :], [])
    rot_done = [pb.dve.now()]

    for idx, (buf, n_t, dstv) in enumerate(((kn, NTX, V["kT"]), (qn, NT, V["qT"]))):
        for j in range(n_t):
            for fb in range(4):
                pb.op("tensor", lambda t, buf=buf, j=j, fb=fb: t.transpose(
                    T["ps_tr"][:, :], buf[:, j, fb * 128:(fb + 1) * 128], T["id_sb"][:, :]),
                    wait=(rot_done + [pb.act.now()]), inc=pb.pe)
                pb.op("scalar", lambda s, dstv=dstv, j=j, fb=fb: s.copy(
                    dstv[:, fb, j * 128:(j + 1) * 128], T["ps_tr"][:, :]),
                    wait=[pb.pe.now()], inc=pb.act)
    tq_done = [pb.act.now()]

    for qt in range(NT):
        for h in range(8):
            pbase = (h % 2) * 64
            fb = h // 2
            w0 = tq_done + [pb.act.now(), pb.dve.now()]
            for s5 in range(5):
                pb.op("tensor", lambda t, qt=qt, s5=s5, pbase=pbase, fb=fb: t.matmul(
                    T["ps_a"][:, s5 * 128:(s5 + 1) * 128],
                    V["kT"][pbase:pbase + 64, fb, (qt + s5) * 128:(qt + s5 + 1) * 128],
                    V["qT"][pbase:pbase + 64, fb, qt * 128:(qt + 1) * 128],
                    start=True, stop=True),
                    wait=(w0 if s5 == 0 else ()), inc=pb.pe)
            mw = [pb.pe.now()]
            for s5 in range(5):
                msk = (T["mskb"][:, qt, s5, :] if qt < 4 else T["msk5"][:, s5, :])
                pb.op("vector", lambda v, s5=s5, msk=msk: v.tensor_tensor(
                    T["ps_a"][:, s5 * 128:(s5 + 1) * 128],
                    T["ps_a"][:, s5 * 128:(s5 + 1) * 128], msk, ALU.add),
                    wait=(mw if s5 == 0 else ()), inc=pb.dve)
            pb.op("scalar", lambda s: s.activation(T["P_sb"][:, :], T["ps_a"][:, 0:640],
                                                   AF.Exp), wait=[pb.dve.now()], inc=pb.act)
            pw = [pb.act.now()]
            for s5 in range(5):
                pb.op("tensor", lambda t, qt=qt, s5=s5, h=h: t.matmul(
                    T["ps_s"][:, 0:64], T["P_sb"][:, s5 * 128:(s5 + 1) * 128],
                    vn[:, qt + s5, h * 64:(h + 1) * 64],
                    start=(s5 == 0), stop=(s5 == 4)),
                    wait=(pw if s5 == 0 else ()), inc=pb.pe)
                pb.op("tensor", lambda t, s5=s5: t.matmul(
                    T["ps_c"][:, 0:1], T["P_sb"][:, s5 * 128:(s5 + 1) * 128],
                    T["ones_sb"][:, 0:1], start=(s5 == 0), stop=(s5 == 4)),
                    wait=([pb.dve.now()] if s5 == 0 else ()), inc=pb.pe)
            pb.op("vector", lambda v: v.reciprocal(T["nrm"][:, 8:9], T["ps_c"][:, 0:1]),
                  wait=[pb.pe.now()], inc=pb.dve)
            pb.op("vector", lambda v, qt=qt, h=h: v.tensor_scalar_mul(
                V["o_n"][:, qt, h * 64:(h + 1) * 64], T["ps_s"][:, 0:64],
                T["nrm"][:, 8:9]), wait=[pb.dve.now()], inc=pb.dve)
    attn_done = [pb.dve.now()]

    for j in range(NT):
        for fb in range(4):
            pb.op("tensor", lambda t, j=j, fb=fb: t.transpose(
                T["ps_tr"][:, :], V["o_n"][:, j, fb * 128:(fb + 1) * 128], T["id_sb"][:, :]),
                wait=(attn_done + [pb.act.now()]), inc=pb.pe)
            pb.op("scalar", lambda s, j=j, fb=fb: s.copy(
                V["oT"][:, fb, j * 128:(j + 1) * 128], T["ps_tr"][:, :]),
                wait=[pb.pe.now()], inc=pb.act)
    oT_done = [pb.act.now()]

    pb.dmaop("gpsimd", lambda g, l=l: g.dma_start(
        out=V["wo_sb"][:, :, :],
        in_=T[f"wo{l}"].rearrange("k p m f -> p k (m f)", p=128)), wait=oT_done)
    wo_w = [pb.dmag.now()]
    for m in range(NE):
        for n in range(4):
            w0 = oT_done + wo_w + [pb.dve.now()]
            for k in range(4):
                pb.op("tensor", lambda t, m=m, n=n, k=k: t.matmul(
                    T["ps_c"][:, 0:512], V["wo_sb"][:, k, m * 128:(m + 1) * 128],
                    V["oT"][:, k, n * 512:(n + 1) * 512],
                    start=(k == 0), stop=(k == 3)),
                    wait=(w0 if k == 0 else ()), inc=pb.pe)
            pb.op("vector", lambda v, m=m, n=n: v.tensor_tensor(
                T["uT"][:, m, HW + n * 512:HW + (n + 1) * 512],
                T["uT"][:, m, HW + n * 512:HW + (n + 1) * 512],
                T["ps_c"][:, 0:512], ALU.add), wait=[pb.pe.now()], inc=pb.dve)
    wo_done = [pb.dve.now()]

    emit_mid_halo(pb, T, V)
    mh_done = [pb.act.now(), pb.dve.now()]

    def h2T(i, c0, width):
        off = HW + c0 - (0 if i < 4 else 1)
        return T["uT"][:, i, off:off + width]

    for n in range(4):
        w0 = mh_done + [pb.act.now(), pb.dve.now()]
        for i in range(NE):
            pb.op("tensor", lambda t, n=n, i=i: t.matmul(
                T["ps_b"][0:1, 0:512], T["ones_sb"][:, 0:1], h2T(i, n * 512, 512),
                start=(i == 0), stop=(i == NE - 1)),
                wait=(w0 if i == 0 else ()), inc=pb.pe)
        pb.op("scalar", lambda s, n=n: s.activation(
            T["h2sq"][:, 0:4, :],
            T["uT"][:, 0:4, HW + n * 512:HW + (n + 1) * 512], AF.Square),
            wait=mh_done + [pb.pe.now()], inc=pb.act)
        pb.op("scalar", lambda s, n=n: s.activation(
            T["h2sq"][:, 4:8, :],
            T["uT"][:, 4:8, HW + n * 512 - 1:HW + (n + 1) * 512 - 1], AF.Square),
            inc=pb.act)
        sq_w = [pb.act.now()]
        for i in range(NE):
            pb.op("tensor", lambda t, n=n, i=i: t.matmul(
                T["ps_b"][0:1, 512:1024], T["ones_sb"][:, 0:1], T["h2sq"][:, i, :],
                start=(i == 0), stop=(i == NE - 1)),
                wait=(sq_w if i == 0 else ()), inc=pb.pe)
        pb.op("vector", lambda v: v.tensor_scalar_mul(
            T["mv"][0:1, 0, :], T["ps_b"][0:1, 0:512], 1.0 / D),
            wait=[pb.pe.now()], inc=pb.dve)
        pb.op("vector", lambda v: v.tensor_scalar_mul(
            T["mv"][0:1, 1, :], T["ps_b"][0:1, 512:1024], 1.0 / D), inc=pb.dve)
        pb.op("vector", lambda v: v.tensor_mul(T["mv"][0:1, 2, :], T["mv"][0:1, 0, :],
                                               T["mv"][0:1, 0, :]),
              wait=[pb.dve.now()], inc=pb.dve)
        pb.op("vector", lambda v: v.tensor_sub(T["mv"][0:1, 1, :], T["mv"][0:1, 1, :],
                                               T["mv"][0:1, 2, :]),
              wait=[pb.dve.now()], inc=pb.dve)
        pb.op("scalar", lambda s: s.activation(T["mv"][0:1, 1, :], T["mv"][0:1, 1, :],
                                               AF.Ln, bias=cp[0:1, CP_EPS:CP_EPS + 1]),
              wait=[pb.dve.now()], inc=pb.act)
        pb.op("scalar", lambda s: s.activation(T["mv"][0:1, 1, :], T["mv"][0:1, 1, :],
                                               AF.Exp, scale=cp[0:1, CP_NEGH:CP_NEGH + 1]),
              wait=[pb.act.now()], inc=pb.act)
        pb.op("scalar", lambda s: s.copy(T["mvb"][0:1, 0, :], T["mv"][0:1, 0, :]),
              wait=[pb.act.now()], inc=pb.act)
        pb.op("scalar", lambda s: s.copy(T["mvb"][0:1, 1, :], T["mv"][0:1, 1, :]),
              inc=pb.act)
        mv_done = [pb.act.now()]
        for which, dst in ((0, "mub"), (1, "rsb")):
            pb.op("tensor", lambda t, which=which: t.matmul(
                T["ps_b"][:, 0:512], T["ones_sb"][0:1, 0:128],
                T["mvb"][0:1, which, :], start=True, stop=True),
                wait=mv_done + [pb.act.now()], inc=pb.pe)
            pb.op("scalar", lambda s, dst=dst: s.copy(T[dst][:, :], T["ps_b"][:, 0:512]),
                  wait=[pb.pe.now()], inc=pb.act)
        bc_done = [pb.act.now()]
        for i in range(NE):
            pb.op("vector", lambda v, i=i, n=n: v.tensor_tensor(
                V["hln"][:, i, n * 512:(n + 1) * 512], h2T(i, n * 512, 512),
                T["mub"][:, :], ALU.subtract),
                wait=bc_done + ([pb.pe.now()] if i == 0 else []), inc=pb.dve)
            pb.op("vector", lambda v, i=i, n=n: v.tensor_tensor(
                V["hln"][:, i, n * 512:(n + 1) * 512],
                V["hln"][:, i, n * 512:(n + 1) * 512], T["rsb"][:, :], ALU.mult),
                wait=[pb.dve.now()], inc=pb.dve)
            pb.op("vector", lambda v, i=i, n=n: v.tensor_scalar(
                V["hln"][:, i, n * 512:(n + 1) * 512],
                V["hln"][:, i, n * 512:(n + 1) * 512], T["lg_sb"][:, i:i + 1],
                T["lb_sb"][:, i:i + 1], ALU.mult, ALU.add),
                wait=[pb.dve.now()], inc=pb.dve)
    hln_done = [pb.dve.now()]

    for n in range(4):
        for m in range(HIDT):
            db = m % 2
            pb.dmaop("gpsimd", lambda g, l=l, m=m, db=db: g.dma_start(
                out=V["w1d"][:, db, 0, :, :],
                in_=T[f"w1a{l}"][m].rearrange("(i p) f -> p i f", p=128)),
                wait=[pb.pe.now()] if (n > 0 or m >= 2) else hln_done)
            pb.dmaop("gpsimd", lambda g, l=l, m=m, db=db: g.dma_start(
                out=V["w1d"][:, db, 1, :, :],
                in_=T[f"w1g{l}"][m].rearrange("(i p) f -> p i f", p=128)))
            w1w = [pb.dmag.now()]
            for ag in range(2):
                ps = T["ps_a"][:, ag * 512:(ag + 1) * 512]
                for i in range(NE):
                    pb.op("tensor", lambda t, n=n, i=i, ag=ag, db=db, ps=ps: t.matmul(
                        ps, V["w1d"][:, db, ag, i, :],
                        V["hln"][:, i, n * 512:(n + 1) * 512],
                        start=(i == 0), stop=(i == NE - 1)),
                        wait=(hln_done + w1w + [pb.dve.now(), pb.act.now()]
                              if (ag == 0 and i == 0) else ()), inc=pb.pe)
            pb.op("scalar", lambda s, m=m: s.activation(
                T["gsc"][:, :], T["ps_a"][:, 512:1024], AF.Gelu,
                bias=T["b1g_sb"][:, m:m + 1]), wait=[pb.pe.now(), pb.dve.now()],
                  inc=pb.act)
            pb.op("vector", lambda v, m=m: v.scalar_tensor_tensor(
                V["actT"][:, m, :], T["ps_a"][:, 0:512], T["b1a_sb"][:, m:m + 1],
                T["gsc"][:, :], ALU.add, ALU.mult), wait=[pb.act.now()], inc=pb.dve)
        act_done = [pb.dve.now()]
        for mo in range(NE):
            db = mo % 2
            pb.dmaop("gpsimd", lambda g, l=l, mo=mo, db=db: g.dma_start(
                out=V["w2d"][:, db, :, :],
                in_=T[f"w2{l}"][:, mo].rearrange("k p f -> p k f")),
                wait=[pb.pe.now()] if (n > 0 or mo >= 2) else act_done)
            w2w = [pb.dmag.now()]
            for k in range(HIDT):
                pb.op("tensor", lambda t, k=k, db=db: t.matmul(
                    T["ps_b"][:, 0:512], V["w2d"][:, db, k, :], V["actT"][:, k, :],
                    start=(k == 0), stop=(k == HIDT - 1)),
                    wait=(act_done + w2w + [pb.dve.now()] if k == 0 else ()), inc=pb.pe)
            pb.op("vector", lambda v, mo=mo, n=n: v.scalar_tensor_tensor(
                T["uT"][:, mo, HW + n * 512:HW + (n + 1) * 512], T["ps_b"][:, 0:512],
                T["b2_sb"][:, mo:mo + 1],
                T["uT"][:, mo, HW + n * 512:HW + (n + 1) * 512], ALU.add, ALU.add),
                wait=[pb.pe.now()], inc=pb.dve)


def emit_phase2(pb, T):
    cpv = pb.cpool
    pb.dmaop("sync", lambda s: s.dma_start(out=T["cpool"][:, :], in_=T["cpool_c"]))
    pb.dmaop("sync", lambda s: s.dma_start(out=T["id_sb"][:, :], in_=T["ident"]))
    pb.dmaop("sync", lambda s: s.dma_start(out=T["ones_sb"][:, :], in_=T["ones_c"][:, 0:128]))
    pb.dmaop("sync", lambda s: s.dma_start(out=T["selprev"][:, :], in_=T["selprev_i"]))
    pb.dmaop("sync", lambda s: s.dma_start(out=T["selprev1"][0:1, :], in_=T["selprev1_i"]))
    for tn, ti in (("tcs", 0), ("tss", 1), ("tcd", 2), ("tsd", 3)):
        pb.dmaop("gpsimd", lambda g, tn=tn, ti=ti: g.dma_start(
            out=T["tbl"][:, ti, :, :],
            in_=T[tn + "_i"].rearrange("(j p) f -> p j f", p=128)))
    pb.dmaop("gpsimd", lambda g: g.dma_start(
        out=T["msk5"][:, :, :], in_=T["mask5_i"].rearrange("s p f -> p s f")))
    pb.dmaop("gpsimd", lambda g: g.dma_start(
        out=T["mskb"][:, :, :, :], in_=T["maskb_i"].rearrange("q s p f -> p q s f")))
    ld0 = [pb.dma.now(), pb.dmag.now()]

    for j in range(NT):
        pb.dmaop("sync", lambda s, j=j: s.dma_start(
            out=T["xt"][:, :], in_=T["x_in"][j * 128:(j + 1) * 128, :]),
            wait=[pb.dve.now()])
        emit_ln_tile(pb, T["xt"][:, :], T["un"][:, :], T["stats"][:, 0, :],
                     ld0 + [pb.dma.now(), pb.act.now()])
        for i in range(NE):
            pb.op("tensor", lambda t, i=i: t.transpose(
                T["ps_tr"][:, :], T["un"][:, i * 128:(i + 1) * 128], T["id_sb"][:, :]),
                wait=[pb.dve.now(), pb.act.now()], inc=pb.pe)
            pb.op("scalar", lambda s, j=j, i=i: s.copy(
                T["uT"][:, i, HW + j * 128:HW + (j + 1) * 128], T["ps_tr"][:, :]),
                wait=[pb.pe.now()], inc=pb.act)

    emit_inner_layer(pb, T, 0)
    emit_inner_layer(pb, T, 1)

    for j in range(NT):
        for i in range(NE):
            pb.op("tensor", lambda t, j=j, i=i: t.transpose(
                T["ps_tr"][:, :], T["uT"][:, i, HW + j * 128:HW + (j + 1) * 128],
                T["id_sb"][:, :]),
                wait=[pb.dve.now(), pb.act.now()], inc=pb.pe)
            pb.op("scalar", lambda s, i=i: s.copy(
                T["un"][:, i * 128:(i + 1) * 128], T["ps_tr"][:, :]),
                wait=[pb.pe.now()], inc=pb.act)
        pb.dmaop("sync", lambda s, j=j: s.dma_start(
            out=T["xt"][:, :], in_=T["x_in"][j * 128:(j + 1) * 128, :]),
            wait=[pb.dve.now()])
        emit_ln_tile(pb, T["un"][:, :], T["un"][:, :], T["stats"][:, 0, :],
                     [pb.act.now()])
        pb.op("vector", lambda v: v.tensor_add(T["xt"][:, :], T["xt"][:, :],
                                               T["un"][:, :]),
              wait=[pb.dve.now(), pb.dma.now()], inc=pb.dve)
        pb.dmaop("sync", lambda s, j=j: s.dma_start(
            out=T["x_out"][j * 128:(j + 1) * 128, :], in_=T["xt"][:, :]),
            wait=[pb.dve.now()])
    pb.op("sync", None, wait=[pb.dma.now()])


def sbuf_p2(nc, T, ar, ps=None):
    sb = lambda n, shp, dt: T.__setitem__(n, ar.take(shp, dt))
    sb("uT", [128, NE, HW + TCORE], BF16)
    sb("av", [128, HIDT * 512], BF16)
    sb("qk", [128, 36 * 512], BF16)
    sb("tq", [128, 4 * 2560 + 4 * 2048], BF16)
    sb("wz", [128, 9728], BF16)
    sb("tbl", [128, 4, NTX, 64], BF16)
    sb("msk5", [128, 5, 128], BF16)
    sb("mskb", [128, 4, 5, 128], BF16)
    sb("P_sb", [128, 640], BF16)
    sb("sq", [128, 512], F32)
    sb("t1", [128, 512], BF16)
    sb("t2", [128, 512], BF16)
    sb("nrm", [128, 16], F32)
    sb("h2sq", [128, NE, 512], BF16)
    sb("mv", [1, 3, 512], F32)
    sb("mvb", [1, 2, 512], BF16)
    sb("mub", [128, 512], BF16)
    sb("rsb", [128, 512], BF16)
    sb("gsc", [128, 512], BF16)
    sb("selprev", [128, 4], F32)
    sb("selprev1", [1, 4], F32)
    sb("qs_sb", [128, 64], F32); sb("ks_sb", [128, 64], F32)
    sb("lg_sb", [128, 8], F32); sb("lb_sb", [128, 8], F32)
    sb("b1a_sb", [128, HIDT], F32); sb("b1g_sb", [128, HIDT], F32)
    sb("b2_sb", [128, 8], F32)
    sb("stats", [128, 1, 16], F32)
    sb("un", [128, D], BF16)
    sb("xt", [128, D], F32)
    sb("id_sb", [128, 128], BF16)
    sb("ones_sb", [128, 128], BF16)
    sb("cpool", [128, 8], F32)
    return psum_std(nc, T, split_ab=True, ps=ps)


def decls_p2(nc, T, xio=True):
    def inp(name, shape, dtype=F32):
        T[name] = nc.dram_tensor(name, list(shape), dtype, kind="ExternalInput").ap()

    if xio:
        inp("x_in", [TCORE, D])
        T["x_out"] = nc.dram_tensor("x_out", [TCORE, D], F32,
                                    kind="ExternalOutput").ap()
    inp("selprev_i", [128, 4]); inp("selprev1_i", [1, 4])
    inp("mask5_i", [5, 128, 128]); inp("maskb_i", [4, 5, 128, 128])
    for tn in ("tcs", "tss", "tcd", "tsd"):
        inp(tn + "_i", [NTX * 128, 64])
    for l in range(2):
        inp(f"wqkv{l}", [D, 1536]); inp(f"wo{l}", [4, 128, 8, 128])
        inp(f"w1a{l}", [HIDT, D, 128]); inp(f"w1g{l}", [HIDT, D, 128])
        inp(f"w2{l}", [HIDT, 8, 128, 128])
        inp(f"qs{l}", [128, 64]); inp(f"ks{l}", [128, 64])
        inp(f"lg{l}", [128, 8]); inp(f"lb{l}", [128, 8])
        inp(f"b1a{l}", [128, HIDT]); inp(f"b1g{l}", [128, HIDT])
        inp(f"b2{l}", [128, 8])
    T["ccH_in"] = nc.dram_tensor("ccH_in", [D, HW], BF16, kind="Internal")
    T["ccH_out"] = nc.dram_tensor("ccH_out", [4 * D, HW], BF16, kind="Internal")
    T["ccM_in"] = nc.dram_tensor("ccM_in", [1, D], BF16, kind="Internal")
    T["ccM_out"] = nc.dram_tensor("ccM_out", [4, D], BF16, kind="Internal")
    decl_shared(nc, T)


def build_p2(limit=None):
    nc = bass.Bass("TRN2", target_bir_lowering=False)
    T = {}
    decls_p2(nc, T)
    sbuf_p2(nc, T, Arena(nc, "G", ARENA_N))
    T["hbuf_v"] = T["wz"].ap()[:, 0:8 * HW].rearrange("p (a b) -> p a b", b=HW)

    pb = PB(nc)
    pb.limit = limit
    pb.cpool = T["cpool"][:, :]
    emit_phase2(pb, T)
    with nc.Block() as blk:
        pb.replay(blk)
    return nc


def host_inputs_p2(full, x_stream=None):
    DIM_HEAD, WINDOW = 64, 512
    half = np.arange(0, DIM_HEAD, 2, dtype=np.float64)
    inv = 1.0 / 10000.0 ** (half / DIM_HEAD)
    basev = (half + 0.4 * DIM_HEAD) / (1.4 * DIM_HEAD)

    wqkv = np.asarray(full["L0_ffn_wqkv"], np.float32)
    wo = np.asarray(full["L0_ffn_wo"], np.float32)
    w1 = np.asarray(full["L0_ffn_w1"], np.float32)
    w2 = np.asarray(full["L0_ffn_w2"], np.float32)
    qsg = np.asarray(full["L0_ffn_q_scale"], np.float32)
    ksg = np.asarray(full["L0_ffn_k_scale"], np.float32)
    lng = np.asarray(full["L0_ffn_ln_g"], np.float32)
    lnb = np.asarray(full["L0_ffn_ln_b"], np.float32)
    b1 = np.asarray(full["L0_ffn_b1"], np.float32)
    b2 = np.asarray(full["L0_ffn_b2"], np.float32)

    per_layer = {}
    for l in range(2):
        wov = wo[l].reshape(4, 128, 8, 128)
        w1a = np.zeros((D, HIDT * 128), np.float32); w1a[:, 0:2730] = w1[l][:, 0:2730]
        w1g = np.zeros((D, HIDT * 128), np.float32); w1g[:, 0:2730] = w1[l][:, 2730:5460]
        w1ar = w1a.reshape(D, HIDT, 128).transpose(1, 0, 2)
        w1gr = w1g.reshape(D, HIDT, 128).transpose(1, 0, 2)
        w2p = np.zeros((HIDT * 128, D), np.float32); w2p[0:2730] = w2[l]
        w2r = w2p.reshape(HIDT, 128, 8, 128).transpose(0, 2, 1, 3)
        b1a = np.zeros(HIDT * 128, np.float32); b1a[0:2730] = b1[l][0:2730]
        b1g = np.zeros(HIDT * 128, np.float32); b1g[0:2730] = b1[l][2730:5460]
        per_layer[l] = dict(
            wqkv=np.ascontiguousarray(wqkv[l]),
            wo=np.ascontiguousarray(wov),
            w1a=np.ascontiguousarray(w1ar), w1g=np.ascontiguousarray(w1gr),
            w2=np.ascontiguousarray(w2r),
            qs=bcast_np(qsg[l]), ks=bcast_np(ksg[l]),
            lg=np.ascontiguousarray(lng[l].reshape(8, 128).T),
            lb=np.ascontiguousarray(lnb[l].reshape(8, 128).T),
            b1a=np.ascontiguousarray(b1a.reshape(HIDT, 128).T),
            b1g=np.ascontiguousarray(b1g.reshape(HIDT, 128).T),
            b2=np.ascontiguousarray(b2[l].reshape(8, 128).T),
        )

    m = np.arange(128)
    base = np.zeros((5, 128, 128), np.float32)
    for s in range(5):
        mt = s * 128 + m
        allowed = (mt[:, None] >= np.arange(128)[None, :]) & \
                  (mt[:, None] <= np.arange(128)[None, :] + 512)
        base[s] = np.where(allowed, -30.0, -1e30)

    maps = []
    for c in range(8):
        b, q = c // 4, c % 4
        t0 = q * TCORE
        xs = (np.ascontiguousarray(x_stream[b, t0:t0 + TCORE])
              if x_stream is not None else None)
        selprev = np.zeros((128, 4), np.float32)
        selprev1 = np.zeros((1, 4), np.float32)
        if q > 0:
            selprev[:, q - 1] = 1.0
            selprev1[0, q - 1] = 1.0
        maskb = np.broadcast_to(base, (4, 5, 128, 128)).copy()
        if q == 0:
            for qt in range(4):
                for s in range(5):
                    tkey = (qt + s) * 128 + m - 512
                    maskb[qt, s][tkey < 0, :] = -1e30
        t = (t0 - 512 + np.arange(NTX * 128)).astype(np.float64)
        fr = t[:, None] * inv[None, :]
        power = (t - 8192 // 2) / float(WINDOW // 2)
        sc = basev[None, :] ** power[:, None]
        tcs = np.concatenate([np.cos(fr) * sc] * 2, axis=1).astype(np.float32)
        tss = np.concatenate([np.sin(fr) * sc] * 2, axis=1).astype(np.float32)
        tcd = np.concatenate([np.cos(fr) / sc] * 2, axis=1).astype(np.float32)
        tsd = np.concatenate([np.sin(fr) / sc] * 2, axis=1).astype(np.float32)
        mp = dict(selprev_i=selprev, selprev1_i=selprev1,
                  mask5_i=base, maskb_i=maskb,
                  tcs_i=tcs, tss_i=tss, tcd_i=tcd, tsd_i=tsd)
        if x_stream is not None:
            mp["x_in"] = xs
        for l in range(2):
            for k, v in per_layer[l].items():
                mp[f"{k}{l}"] = v
        maps.append(mp)
    return maps


NKB = 16
NQB = 16


def emit_phase3(pb, T):
    cp = pb.cpool
    uT = T["a1"].ap().rearrange("p (i t) -> p i t", t=TCORE)
    wband = T["a1"].ap().rearrange("p (i t) -> p i t", t=TCORE)
    krp = T["a1"].ap()[:, 0:8192]
    krm = T["a1"].ap()[:, 8192:16384]
    qT_loc = T["a2"].ap().rearrange("p (i t) -> p i t", t=TCORE)
    qTm = T["a2"].ap().rearrange("p (i g t) -> p i g t", i=2, g=4)
    kvT = T["a3"].ap().rearrange("p (i t) -> p i t", t=TCORE)
    oT = T["a3"].ap().rearrange("p (i t) -> p i t", t=8192)
    latT = T["latT"].ap().rearrange("p (i t) -> p i t", t=8192)
    krT = T["krT"].ap()
    vrec = T["vrec"].ap().rearrange("p (k w) -> p k w", w=65)

    pb.dmaop("sync", lambda s: s.dma_start(out=T["cpool"][:, :], in_=T["cpool_c"]))
    pb.dmaop("sync", lambda s: s.dma_start(out=T["id_sb"][:, :], in_=T["ident"]))
    pb.dmaop("sync", lambda s: s.dma_start(out=T["ones_sb"][:, :], in_=T["ones_c"]))
    for nm in ("rowqp", "rowqm", "rowkp", "rowkm"):
        pb.dmaop("gpsimd", lambda g, nm=nm: g.dma_start(out=T[nm + "_sb"][:, :],
                                                        in_=T[nm + "_i"]))
    for nm in ("creltab", "dconst", "selg"):
        pb.dmaop("sync", lambda s, nm=nm: s.dma_start(out=T[nm + "_sb"][:, :],
                                                      in_=T[nm + "_i"]))
    ld0 = [pb.dma.now()]

    for j in range(NT):
        pb.dmaop("sync", lambda s, j=j: s.dma_start(
            out=T["xt"][:, :], in_=T["x_in"][j * 128:(j + 1) * 128, :]),
            wait=[pb.dve.now()])
        emit_ln_tile(pb, T["xt"][:, :], T["un"][:, :], T["stats"][:, :],
                     ld0 + [pb.dma.now(), pb.act.now()])
        for i in range(NE):
            pb.op("tensor", lambda t, i=i: t.transpose(
                T["ps_tr"][:, :], T["un"][:, i * 128:(i + 1) * 128], T["id_sb"][:, :]),
                wait=[pb.dve.now(), pb.act.now()], inc=pb.pe)
            pb.op("scalar", lambda s, j=j, i=i: s.copy(
                uT[:, i, j * 128:(j + 1) * 128], T["ps_tr"][:, :]),
                wait=[pb.pe.now()], inc=pb.act)
    xlnT_done = [pb.act.now()]

    def proj(wname, out_view, acc_with=None, done_wait=()):
        pb.dmaop("gpsimd", lambda g: g.dma_start(
            out=T["wbuf"][:, :, :],
            in_=T[wname].rearrange("(i p) f -> p i f", p=128)),
            wait=list(done_wait))
        ww = [pb.dmag.now()]
        for m in range(NE):
            for n in range(4):
                first = not acc_with
                ps = T["ps_ab"][:, ((m + n) % 2) * 512:((m + n) % 2) * 512 + 512]
                for i in range(NE):
                    pb.op("tensor", lambda t, m=m, n=n, i=i, ps=ps: t.matmul(
                        ps, T["wbuf"][:, i, m * 128:(m + 1) * 128],
                        uT[:, i, n * 512:(n + 1) * 512],
                        start=(i == 0), stop=(i == NE - 1)),
                        wait=(xlnT_done + ww + [pb.act.now()] if i == 0 else ()),
                        inc=pb.pe)
                pb.op("scalar", lambda s, m=m, n=n, ps=ps: s.copy(
                    out_view[:, m, n * 512:(n + 1) * 512], ps),
                    wait=[pb.pe.now()], inc=pb.act)
    proj("Wq_i", qT_loc)
    pb.dmaop("gpsimd", lambda g: g.dma_start(
        out=T["ccQ_in"].ap().rearrange("(i p) t -> p i t", p=128), in_=qT_loc[:, :, :]),
        wait=[pb.act.now()])
    pb.op("gpsimd", lambda g: g.collective_compute(
        "AllGather", ALU.bypass, replica_groups=GROUPS,
        ins=[T["ccQ_in"].ap().opt()], outs=[T["ccQ_out"].ap().opt()]),
        wait=[pb.dmag.now()], inc=pb.cc, inck=None)

    proj("Wk_i", kvT, done_wait=[pb.pe.now()])
    pb.dmaop("gpsimd", lambda g: g.dma_start(
        out=T["wbuf"][:, :, :], in_=T["Wv_i"].rearrange("(i p) f -> p i f", p=128)),
        wait=[pb.pe.now()])
    wv = [pb.dmag.now()]
    for m in range(NE):
        for n in range(4):
            ps = T["ps_ab"][:, 1024 + ((m + n) % 2) * 512:1536 + ((m + n) % 2) * 512]
            for i in range(NE):
                pb.op("tensor", lambda t, m=m, n=n, i=i, ps=ps: t.matmul(
                    ps, T["wbuf"][:, i, m * 128:(m + 1) * 128],
                    uT[:, i, n * 512:(n + 1) * 512],
                    start=(i == 0), stop=(i == NE - 1)),
                    wait=(wv + [pb.dve.now(), pb.act.now()] if i == 0 else ()),
                    inc=pb.pe)
            pb.op("vector", lambda v, m=m, n=n, ps=ps: v.tensor_add(
                kvT[:, m, n * 512:(n + 1) * 512],
                kvT[:, m, n * 512:(n + 1) * 512], ps), wait=[pb.pe.now()], inc=pb.dve)
    kv_done = [pb.dve.now()]

    wdv = T["wbuf"].ap().rearrange("p i f -> p (i f)").rearrange(
        "p (i f) -> p i f", f=256)
    pb.dmaop("gpsimd", lambda g: g.dma_start(
        out=wdv[:, 0:8, :],
        in_=T["Wd_i"].rearrange("(i p) f -> p i f", p=128)), wait=kv_done)
    wd = [pb.dmag.now()]
    for m in range(2):
        for n in range(4):
            for i in range(NE):
                pb.op("tensor", lambda t, m=m, n=n, i=i: t.matmul(
                    T["ps_ab"][:, 0:512], wdv[:, i, m * 128:(m + 1) * 128],
                    kvT[:, i, n * 512:(n + 1) * 512],
                    start=(i == 0), stop=(i == NE - 1)),
                    wait=(kv_done + wd + [pb.act.now()] if i == 0 else ()), inc=pb.pe)
            pb.op("scalar", lambda s, m=m, n=n: s.copy(
                T["dxs"][:, :], T["ps_ab"][:, 0:512]),
                wait=[pb.pe.now(), pb.dma.now()], inc=pb.act)
            pb.dmaop("sync", lambda s, m=m, n=n: s.dma_start(
                out=T["ccL_in"].ap()[m * 128:(m + 1) * 128, n * 512:(n + 1) * 512],
                in_=T["dxs"][:, :]), wait=[pb.act.now()])
    pb.op("gpsimd", lambda g: g.collective_compute(
        "AllGather", ALU.bypass, replica_groups=GROUPS,
        ins=[T["ccL_in"].ap().opt()], outs=[T["ccL_out"].ap().opt()]),
        wait=[pb.dma.now()], inc=pb.cc, inck=None)
    cc2 = [pb.cc.now()]

    for p4 in range(4):
        pb.dmaop("sync", lambda s, p4=p4: s.dma_start(
            out=latT[:, :, p4 * TCORE:(p4 + 1) * TCORE],
            in_=T["ccL_out"].ap()[p4 * 256:(p4 + 1) * 256, :]
            .rearrange("(i p) t -> p i t", p=128)), wait=cc2)
    for p4 in range(4):
        pb.dmaop("sync", lambda s, p4=p4: s.dma_start(
            out=wband[:, :, :],
            in_=T["ccQ_out"].ap()[p4 * D:(p4 + 1) * D, :]
            .rearrange("(i p) t -> p i t", p=128)),
            wait=cc2 + [pb.dve.now()])
        for g4 in range(4):
            if g4 == 0:
                pb.op("vector", lambda v, p4=p4: v.tensor_scalar_mul(
                    qTm[:, :, p4, :], wband[:, 0:2, :], T["selg_sb"][:, 0:1]),
                    wait=[pb.dma.now()], inc=pb.dve)
            else:
                pb.op("vector", lambda v, p4=p4, g4=g4: v.scalar_tensor_tensor(
                    qTm[:, :, p4, :], wband[:, 2 * g4:2 * g4 + 2, :],
                    T["selg_sb"][:, g4:g4 + 1], qTm[:, :, p4, :], ALU.mult, ALU.add),
                    inc=pb.dve)
    asm_done = [pb.dve.now(), pb.dma.now()]

    pb.dmaop("gpsimd", lambda g: g.dma_start(
        out=T["wupk_sb"][:, :, :],
        in_=T["wupk_i"].rearrange("(i p) f -> p i f", p=128)))
    pb.dmaop("gpsimd", lambda g: g.dma_start(
        out=T["wupv_sb"][:, :, :],
        in_=T["wupv_i"].rearrange("(i p) f -> p i f", p=128)))
    wup = [pb.dmag.now()]

    for ht in range(2):
        for n in range(NKB):
            for i in range(2):
                pb.op("tensor", lambda t, ht=ht, n=n, i=i: t.matmul(
                    T["ps_ab"][:, 0:512],
                    T["wupk_sb"][:, i, ht * 128:(ht + 1) * 128],
                    latT[:, i, n * 512:(n + 1) * 512],
                    start=(i == 0), stop=(i == 1)),
                    wait=(asm_done + wup + [pb.act.now(), pb.dve.now()]
                          if (n == 0 and i == 0) else
                          ([pb.act.now()] if i == 0 else ())), inc=pb.pe)
            pb.op("scalar", lambda s, n=n: s.copy(krT[:, n * 512:(n + 1) * 512],
                                                  T["ps_ab"][:, 0:512]),
                  wait=[pb.pe.now()], inc=pb.act)
        krw = [pb.act.now()]
        pb.op("vector", lambda v: v.tensor_tensor(
            krp.rearrange("p (n t) -> p n t", t=512),
            krT[:, :].rearrange("p (n t) -> p n t", t=512),
            T["rowkp_sb"][:, :].unsqueeze(1).to_broadcast([128, NKB, 512]),
            ALU.mult), wait=krw, inc=pb.dve)
        pb.op("vector", lambda v: v.tensor_tensor(
            krm.rearrange("p (n t) -> p n t", t=512),
            krT[:, :].rearrange("p (n t) -> p n t", t=512),
            T["rowkm_sb"][:, :].unsqueeze(1).to_broadcast([128, NKB, 512]),
            ALU.mult), inc=pb.dve)
        kr_done = [pb.dve.now()]
        for hsub in range(2):
            h = 2 * ht + hsub
            hb = hsub * 64
            pb.op("vector", lambda v: v.memset(vrec[:, :, 64:65], 1.0),
                  wait=[pb.pe.now()], inc=pb.dve)
            for kt in range(64):
                for i in range(2):
                    pb.op("tensor", lambda t, h=h, kt=kt, i=i: t.matmul(
                        T["ps_ab"][:, 1024:1088], latT[:, i, kt * 128:(kt + 1) * 128],
                        T["wupv_sb"][:, i, h * 64:(h + 1) * 64],
                        start=(i == 0), stop=(i == 1)),
                        wait=(kr_done + [pb.act.now(), pb.dve.now()]
                              if kt == 0 else
                              ([pb.act.now()] if i == 0 else ())), inc=pb.pe)
                pb.op("scalar", lambda s, kt=kt: s.copy(vrec[:, kt, 0:64],
                                                        T["ps_ab"][:, 1024:1088]),
                      wait=[pb.pe.now()], inc=pb.act)
            vr_done = [pb.act.now(), pb.dve.now()]
            for kb in range(NKB):
                for kt in range(4):
                    pb.op("tensor", lambda t, kb=kb, kt=kt: t.matmul(
                        T["ps_s"][0:65, 0:1], vrec[:, kb * 4 + kt, :],
                        T["ones_sb"][:, 0:1], start=(kt == 0), stop=(kt == 3)),
                        wait=(vr_done if (kb == 0 and kt == 0) else [pb.dve.now()]),
                        inc=pb.pe)
                pb.op("vector", lambda v, kb=kb: v.tensor_copy(
                    T["vs"][0:65, kb:kb + 1], T["ps_s"][0:65, 0:1]),
                    wait=[pb.pe.now()], inc=pb.dve)
            cur = "vs"
            for st in (1, 2, 4, 8):
                nxt = "vp2" if cur == "vs" else "vs"
                pb.op("vector", lambda v, cur=cur, nxt=nxt, st=st: v.tensor_copy(
                    T[nxt][0:65, 0:st], T[cur][0:65, 0:st]),
                    wait=[pb.dve.now()], inc=pb.dve)
                pb.op("vector", lambda v, cur=cur, nxt=nxt, st=st: v.tensor_add(
                    T[nxt][0:65, st:16], T[cur][0:65, st:16], T[cur][0:65, 0:16 - st]),
                    inc=pb.dve)
                cur = nxt
            vpref = T[cur]
            pref_done = [pb.dve.now()]

            for qb in range(NQB):
                qsl = qTm[hb:hb + 64, ht, qb // 4,
                          (qb % 4) * 512:(qb % 4) * 512 + 512]
                pb.op("vector", lambda v, qsl=qsl, hb=hb: v.tensor_mul(
                    T["qp"][hb:hb + 64, :], qsl, T["rowqp_sb"][hb:hb + 64, :]),
                    wait=pref_done + [pb.act.now()], inc=pb.dve)
                pb.op("vector", lambda v, qsl=qsl, hb=hb: v.tensor_mul(
                    T["qm"][hb:hb + 64, :], qsl, T["rowqm_sb"][hb:hb + 64, :]),
                    inc=pb.dve)
                q_done = [pb.dve.now()]
                kbs = [kb for kb in range(NKB) if abs(kb - qb) <= 4]
                started = False
                for kb in kbs:
                    dB = abs(kb - qb)
                    if kb != qb:
                        kr_use, q_use = (krm, "qp") if kb > qb else (krp, "qm")
                        for kt in range(4):
                            pb.op("tensor", lambda t, kr_use=kr_use, q_use=q_use,
                                  kb=kb, kt=kt, hb=hb: t.matmul(
                                T["ps_ab"][:, kt * 512:(kt + 1) * 512],
                                kr_use[hb:hb + 64,
                                       (kb * 4 + kt) * 128:(kb * 4 + kt + 1) * 128],
                                T[q_use][hb:hb + 64, :], start=True, stop=True),
                                wait=(q_done + [pb.act.now(), pb.dve.now()]
                                      if kt == 0 else ()), inc=pb.pe)
                        if dB <= 2:
                            pb.op("scalar", lambda s, dB=dB: s.activation(
                                T["Pv"][:, :], T["ps_ab"][:, :], AF.Exp,
                                scale=T["creltab_sb"][:, dB:dB + 1]),
                                wait=[pb.pe.now()], inc=pb.act)
                        else:
                            pb.op("vector", lambda v, dB=dB: v.tensor_scalar(
                                T["Pv"][:, :], T["ps_ab"][:, :],
                                T["creltab_sb"][:, dB:dB + 1],
                                1.0, ALU.mult, ALU.add), wait=[pb.pe.now()],
                                inc=pb.dve)
                        pdone = [pb.act.now(), pb.dve.now()]
                        for kt in range(4):
                            pb.op("tensor", lambda t, kb=kb, kt=kt,
                                  started=started: t.matmul(
                                T["ps_c"][0:65, 0:512], vrec[:, kb * 4 + kt, :],
                                T["Pv"][:, kt * 512:(kt + 1) * 512],
                                start=(not started and kt == 0), stop=False,
                                skip_group_check=True),
                                wait=(pdone if kt == 0 else ()), inc=pb.pe)
                        started = True
                    else:
                        for variant, kr_use, q_use in (("Pv", krm, "qp"),
                                                       ("Pv2", krp, "qm")):
                            for kt in range(4):
                                pb.op("tensor", lambda t, kr_use=kr_use, q_use=q_use,
                                      kb=kb, kt=kt, hb=hb: t.matmul(
                                    T["ps_ab"][:, kt * 512:(kt + 1) * 512],
                                    kr_use[hb:hb + 64,
                                           (kb * 4 + kt) * 128:(kb * 4 + kt + 1) * 128],
                                    T[q_use][hb:hb + 64, :], start=True, stop=True),
                                    wait=(q_done + [pb.act.now(), pb.dve.now()]
                                          if kt == 0 else ()), inc=pb.pe)
                            wexp = [pb.pe.now()]
                            for kt in range(4):
                                if variant == "Pv":
                                    lo2, hi2 = kt * 512, kt * 512 + kt * 128
                                else:
                                    lo2, hi2 = kt * 512 + (kt + 1) * 128, (kt + 1) * 512
                                if hi2 > lo2:
                                    pb.op("scalar", lambda s, variant=variant,
                                          lo2=lo2, hi2=hi2: s.activation(
                                        T[variant][:, lo2:hi2],
                                        T["ps_ab"][:, lo2:hi2], AF.Exp,
                                        scale=T["creltab_sb"][:, 0:1]),
                                        wait=wexp, inc=pb.act)
                        for kt in range(4):
                            qsl2 = qTm[hb:hb + 64, ht, qb // 4,
                                       (qb % 4) * 512 + kt * 128:
                                       (qb % 4) * 512 + kt * 128 + 128]
                            pb.op("tensor", lambda t, kb=kb, kt=kt, qsl2=qsl2,
                                  hb=hb: t.matmul(
                                T["ps_c"][:, 512 + kt * 128:512 + (kt + 1) * 128],
                                krT[hb:hb + 64,
                                    (kb * 4 + kt) * 128:(kb * 4 + kt + 1) * 128],
                                qsl2, start=True, stop=True),
                                wait=([pb.act.now()] if kt == 0 else ()), inc=pb.pe)
                        pb.op("vector", lambda v: v.tensor_tensor(
                            T["Pd"][:, :].rearrange("p (k f) -> p k f", f=128),
                            T["ps_c"][:, 512:1024].rearrange("p (k f) -> p k f", f=128),
                            T["dconst_sb"][:, :].unsqueeze(1)
                            .to_broadcast([128, 4, 128]),
                            ALU.mult), wait=[pb.pe.now()], inc=pb.dve)
                        pb.op("scalar", lambda s: s.activation(
                            T["Pd"][:, :], T["Pd"][:, :], AF.Exp),
                            wait=[pb.dve.now()], inc=pb.act)
                        pdone = [pb.act.now()]
                        for kt in range(4):
                            for qs4 in range(4):
                                src2 = ("Pd" if kt == qs4 else
                                        ("Pv" if kt > qs4 else "Pv2"))
                                off = (kt * 128 if src2 == "Pd"
                                       else kt * 512 + qs4 * 128)
                                pb.op("tensor", lambda t, kb=kb, kt=kt, qs4=qs4,
                                      src2=src2, off=off, started=started: t.matmul(
                                    T["ps_c"][0:65, qs4 * 128:(qs4 + 1) * 128],
                                    vrec[:, kb * 4 + kt, :],
                                    T[src2][:, off:off + 128],
                                    start=(not started and kt == 0 and qs4 == 0),
                                    stop=False, skip_group_check=True),
                                    wait=(pdone if (kt == 0 and qs4 == 0) else ()),
                                    inc=pb.pe)
                        started = True
                lo, hi = qb - 5, qb + 4
                pb.op("vector", lambda v: v.memset(T["farc"][0:65, 0:1], 0.0),
                      wait=[pb.pe.now()], inc=pb.dve)
                if lo >= 0:
                    pb.op("vector", lambda v, lo=lo, vpref=vpref: v.tensor_add(
                        T["farc"][0:65, 0:1], T["farc"][0:65, 0:1],
                        vpref[0:65, lo:lo + 1]), inc=pb.dve)
                if hi < 15:
                    pb.op("vector", lambda v, vpref=vpref: v.tensor_add(
                        T["farc"][0:65, 0:1], T["farc"][0:65, 0:1],
                        vpref[0:65, 15:16]), inc=pb.dve)
                    pb.op("vector", lambda v, hi=hi, vpref=vpref: v.tensor_sub(
                        T["farc"][0:65, 0:1], T["farc"][0:65, 0:1],
                        vpref[0:65, hi:hi + 1]), inc=pb.dve)
                pb.op("scalar", lambda s: s.copy(T["farb"][0:65, 0:1],
                                                 T["farc"][0:65, 0:1]),
                      wait=[pb.dve.now()], inc=pb.act)
                pb.op("tensor", lambda t: t.transpose(
                    T["ps_tr"][0:1, 0:65], T["farb"][0:65, 0:1],
                    T["id_sb"][0:65, 0:65]), wait=[pb.act.now()], inc=pb.pe)
                pb.op("scalar", lambda s: s.copy(T["farrow"][0:1, 0:65],
                                                 T["ps_tr"][0:1, 0:65]),
                      wait=[pb.pe.now()], inc=pb.act)
                pb.op("tensor", lambda t: t.matmul(
                    T["ps_c"][0:65, 0:512], T["farrow"][0:1, 0:65],
                    T["ones_sb"][0:1, 0:512], start=False, stop=True,
                    skip_group_check=True),
                    wait=[pb.act.now()], inc=pb.pe)
                pb.op("vector", lambda v: v.reciprocal(T["lrow"][64:65, 0:512],
                                                       T["ps_c"][64:65, 0:512]),
                      wait=[pb.pe.now()], inc=pb.dve)
                pb.op("scalar", lambda s: s.copy(T["lrowb"][64:65, 0:512],
                                                 T["lrow"][64:65, 0:512]),
                      wait=[pb.dve.now()], inc=pb.act)
                pb.op("tensor", lambda t: t.matmul(
                    T["ps_ab"][0:64, 0:512], T["ones_sb"][64:65, 0:64],
                    T["lrowb"][64:65, 0:512], start=True, stop=True),
                    wait=[pb.act.now(), pb.dve.now()], inc=pb.pe)
                pb.op("scalar", lambda s: s.copy(T["osb"][0:64, 0:512],
                                                 T["ps_c"][0:64, 0:512]),
                      wait=[pb.pe.now()], inc=pb.act)
                pb.op("vector", lambda v: v.tensor_tensor(
                    T["ob0"][0:64, 0:512], T["osb"][0:64, 0:512],
                    T["ps_ab"][0:64, 0:512], ALU.mult),
                    wait=[pb.act.now(), pb.dma.now()], inc=pb.dve)
                pb.dmaop("sync", lambda s, h=h, qb=qb: s.dma_start(
                    out=T["oD"].ap()[h * 64:(h + 1) * 64,
                                     qb * 512:(qb + 1) * 512],
                    in_=T["ob0"][0:64, 0:512]), wait=[pb.dve.now()])
    attn_done = [pb.dma.now()]
    pb.dmaop("sync", lambda s: s.dma_start(
        out=oT[:, :, :], in_=T["oD"].ap().rearrange("(i p) t -> p i t", p=128)),
        wait=attn_done)
    attn_done = [pb.dma.now()]

    pb.dmaop("gpsimd", lambda g: g.dma_start(
        out=T["wbuf"][:, 0:2, :],
        in_=T["wo_i"].rearrange("k p m f -> p k (m f)", p=128)), wait=attn_done)
    wo_w = [pb.dmag.now()]
    for m in range(NE):
        for n in range(NKB):
            for k in range(2):
                pb.op("tensor", lambda t, m=m, n=n, k=k: t.matmul(
                    T["ps_ab"][:, 0:512], T["wbuf"][:, k, m * 128:(m + 1) * 128],
                    oT[:, k, n * 512:(n + 1) * 512], start=(k == 0), stop=(k == 1)),
                    wait=(attn_done + wo_w + [pb.act.now()] if (k == 0) else ()),
                    inc=pb.pe)
            pb.op("scalar", lambda s, m=m, n=n: s.copy(T["dxs"][:, :],
                                                       T["ps_ab"][:, 0:512]),
                  wait=[pb.pe.now(), pb.dma.now()], inc=pb.act)
            pb.dmaop("sync", lambda s, m=m, n=n: s.dma_start(
                out=T["ccR_in"].ap()[(n // 4) * D + m * 128:(n // 4) * D + (m + 1) * 128,
                                     (n % 4) * 512:(n % 4) * 512 + 512],
                in_=T["dxs"][:, :]), wait=[pb.act.now()])
    pb.op("gpsimd", lambda g: g.collective_compute(
        "ReduceScatter", ALU.add, replica_groups=GROUPS,
        ins=[T["ccR_in"].ap().opt()], outs=[T["ccR_out"].ap().opt()]),
        wait=[pb.dma.now(), pb.dmag.now()], inc=pb.cc, inck=None)
    pb.dmaop("sync", lambda s: s.dma_start(
        out=uT[:, :, :],
        in_=T["ccR_out"].ap().rearrange("(i p) t -> p i t", p=128)),
        wait=[pb.cc.now()])
    dx_done = [pb.dma.now()]

    for j in range(NT):
        for i in range(NE):
            pb.op("tensor", lambda t, j=j, i=i: t.transpose(
                T["ps_tr"][:, :], uT[:, i, j * 128:(j + 1) * 128], T["id_sb"][:, :]),
                wait=(dx_done + [pb.dve.now(), pb.act.now()]), inc=pb.pe)
            pb.op("scalar", lambda s, i=i: s.copy(
                T["un"][:, i * 128:(i + 1) * 128], T["ps_tr"][:, :]),
                wait=[pb.pe.now()], inc=pb.act)
        pb.dmaop("sync", lambda s, j=j: s.dma_start(
            out=T["xt"][:, :], in_=T["x_in"][j * 128:(j + 1) * 128, :]),
            wait=[pb.dve.now()])
        emit_ln_tile(pb, T["un"][:, :], T["un"][:, :], T["stats"][:, :],
                     [pb.act.now()])
        pb.op("vector", lambda v: v.tensor_add(T["xt"][:, :], T["xt"][:, :],
                                               T["un"][:, :]),
              wait=[pb.dve.now(), pb.dma.now()], inc=pb.dve)
        pb.dmaop("sync", lambda s, j=j: s.dma_start(
            out=T["x_out"][j * 128:(j + 1) * 128, :], in_=T["xt"][:, :]),
            wait=[pb.dve.now()])
    pb.op("sync", None, wait=[pb.dma.now()])


def sbuf_p3(nc, T, ar, ps=None):
    sb = lambda n, shp, dt: T.__setitem__(n, ar.take(shp, dt))
    sb("a1", [128, 16384], BF16)
    sb("a2", [128, 16384], BF16)
    sb("a3", [128, 16384], BF16)
    sb("latT", [128, 16384], BF16)
    sb("krT", [128, 8192], BF16)
    sb("vrec", [128, 64 * 65], BF16)
    sb("wbuf", [128, NE, D], BF16)
    sb("wupk_sb", [128, 2, 256], BF16)
    sb("wupv_sb", [128, 2, 256], BF16)
    sb("Pv", [128, 2048], BF16)
    sb("Pv2", [128, 2048], BF16)
    sb("Pd", [128, 512], BF16)
    sb("qp", [128, 512], BF16)
    sb("qm", [128, 512], BF16)
    sb("vs", [65, 16], F32)
    sb("vp2", [65, 16], F32)
    sb("farc", [65, 1], F32)
    sb("farb", [65, 1], BF16)
    sb("farrow", [1, 65], BF16)
    sb("lrow", [128, 512], F32)
    sb("lrowb", [128, 512], BF16)
    sb("osb", [64, 512], BF16)
    sb("ob0", [64, 512], BF16)
    sb("dxs", [128, 512], BF16)
    sb("rowqp_sb", [128, 512], BF16)
    sb("rowqm_sb", [128, 512], BF16)
    sb("rowkp_sb", [128, 512], BF16)
    sb("rowkm_sb", [128, 512], BF16)
    sb("creltab_sb", [128, 8], F32)
    sb("dconst_sb", [128, 128], F32)
    sb("selg_sb", [128, 4], F32)
    sb("stats", [128, 16], F32)
    sb("un", [128, D], BF16)
    sb("xt", [128, D], F32)
    sb("id_sb", [128, 128], BF16)
    sb("ones_sb", [128, 512], BF16)
    sb("cpool", [128, 8], F32)
    return psum_std(nc, T, split_ab=False, ps=ps)


def decls_p3(nc, T, xio=True):
    def inp(name, shape, dtype=F32):
        T[name] = nc.dram_tensor(name, list(shape), dtype, kind="ExternalInput").ap()

    if xio:
        inp("x_in", [TCORE, D])
        T["x_out"] = nc.dram_tensor("x_out", [TCORE, D], F32,
                                    kind="ExternalOutput").ap()
    inp("Wq_i", [D, D]); inp("Wk_i", [D, D]); inp("Wv_i", [D, D])
    inp("Wd_i", [D, 256])
    inp("wupk_i", [256, 256]); inp("wupv_i", [256, 256])
    inp("wo_i", [2, 128, 8, 128])
    inp("rowqp_i", [128, 512]); inp("rowqm_i", [128, 512])
    inp("rowkp_i", [128, 512]); inp("rowkm_i", [128, 512])
    inp("creltab_i", [128, 8]); inp("dconst_i", [128, 128]); inp("selg_i", [128, 4])
    T["ccQ_in"] = nc.dram_tensor("ccQ_in", [D, TCORE], BF16, kind="Internal")
    T["ccQ_out"] = nc.dram_tensor("ccQ_out", [4 * D, TCORE], BF16, kind="Internal")
    T["ccL_in"] = nc.dram_tensor("ccL_in", [256, TCORE], BF16, kind="Internal")
    T["ccL_out"] = nc.dram_tensor("ccL_out", [4 * 256, TCORE], BF16, kind="Internal")
    T["ccR_in"] = nc.dram_tensor("ccR_in", [4 * D, TCORE], BF16, kind="Internal")
    T["ccR_out"] = nc.dram_tensor("ccR_out", [D, TCORE], BF16, kind="Internal")
    T["oD"] = nc.dram_tensor("oD", [256, 8192], BF16, kind="Internal")
    decl_shared(nc, T)


def build_p3(limit=None):
    nc = bass.Bass("TRN2", target_bir_lowering=False)
    T = {}
    decls_p3(nc, T)
    sbuf_p3(nc, T, Arena(nc, "G", ARENA_N))

    pb = PB(nc)
    pb.limit = limit
    pb.cpool = T["cpool"][:, :]
    emit_phase3(pb, T)
    with nc.Block() as blk:
        pb.replay(blk)
    return nc


def host_inputs_p3(full, x_stream=None):
    lam = 0.01
    j = np.arange(512, dtype=np.float64)
    rowqp = bcast_np(np.exp(lam * j) / 32.0)
    rowqm = bcast_np(np.exp(-lam * j) / 32.0)
    rowkp = bcast_np(np.exp(lam * j))
    rowkm = bcast_np(np.exp(-lam * j))
    crel = bcast_np(np.exp(-512.0 * lam * np.arange(8)))
    pk = np.arange(128, dtype=np.float64)
    dconst = (np.exp(-lam * np.abs(pk[:, None] - pk[None, :])) / 32.0).astype(np.float32)
    Wq = np.asarray(full["L1_attn_W_q"], np.float32)
    Wk = np.asarray(full["L1_attn_W_k"], np.float32)
    Wv = np.asarray(full["L1_attn_W_v"], np.float32)
    Wd = np.asarray(full["L1_attn_W_down_kv"], np.float32)
    Wuk = np.asarray(full["L1_attn_W_up_k"], np.float32)
    Wuv = np.asarray(full["L1_attn_W_up_v"], np.float32)
    Wo = np.asarray(full["L1_attn_W_o"], np.float32)
    maps = []
    for c in range(8):
        b, g = c // 4, c % 4
        t0 = g * TCORE
        selg = np.zeros((128, 4), np.float32)
        selg[:, g] = 1.0
        maps.append(dict(
            **({"x_in": np.ascontiguousarray(x_stream[b, t0:t0 + TCORE])}
               if x_stream is not None else {}),
            Wq_i=Wq, Wk_i=Wk, Wv_i=Wv, Wd_i=Wd,
            wupk_i=np.ascontiguousarray(Wuk[:, g * 256:(g + 1) * 256]),
            wupv_i=np.ascontiguousarray(Wuv[:, g * 256:(g + 1) * 256]),
            wo_i=np.ascontiguousarray(
                Wo[g * 256:(g + 1) * 256].reshape(2, 128, 8, 128)),
            rowqp_i=rowqp, rowqm_i=rowqm, rowkp_i=rowkp, rowkm_i=rowkm,
            creltab_i=crel, dconst_i=dconst, selg_i=selg,
        ))
    return maps


NPASS = [6, 6, 6, 6, 6, 6, 6, 6, 6, 6, 4]


def emit_phase4(pb, T):
    cp = pb.cpool
    xlnT = T["xlnT4"].ap().rearrange("p (i t) -> p i t", t=TCORE)
    wup = T["wz4"].ap()[:, 0:6 * 8 * 512].rearrange("p (e i f) -> p e i f", e=6, i=8)
    wdn = T["wz4"].ap()[:, 6 * 8 * 512:6 * 8 * 512 + 6 * 4 * 1024].rearrange(
        "p (e k f) -> p e k f", e=6, k=4)
    wsu = T["wz4"].ap()[:, 0:8 * 2048].rearrange("p (i f) -> p i f", f=2048)
    wsd = T["wz4"].ap()[:, 8 * 2048:8 * 2048 + 16 * 1024].rearrange(
        "p (k f) -> p k f", f=1024)
    act6 = T["act6"].ap().rearrange("p (e k t) -> p e k t", e=6, k=4)
    acts = T["act6"].ap().rearrange("p (k t) -> p k t", t=512)
    outb = T["outb"].ap().rearrange("p (j f) -> p j f", f=D)

    pb.dmaop("sync", lambda s: s.dma_start(out=T["cpool"][:, :], in_=T["cpool_c"]))
    pb.dmaop("sync", lambda s: s.dma_start(out=T["id_sb"][:, :], in_=T["ident"]))
    pb.dmaop("sync", lambda s: s.dma_start(out=T["ones_sb"][:, :], in_=T["ones_c"]))
    pb.dmaop("sync", lambda s: s.dma_start(out=T["cbias_sb"][:, :], in_=T["cbias_i"]))
    pb.dmaop("gpsimd", lambda g: g.dma_start(
        out=T["rw_sb"][:, :, :], in_=T["rwT_i"].rearrange("(i p) f -> p i f", p=128)))
    ld0 = [pb.dma.now(), pb.dmag.now()]

    for j in range(NT):
        pb.dmaop("sync", lambda s, j=j: s.dma_start(
            out=T["xt"][:, :], in_=T["x_in"][j * 128:(j + 1) * 128, :]),
            wait=[pb.dve.now()])
        emit_ln_tile(pb, T["xt"][:, :], T["un"][:, :], T["stats"][:, :],
                     ld0 + [pb.dma.now(), pb.act.now()])
        for i in range(NE):
            pb.op("tensor", lambda t, i=i: t.transpose(
                T["ps_tr"][:, :], T["un"][:, i * 128:(i + 1) * 128], T["id_sb"][:, :]),
                wait=[pb.dve.now(), pb.act.now()], inc=pb.pe)
            pb.op("scalar", lambda s, j=j, i=i: s.copy(
                xlnT[:, i, j * 128:(j + 1) * 128], T["ps_tr"][:, :]),
                wait=[pb.pe.now()], inc=pb.act)
    xlnT_done = [pb.act.now()]

    for j in range(NT):
        for i in range(NE):
            pb.op("tensor", lambda t, j=j, i=i: t.matmul(
                T["ps_s"][:, 0:64], xlnT[:, i, j * 128:(j + 1) * 128],
                T["rw_sb"][:, i, :], start=(i == 0), stop=(i == NE - 1)),
                wait=(xlnT_done + [pb.dve.now()] if i == 0 else ()), inc=pb.pe)
        S_ = T["rsc"]
        pb.op("scalar", lambda s, j=j: s.activation(S_[:, j, :], T["ps_s"][:, 0:64],
                                                    AF.Sigmoid),
              wait=[pb.pe.now()], inc=pb.act)
        pb.op("vector", lambda v, j=j: v.tensor_add(T["sfc"][:, :], S_[:, j, :],
                                                    T["cbias_sb"][:, :]),
              wait=[pb.act.now()], inc=pb.dve)
        for g8 in range(8):
            pb.op("vector", lambda v, g8=g8: v.max(
                out=T["m8"][:, :], in_=T["sfc"][:, g8 * 8:(g8 + 1) * 8]), inc=pb.dve)
            pb.op("vector", lambda v, g8=g8: v.tensor_add(
                T["gs"][:, g8:g8 + 1], T["m8"][:, 0:1], T["m8"][:, 1:2]), inc=pb.dve)
        pb.op("vector", lambda v: v.max(out=T["m8"][:, :], in_=T["gs"][:, :]),
              inc=pb.dve)
        pb.op("vector", lambda v: v.tensor_single_scalar(
            T["gm"][:, :], T["gs"][:, :], T["m8"][:, 3:4], ALU.is_ge), inc=pb.dve)
        for g8 in range(8):
            pb.op("vector", lambda v, g8=g8: v.tensor_scalar_mul(
                T["msk"][:, g8 * 8:(g8 + 1) * 8], T["sfc"][:, g8 * 8:(g8 + 1) * 8],
                T["gm"][:, g8:g8 + 1]), inc=pb.dve)
        pb.op("vector", lambda v: v.max(out=T["m8"][:, :], in_=T["msk"][:, :]),
              inc=pb.dve)
        pb.op("vector", lambda v: v.tensor_single_scalar(
            T["sel"][:, :], T["msk"][:, :], T["m8"][:, 7:8], ALU.is_ge), inc=pb.dve)
        pb.op("vector", lambda v, j=j: v.tensor_mul(T["sel"][:, :], T["sel"][:, :],
                                                    S_[:, j, :]), inc=pb.dve)
        pb.op("vector", lambda v: v.tensor_reduce(
            T["m8"][:, 0:1], T["sel"][:, :], AX.X, ALU.add), inc=pb.dve)
        pb.op("vector", lambda v: v.tensor_scalar_add(
            T["m8"][:, 0:1], T["m8"][:, 0:1], 1e-20), inc=pb.dve)
        pb.op("vector", lambda v: v.reciprocal(T["m8"][:, 1:2], T["m8"][:, 0:1]),
              inc=pb.dve)
        pb.op("vector", lambda v, j=j: v.tensor_scalar(
            T["Cr"][:, j, :], T["sel"][:, :], T["m8"][:, 1:2], 2.5,
            ALU.mult, ALU.mult), inc=pb.dve)
    rt_done = [pb.dve.now()]

    for j in range(NT):
        pb.op("tensor", lambda t, j=j: t.transpose(
            T["ps_tr"][0:64, 0:128], T["Cr"][:, j, :], T["id_sb"][:, :]),
            wait=(rt_done + [pb.act.now(), pb.dma.now()]), inc=pb.pe)
        pb.op("scalar", lambda s, j=j: s.copy(T["ctb"][0:64, :],
                                              T["ps_tr"][0:64, 0:128]),
              wait=[pb.pe.now()], inc=pb.act)
        pb.dmaop("sync", lambda s, j=j: s.dma_start(
            out=T["CTd"].ap()[:, j * 128:(j + 1) * 128],
            in_=T["ctb"][0:64, :]), wait=[pb.act.now()])
    ct_done = [pb.dma.now()]


    pb.op("vector", lambda v: v.memset(T["outb"].ap()[:, :], 0.0), inc=pb.dve)

    e0 = 0
    for pi, psz in enumerate(NPASS):
        for ei in range(psz):
            pb.dmaop("gpsimd", lambda g, e=e0 + ei, ei=ei: g.dma_start(
                out=wup[:, ei, :, :],
                in_=T["upw_i"][e].rearrange("(i p) f -> p i f", p=128)),
                wait=[pb.pe.now()])
            pb.dmaop("gpsimd", lambda g, e=e0 + ei, ei=ei: g.dma_start(
                out=wdn[:, ei, :, :],
                in_=T["dnw_i"][e].rearrange("(k p) f -> p k f", p=128)))
        wl = [pb.dmag.now()]
        for ch in range(4):
            for ei in range(psz):
                pb.dmaop("sync", lambda s, e=e0 + ei, ch=ch: s.dma_start(
                    out=T["crow"][0:1, :],
                    in_=T["CTd"].ap()[e:e + 1, ch * 512:(ch + 1) * 512]),
                    wait=ct_done + [pb.pe.now()])
                pb.op("tensor", lambda t: t.matmul(
                    T["ps_c"][:, 0:512], T["ones_sb"][0:1, 0:128], T["crow"][0:1, :],
                    start=True, stop=True),
                    wait=[pb.dma.now(), pb.act.now(), pb.dve.now()], inc=pb.pe)
                pb.op("scalar", lambda s: s.copy(T["cbc"][:, :], T["ps_c"][:, 0:512]),
                      wait=[pb.pe.now(), pb.dve.now()], inc=pb.act)
                for mt in range(4):
                    for i in range(NE):
                        pb.op("tensor", lambda t, ei=ei, ch=ch, mt=mt, i=i: t.matmul(
                            T["ps_ab"][:, mt * 512:(mt + 1) * 512],
                            wup[:, ei, i, mt * 128:(mt + 1) * 128],
                            xlnT[:, i, ch * 512:(ch + 1) * 512],
                            start=(i == 0), stop=(i == NE - 1)),
                            wait=(wl + xlnT_done + [pb.act.now(), pb.dve.now()]
                                  if (mt == 0 and i == 0) else ()), inc=pb.pe)
                pb.op("scalar", lambda s: s.activation(T["rl"][:, :], T["ps_ab"][:, :],
                                                       AF.Relu),
                      wait=[pb.pe.now()], inc=pb.act)
                pb.op("vector", lambda v: v.tensor_mul(T["rl"][:, :], T["rl"][:, :],
                                                       T["rl"][:, :]),
                      wait=[pb.act.now()], inc=pb.dve)
                pb.op("vector", lambda v, ei=ei: v.tensor_tensor(
                    act6[:, ei, :, :], T["rl"][:, :].rearrange("p (k t) -> p k t", t=512),
                    T["cbc"][:, :].unsqueeze(1).to_broadcast([128, 4, 512]),
                    ALU.mult), wait=[pb.dve.now()], inc=pb.dve)
            up_done = [pb.dve.now()]
            for mt in range(4):
                for n in range(2):
                    for ei in range(psz):
                        for k in range(4):
                            pb.op("tensor", lambda t, ei=ei, k=k, mt=mt, n=n,
                                  psz=psz: t.matmul(
                                T["ps_c"][:, 0:512],
                                act6[:, ei, k, mt * 128:(mt + 1) * 128],
                                wdn[:, ei, k, n * 512:(n + 1) * 512],
                                start=(ei == 0 and k == 0),
                                stop=(ei == psz - 1 and k == 3)),
                                wait=(up_done + [pb.dve.now(), pb.act.now()]
                                      if (ei == 0 and k == 0) else ()), inc=pb.pe)
                    pb.op("vector", lambda v, ch=ch, mt=mt, n=n: v.tensor_tensor(
                        outb[:, ch * 4 + mt, n * 512:(n + 1) * 512],
                        outb[:, ch * 4 + mt, n * 512:(n + 1) * 512],
                        T["ps_c"][:, 0:512], ALU.add), wait=[pb.pe.now()], inc=pb.dve)
        e0 += psz

    pb.dmaop("gpsimd", lambda g: g.dma_start(
        out=wsu[:, :, :], in_=T["shu_i"].rearrange("(i p) f -> p i f", p=128)),
        wait=[pb.pe.now()])
    pb.dmaop("gpsimd", lambda g: g.dma_start(
        out=wsd[:, :, :], in_=T["shd_i"].rearrange("(k p) f -> p k f", p=128)))
    wl = [pb.dmag.now()]
    for ch in range(4):
        for mt in range(16):
            for i in range(NE):
                pb.op("tensor", lambda t, ch=ch, mt=mt, i=i: t.matmul(
                    T["ps_ab"][:, (mt % 4) * 512:(mt % 4) * 512 + 512],
                    wsu[:, i, mt * 128:(mt + 1) * 128],
                    xlnT[:, i, ch * 512:(ch + 1) * 512],
                    start=(i == 0), stop=(i == NE - 1)),
                    wait=(wl + [pb.act.now(), pb.dve.now()]
                          if i == 0 else ()), inc=pb.pe)
            pb.op("scalar", lambda s, mt=mt: s.activation(
                T["rl"][:, (mt % 4) * 512:(mt % 4) * 512 + 512],
                T["ps_ab"][:, (mt % 4) * 512:(mt % 4) * 512 + 512], AF.Relu),
                wait=[pb.pe.now()], inc=pb.act)
            pb.op("vector", lambda v, mt=mt: v.tensor_mul(
                acts[:, mt, :], T["rl"][:, (mt % 4) * 512:(mt % 4) * 512 + 512],
                T["rl"][:, (mt % 4) * 512:(mt % 4) * 512 + 512]),
                wait=[pb.act.now()], inc=pb.dve)
        sh_done = [pb.dve.now()]
        for mt in range(4):
            for n in range(2):
                for k in range(16):
                    pb.op("tensor", lambda t, k=k, mt=mt, n=n: t.matmul(
                        T["ps_c"][:, 0:512], acts[:, k, mt * 128:(mt + 1) * 128],
                        wsd[:, k, n * 512:(n + 1) * 512],
                        start=(k == 0), stop=(k == 15)),
                        wait=(sh_done + [pb.dve.now(), pb.act.now()]
                              if k == 0 else ()), inc=pb.pe)
                pb.op("vector", lambda v, ch=ch, mt=mt, n=n: v.tensor_tensor(
                    outb[:, ch * 4 + mt, n * 512:(n + 1) * 512],
                    outb[:, ch * 4 + mt, n * 512:(n + 1) * 512],
                    T["ps_c"][:, 0:512], ALU.add), wait=[pb.pe.now()], inc=pb.dve)
    moe_done = [pb.dve.now()]

    for j in range(NT):
        pb.dmaop("sync", lambda s, j=j: s.dma_start(
            out=T["xt"][:, :], in_=T["x_in"][j * 128:(j + 1) * 128, :]),
            wait=[pb.dve.now()])
        emit_ln_tile(pb, outb[:, j, :], T["un"][:, :], T["stats"][:, :],
                     moe_done + [pb.dma.now()])
        pb.op("vector", lambda v: v.tensor_add(T["xt"][:, :], T["xt"][:, :],
                                               T["un"][:, :]),
              wait=[pb.dve.now(), pb.dma.now()], inc=pb.dve)
        pb.dmaop("sync", lambda s, j=j: s.dma_start(
            out=T["x_out"][j * 128:(j + 1) * 128, :], in_=T["xt"][:, :]),
            wait=[pb.dve.now()])
    pb.op("sync", None, wait=[pb.dma.now()])


def sbuf_p4(nc, T, ar, ps=None):
    sb = lambda n, shp, dt: T.__setitem__(n, ar.take(shp, dt))
    sb("xlnT4", [128, NE * TCORE], BF16)
    sb("wz4", [128, 6 * 8 * 512 + 6 * 4 * 1024], BF16)
    sb("outb", [128, NT * D], BF16)
    sb("act6", [128, 6 * 4 * 512], BF16)
    sb("rl", [128, 2048], BF16)
    sb("cbc", [128, 512], BF16)
    sb("crow", [1, 512], BF16)
    sb("rw_sb", [128, 8, 64], BF16)
    sb("rsc", [128, NT, 64], F32)
    sb("Cr", [128, NT, 64], BF16)
    sb("sfc", [128, 64], F32)
    sb("msk", [128, 64], F32)
    sb("sel", [128, 64], F32)
    sb("gs", [128, 8], F32)
    sb("gm", [128, 8], F32)
    sb("m8", [128, 8], F32)
    sb("cbias_sb", [128, 64], F32)
    sb("ctb", [64, 128], BF16)
    sb("stats", [128, 16], F32)
    sb("un", [128, D], BF16)
    sb("xt", [128, D], F32)
    sb("id_sb", [128, 128], BF16)
    sb("ones_sb", [128, 512], BF16)
    sb("cpool", [128, 8], F32)
    return psum_std(nc, T, split_ab=False, ps=ps)


def decls_p4(nc, T, xio=True):
    def inp(name, shape, dtype=F32):
        T[name] = nc.dram_tensor(name, list(shape), dtype, kind="ExternalInput").ap()

    if xio:
        inp("x_in", [TCORE, D])
        T["x_out"] = nc.dram_tensor("x_out", [TCORE, D], F32,
                                    kind="ExternalOutput").ap()
    inp("rwT_i", [D, 64])
    inp("cbias_i", [128, 64])
    inp("upw_i", [64, D, 512])
    inp("dnw_i", [64, 512, D])
    inp("shu_i", [D, 2048])
    inp("shd_i", [2048, D])
    T["CTd"] = nc.dram_tensor("CTd", [64, TCORE], BF16, kind="Internal")
    decl_shared(nc, T)


def build_p4(limit=None):
    nc = bass.Bass("TRN2", target_bir_lowering=False)
    T = {}
    decls_p4(nc, T)
    sbuf_p4(nc, T, Arena(nc, "G", ARENA_N))

    pb = PB(nc)
    pb.limit = limit
    pb.cpool = T["cpool"][:, :]
    emit_phase4(pb, T)
    with nc.Block() as blk:
        pb.replay(blk)
    return nc


def host_inputs_p4(full, x_stream=None):
    rwT = np.ascontiguousarray(np.asarray(full["L1_ffn_router_w"], np.float32).T)
    cbias = bcast_np(np.asarray(full["L1_ffn_correction_bias"], np.float32))
    upw = np.ascontiguousarray(np.asarray(full["L1_ffn_up_w"], np.float32))
    dnw = np.ascontiguousarray(np.asarray(full["L1_ffn_down_w"], np.float32))
    shu = np.ascontiguousarray(np.asarray(full["L1_ffn_shared_up_w"], np.float32))
    shd = np.ascontiguousarray(np.asarray(full["L1_ffn_shared_down_w"], np.float32))
    maps = []
    for c in range(8):
        b, q = c // 4, c % 4
        t0 = q * TCORE
        maps.append(dict(
            **({"x_in": np.ascontiguousarray(x_stream[b, t0:t0 + TCORE])}
               if x_stream is not None else {}),
            rwT_i=rwT, cbias_i=cbias, upw_i=upw, dnw_i=dnw, shu_i=shu, shd_i=shd))
    return maps


def sbuf_p1(nc, T, ar, ps=None):
    sb = lambda n, shp, dt: T.__setitem__(n, ar.take(shp, dt))
    sb("reads", [128, NT * D], BF16)
    sb("xln", [128, NT * D], BF16)
    sb("xlnT", [128, NE, TCORE + 1], BF16)
    sb("v_sb", [128, NT, D], BF16)
    sb("wt_sb", [128, NE, D], BF16)
    sb("wbuf", [128, NE, D], BF16)
    sb("stats", [128, NT, 16], F32)
    sb("gcol", [128, 16], F32)
    sb("gl", [128, 1], F32)
    sb("al", [128, 1], F32)
    sb("g_upd", [128, NQC], F32)
    sb("g_qb", [128, NQC], F32)
    sb("g_glob", [128, NT], F32)
    sb("g_pw", [128, 4], F32)
    sb("pw_mask", [128, 4], F32)
    sb("m_full", [128, NQC, 128], F32)
    sb("smt", [128, 128], BF16)
    sb("ev1", [128, D], F32)
    sb("xt", [128, D], F32)
    sb("zl", [128, D], BF16)
    sb("halo", [1, D + 32], F32)
    sb("id_sb", [128, 128], BF16)
    sb("cpool", [128, 8], F32)
    sb("idf_sb", [128, 128], F32)
    return psum_std(nc, T, split_ab=True, ps=ps)


def decls_p1(nc, T, xio=True):
    def inp(name, shape, dtype=F32):
        T[name] = nc.dram_tensor(name, list(shape), dtype, kind="ExternalInput").ap()

    if xio:
        T["x_out"] = nc.dram_tensor("x_out", [TCORE, D], F32,
                                    kind="ExternalOutput").ap()
    inp("x_in", [TCORE, D]); inp("xhalo", [1, D])
    inp("WwT", [D, D]); inp("WrT", [D, D])
    inp("dec_sc", [128, 1]); inp("la_sc", [128, 1])
    inp("e_upd", [128, NQC]); inp("e_qb", [128, NQC]); inp("e_glob", [128, NT])
    inp("e_m1", [128, 128]); inp("m_tri", [128, 128])
    inp("pexp", [128, 4]); inp("pmask", [128, 4])
    T["wdram"] = nc.dram_tensor("wdram", [TCORE + 1, D], BF16, kind="Internal")
    T["halod"] = nc.dram_tensor("halod", [1, D], BF16, kind="Internal")
    T["ccA_in"] = nc.dram_tensor("ccA_in", [D, D], BF16, kind="Internal")
    T["ccA_out"] = nc.dram_tensor("ccA_out", [4 * D, D], BF16, kind="Internal")
    decl_shared(nc, T)
    T["identf"] = nc.inline_tensor(np.eye(128, dtype=np.float32), name="identf").ap()


def build_p1(limit=None):
    nc = bass.Bass("TRN2", target_bir_lowering=False)
    T = {}
    decls_p1(nc, T)
    sbuf_p1(nc, T, Arena(nc, "G", ARENA_N))

    pb = PB(nc)
    pb.limit = limit
    emit_phase1(pb, T)
    with nc.Block() as blk:
        pb.replay(blk)
    return nc


def host_inputs_p1(full):
    x = np.asarray(full["L0_attn_out"], np.float32)
    WwT = np.ascontiguousarray(np.asarray(full["L0_attn_W_write"], np.float32).T)
    WrT = np.ascontiguousarray(np.asarray(full["L0_attn_W_read"], np.float32).T)
    dec = np.full((128, 1), np.float32(full["L0_attn_decay"]), np.float32)
    la = np.full((128, 1), np.float32(full["L0_attn_log_alpha"]), np.float32)
    p = np.arange(128, dtype=np.float32)[:, None]
    e_upd = np.concatenate([UB - 1 - (c * 128 + p) for c in range(NQC)], axis=1)
    e_qb = np.concatenate([(c * 128 + p) for c in range(NQC)], axis=1)
    e_glob = np.concatenate([(j * 128 + p) for j in range(NT)], axis=1)
    pk = np.arange(128, dtype=np.float32)
    e_m1 = pk[None, :] - 1.0 - pk[:, None]
    m_tri = (pk[None, :] > pk[:, None]).astype(np.float32)
    maps = []
    for c in range(8):
        b, q = c // 4, c % 4
        t0 = q * TCORE
        xs = np.ascontiguousarray(x[b, t0:t0 + TCORE])
        halo = (np.ascontiguousarray(x[b, t0 - 1:t0]) if q > 0
                else np.zeros((1, D), np.float32))
        pexp = np.zeros((128, 4), np.float32)
        pmask = np.zeros((128, 4), np.float32)
        for j in range(4):
            if j < q:
                pexp[:, j] = TCORE * (q - 1 - j)
                pmask[:, j] = 1.0
        maps.append(dict(x_in=xs, xhalo=halo, WwT=WwT, WrT=WrT, dec_sc=dec, la_sc=la,
                         e_upd=np.ascontiguousarray(e_upd),
                         e_qb=np.ascontiguousarray(e_qb),
                         e_glob=np.ascontiguousarray(e_glob),
                         e_m1=np.ascontiguousarray(e_m1), m_tri=m_tri,
                         pexp=pexp, pmask=pmask))
    return maps


def build_fused(limit=None):
    nc = bass.Bass("TRN2", target_bir_lowering=False)
    ar = Arena(nc, "G", ARENA_N)
    xcur = nc.dram_tensor("xcur", [TCORE, D], F32, kind="Internal")
    shared = {}
    decl_shared(nc, shared)
    pb = PB(nc)
    pb.limit = limit
    ps = None
    phases = ((decls_p1, sbuf_p1, emit_phase1),
              (decls_p2, sbuf_p2, emit_phase2),
              (decls_p3, sbuf_p3, emit_phase3),
              (decls_p4, sbuf_p4, emit_phase4))
    for i, (decls, sbufs, emit) in enumerate(phases):
        T = dict(shared)
        decls(nc, T, xio=False)
        if i > 0:
            T["x_in"] = xcur.ap()
        if i == 3:
            T["x_out"] = nc.dram_tensor("x_out", [TCORE, D], F32,
                                        kind="ExternalOutput").ap()
        else:
            T["x_out"] = xcur.ap()
        ar.reset()
        ps = sbufs(nc, T, ar, ps)
        if i == 1:
            T["hbuf_v"] = T["wz"].ap()[:, 0:8 * HW].rearrange(
                "p (a b) -> p a b", b=HW)
        pb.cpool = T["cpool"][:, :]
        emit(pb, T)
        if i < 3:
            pb.barrier()
    with nc.Block() as blk:
        pb.replay(blk)
    return nc


def host_inputs_fused(inputs):
    m1 = host_inputs_p1(inputs)
    m2 = host_inputs_p2(inputs, None)
    m3 = host_inputs_p3(inputs, None)
    m4 = host_inputs_p4(inputs, None)
    maps = []
    for c in range(8):
        mp = {}
        for m in (m1[c], m2[c], m3[c], m4[c]):
            mp.update(m)
        maps.append(mp)
    return maps


def kernel(**inputs):
    nc = build_fused()
    maps = host_inputs_fused(inputs)
    res = run_bass_kernel_spmd(nc, maps, core_ids=list(range(8)))
    out = np.zeros((2, 8192, D), np.float32)
    for c in range(8):
        b, q = c // 4, c % 4
        out[b, q * TCORE:(q + 1) * TCORE] = res.results[c]["x_out"]
    return out


def kernel_unfused(**inputs):
    x = np.asarray(inputs["L0_attn_out"], np.float32)
    for builder, hoster in ((build_p1, None),
                            (build_p2, host_inputs_p2),
                            (build_p3, host_inputs_p3),
                            (build_p4, host_inputs_p4)):
        if hoster is None:
            nc = build_p1()
            maps = host_inputs_p1(inputs)
        else:
            nc = builder()
            maps = hoster(inputs, x)
        res = run_bass_kernel_spmd(nc, maps, core_ids=list(range(8)))
        x = np.zeros((2, 8192, D), np.float32)
        for c in range(8):
            b, q = c // 4, c % 4
            x[b, q * TCORE:(q + 1) * TCORE] = res.results[c]["x_out"]
    return x


# version 05
# speedup vs baseline: 1.1432x; 1.1432x over previous
import numpy as np
import ml_dtypes
from contextlib import ExitStack

import concourse.bass as bass
import concourse.mybir as mybir
from concourse.bass_utils import run_bass_kernel_spmd

F32 = mybir.dt.float32
BF16 = mybir.dt.bfloat16
AF = mybir.ActivationFunctionType
ALU = mybir.AluOpType
AX = mybir.AxisListType
BFNP = ml_dtypes.bfloat16

D = 1024
TCORE = 2048
NT = TCORE // 128
NE = D // 128
GROUPS = [[0, 1, 2, 3], [4, 5, 6, 7]]

QC = 128
UB = 512
NUB = TCORE // UB
NQC = UB // QC


class V:

    def __init__(self, ap):
        self._ap = ap

    def ap(self):
        return self._ap

    def __getitem__(self, k):
        return self._ap[k]


class Arena:

    def __init__(self, nc, name, nelem):
        self.t = nc.alloc_sbuf_tensor(name, [128, nelem], BF16)
        self.nelem = nelem
        self.cur = 0
        self.hi = 0

    def reset(self):
        self.cur = 0

    def take(self, shape, dtype):
        per = 1
        for s in shape[1:]:
            per *= s
        elems = per * (2 if dtype == F32 else 1)
        off = self.cur
        self.cur = (off + elems + 31) & ~31
        self.hi = max(self.hi, self.cur)
        assert self.cur <= self.nelem, f"arena overflow {self.cur} > {self.nelem}"
        ap = self.t.ap()[0:shape[0], off:off + elems]
        if dtype == F32:
            ap = ap.bitcast(F32)
        if len(shape) == 3:
            ap = ap.rearrange("p (a b) -> p a b", b=shape[2])
        elif len(shape) == 4:
            ap = ap.rearrange("p (a b c) -> p a b c", b=shape[2], c=shape[3])
        return V(ap)


def bcast_np(row, p=128):
    return np.ascontiguousarray(np.broadcast_to(np.asarray(row, np.float32), (p, len(row))))


class Ctr:
    def __init__(self, nc, name):
        self.sem = nc.alloc_semaphore(name=name)
        self.n = 0

    def now(self):
        return (self, self.n)


class PB:
    ENGINES = ("tensor", "vector", "scalar", "gpsimd", "sync")

    def __init__(self, nc):
        self.nc = nc
        self.seq = 0
        self.limit = None
        self.ops = {e: [] for e in self.ENGINES}
        self.pe = Ctr(nc, "c_pe")
        self.dve = Ctr(nc, "c_dve")
        self.act = Ctr(nc, "c_act")
        self.pool = Ctr(nc, "c_pool")
        self.dma = Ctr(nc, "c_dma")
        self.dmag = Ctr(nc, "c_dmag")
        self.cc = Ctr(nc, "c_cc")
        self._waited = {e: {} for e in self.ENGINES}

    phase = "p?"

    def op(self, engine, fn, wait=(), inc=None, inck=1):
        selfc = {"vector": self.dve, "scalar": self.act, "gpsimd": self.pool}.get(engine)
        if selfc is not None and fn is not None:
            wait = list(wait) + [(selfc, selfc.n)]
        waits = []
        for w in wait:
            if isinstance(w, Ctr):
                w = (w, w.n)
            c, v = w
            if v > self._waited[engine].get(id(c), -1):
                waits.append((c, v))
                self._waited[engine][id(c)] = v
        self.seq += 1
        if self.limit is not None and self.seq > self.limit:
            return
        if inc is not None and fn is not None:
            inc.n += (1 if inck is None else inck)
        self.ops[engine].append((fn, waits, inc, inck, self.phase))

    def barrier(self):
        wl = [c.now() for c in (self.pe, self.dve, self.act, self.pool,
                                self.dma, self.dmag, self.cc)]
        for e in self.ENGINES:
            self.op(e, None, wait=wl)

    def dmaop(self, engine, fn, wait=(), ctr=None):
        if ctr is None:
            ctr = self.dma if engine == "sync" else self.dmag
        self.op(engine, fn, wait=list(wait) + [ctr.now()], inc=ctr, inck=16)

    def replay(self, block):
        for engine in self.ENGINES:
            ops = self.ops[engine]
            if not ops:
                continue

            def body(eng, ops=ops):
                cur = None
                for fn, waits, inc, inck, ph in ops:
                    if ph != cur:
                        if cur is not None:
                            self.nc.pop_named_scope(cur)
                        self.nc.push_named_scope(ph)
                        cur = ph
                    for c, v in waits:
                        eng.wait_ge(c.sem, v)
                    if fn is not None:
                        instr = fn(eng)
                        if inc is not None:
                            if inck is None:
                                instr.then_inc(inc.sem)
                            else:
                                instr.then_inc(inc.sem, inck)
                if cur is not None:
                    self.nc.pop_named_scope(cur)
            getattr(block, engine)(body)


CP_EPS, CP_NEGH, CP_NEG30, CP_ONE = 0, 1, 2, 3
CPOOL_VALS = [1e-5, -0.5, -30.0, 1.0, 0.0, 0.0, 0.0, 0.0]


def emit_ln_tile(pb, src, dst, st, wait, dst2=None):
    cp = pb.cpool
    pb.op("vector", lambda v: v.bn_stats(st[:, 0:6], src[:, 0:512]), wait=wait, inc=pb.dve)
    pb.op("vector", lambda v: v.bn_stats(st[:, 6:12], src[:, 512:1024]), inc=pb.dve)
    pb.op("vector", lambda v: v.bn_aggr(st[:, 12:14], st[:, 0:12]),
          wait=[pb.dve.now()], inc=pb.dve)
    pb.op("scalar", lambda s: s.activation(st[:, 14:15], st[:, 13:14], AF.Ln,
                                           bias=cp[:, CP_EPS:CP_EPS + 1]),
          wait=[pb.dve.now()], inc=pb.act)
    pb.op("scalar", lambda s: s.activation(st[:, 14:15], st[:, 14:15], AF.Exp,
                                           scale=cp[:, CP_NEGH:CP_NEGH + 1]),
          wait=[pb.act.now()], inc=pb.act)
    pb.op("vector", lambda v: v.tensor_scalar(dst, src, st[:, 12:13], st[:, 14:15],
                                              ALU.subtract, ALU.mult),
          wait=[pb.act.now()], inc=pb.dve)
    if dst2 is not None:
        pb.op("vector", lambda v: v.tensor_scalar(dst2, src, st[:, 12:13], st[:, 14:15],
                                                  ALU.subtract, ALU.mult), inc=pb.dve)


def emit_mm_tiled(pb, out_ps, lhsT_fn, rhs_fn, nk, nn, wait, evict=None,
                  evict_engine="scalar", n_width=512):
    for n in range(nn):
        for i in range(nk):
            pb.op("tensor", lambda t, n=n, i=i: t.matmul(
                out_ps[:, n * n_width:(n + 1) * n_width], lhsT_fn(i), rhs_fn(i, n),
                start=(i == 0), stop=(i == nk - 1)),
                wait=(wait if (n == 0 and i == 0) else ()), inc=pb.pe)
    if evict is not None:
        pb.op(evict_engine, evict, wait=[pb.pe.now()],
              inc=(pb.act if evict_engine == "scalar" else pb.dve))


def emit_phase1(pb, T):
    nc = pb.nc
    XL = T["xln"]
    xln = XL.ap().rearrange("p (j f) -> p j f", f=D)
    readsT = XL.ap().rearrange("p (i t) -> p i t", t=TCORE)
    reads = T["reads"].ap().rearrange("p (j f) -> p j f", f=D)
    cp0 = T["cpool"]

    pb.cpool = T["cpool"]
    pb.dmaop("sync", lambda s: s.dma_start(out=T["cpool"][:, :], in_=T["cpool_c"]))
    pb.dmaop("sync", lambda s: s.dma_start(out=T["halo"][0:1, 0:D], in_=T["xhalo"]))
    pb.dmaop("sync", lambda s: s.dma_start(out=T["id_sb"][:, :], in_=T["ident"]))
    pb.dmaop("sync", lambda s: s.dma_start(out=T["idf_sb"][:, :], in_=T["identf"]))
    pb.dmaop("sync", lambda s: s.dma_start(out=T["gl"][:, :], in_=T["dec_sc"]))
    pb.dmaop("sync", lambda s: s.dma_start(out=T["al"][:, :], in_=T["la_sc"]))
    pb.dmaop("sync", lambda s: s.dma_start(out=T["g_upd"][:, :], in_=T["e_upd"]))
    pb.dmaop("sync", lambda s: s.dma_start(out=T["g_qb"][:, :], in_=T["e_qb"]))
    pb.dmaop("sync", lambda s: s.dma_start(out=T["g_glob"][:, :], in_=T["e_glob"]))
    pb.dmaop("sync", lambda s: s.dma_start(out=T["g_pw"][:, :], in_=T["pexp"]))
    pb.dmaop("sync", lambda s: s.dma_start(out=T["m_full"][:, 0, :], in_=T["m_tri"]))
    pb.dmaop("sync", lambda s: s.dma_start(out=T["m_full"][:, 1, :], in_=T["e_m1"]))
    pb.dmaop("sync", lambda s: s.dma_start(out=T["pw_mask"][:, :], in_=T["pmask"]))
    ld0 = [pb.dma.now()]

    pb.op("scalar", lambda s: s.activation(T["gl"][:, :], T["gl"][:, :], AF.Sigmoid),
          wait=ld0, inc=pb.act)
    pb.op("scalar", lambda s: s.activation(T["gl"][:, :], T["gl"][:, :], AF.Ln),
          wait=[pb.act.now()], inc=pb.act)
    pb.op("scalar", lambda s: s.activation(T["al"][:, :], T["al"][:, :], AF.Exp),
          inc=pb.act)
    wgl = [pb.act.now(), pb.dma.now()]
    for name in ("g_upd", "g_qb", "g_glob", "g_pw"):
        pb.op("vector", lambda v, name=name: v.tensor_scalar_mul(
            T[name][:, :], T[name][:, :], T["gl"][:, 0:1]), wait=wgl, inc=pb.dve)
    pb.op("vector", lambda v: v.tensor_scalar_mul(
        T["m_full"][:, 1, :], T["m_full"][:, 1, :], T["gl"][:, 0:1]), wait=wgl, inc=pb.dve)
    wv = [pb.dve.now()]
    for name in ("g_upd", "g_qb", "g_glob", "g_pw"):
        pb.op("scalar", lambda s, name=name: s.activation(
            T[name][:, :], T[name][:, :], AF.Exp), wait=wv, inc=pb.act)
    pb.op("scalar", lambda s: s.activation(T["m_full"][:, 1, :], T["m_full"][:, 1, :],
                                           AF.Exp), wait=wv, inc=pb.act)
    wa = [pb.act.now()]
    pb.op("vector", lambda v: v.tensor_mul(T["g_pw"][:, :], T["g_pw"][:, :],
                                           T["pw_mask"][:, :]), wait=wa, inc=pb.dve)
    pb.op("vector", lambda v: v.tensor_mul(T["m_full"][:, 0, :], T["m_full"][:, 0, :],
                                           T["m_full"][:, 1, :]), wait=wa, inc=pb.dve)
    pb.op("vector", lambda v: v.tensor_scalar_mul(T["gcol"][:, 0:1], T["gl"][:, 0:1],
                                                  128.0), wait=wgl, inc=pb.dve)
    pb.op("scalar", lambda s: s.activation(T["gcol"][:, 0:1], T["gcol"][:, 0:1], AF.Exp),
          wait=[pb.dve.now()], inc=pb.act)
    pb.op("vector", lambda v: v.tensor_mul(T["gcol"][:, 1:2], T["gcol"][:, 0:1],
                                           T["gcol"][:, 0:1]), wait=[pb.act.now()], inc=pb.dve)
    pb.op("vector", lambda v: v.tensor_mul(T["gcol"][:, 2:3], T["gcol"][:, 1:2],
                                           T["gcol"][:, 1:2]), wait=[pb.dve.now()], inc=pb.dve)
    pb.op("vector", lambda v: v.tensor_scalar_mul(T["m_full"][:, 1, :], T["m_full"][:, 1, :],
                                                  T["gcol"][:, 0:1]), wait=[pb.dve.now()],
          inc=pb.dve)
    pb.op("vector", lambda v: v.tensor_scalar_mul(T["m_full"][:, 2, :], T["m_full"][:, 1, :],
                                                  T["gcol"][:, 0:1]), wait=[pb.dve.now()],
          inc=pb.dve)
    pb.op("vector", lambda v: v.tensor_scalar_mul(T["m_full"][:, 3, :], T["m_full"][:, 1, :],
                                                  T["gcol"][:, 1:2]), inc=pb.dve)
    gamma_ready = [pb.dve.now(), pb.act.now()]

    for j in range(NT):
        pb.dmaop("sync", lambda s, j=j: s.dma_start(
            out=T["xt"][:, :], in_=T["x_in"][j * 128:(j + 1) * 128, :]),
            wait=[pb.dve.now()])
        emit_ln_tile(pb, T["xt"][:, :], xln[:, j, :], T["stats"][:, j, :],
                     [pb.dma.now()])
    xln_done = [pb.dve.now()]

    H = T["halo"]
    pb.op("vector", lambda v: v.bn_stats(H[0:1, D:D + 6], H[0:1, 0:512]), wait=ld0,
          inc=pb.dve)
    pb.op("vector", lambda v: v.bn_stats(H[0:1, D + 6:D + 12], H[0:1, 512:1024]),
          inc=pb.dve)
    pb.op("vector", lambda v: v.bn_aggr(H[0:1, D + 16:D + 18], H[0:1, D:D + 12]),
          wait=[pb.dve.now()], inc=pb.dve)
    pb.op("scalar", lambda s: s.activation(H[0:1, D + 18:D + 19], H[0:1, D + 17:D + 18],
                                           AF.Ln, bias=cp0[0:1, CP_EPS:CP_EPS + 1]),
          wait=[pb.dve.now()], inc=pb.act)
    pb.op("scalar", lambda s: s.activation(H[0:1, D + 18:D + 19], H[0:1, D + 18:D + 19],
                                           AF.Exp, scale=cp0[0:1, CP_NEGH:CP_NEGH + 1]),
          wait=[pb.act.now()], inc=pb.act)
    pb.op("vector", lambda v: v.tensor_scalar(H[0:1, 0:D], H[0:1, 0:D],
                                              H[0:1, D + 16:D + 17], H[0:1, D + 18:D + 19],
                                              ALU.subtract, ALU.mult),
          wait=[pb.act.now()], inc=pb.dve)
    halo_done = [pb.dve.now()]

    pb.op("scalar", lambda s: s.copy(T["zl"][0:1, :], H[0:1, 0:D]), wait=halo_done,
          inc=pb.act)
    pb.dmaop("gpsimd", lambda g: g.dma_start(out=T["wdram"].ap()[0:1, :], in_=H[0:1, 0:D]),
             wait=halo_done)
    for i in range(NE):
        pb.op("tensor", lambda t, i=i: t.transpose(
            T["ps_tr"][:, 0:1], T["zl"][0:1, i * 128:(i + 1) * 128],
            T["id_sb"][0:1, 0:1]), wait=[pb.act.now()], inc=pb.pe)
        pb.op("scalar", lambda s, i=i: s.copy(T["xlnT"][:, i, 0:1], T["ps_tr"][:, 0:1]),
              wait=[pb.pe.now()], inc=pb.act)
    pb.dmaop("gpsimd", lambda g: g.dma_start(
        out=T["wdram"].ap()[1:TCORE + 1, :].rearrange("(j p) f -> p j f", p=128),
        in_=xln), wait=xln_done)
    wdram_done = [pb.dmag.now()]

    for idx, (j, i) in enumerate([(j, i) for j in range(NT) for i in range(NE)]):
        pb.op("tensor", lambda t, j=j, i=i: t.transpose(
            T["ps_tr"][:, :], xln[:, j, i * 128:(i + 1) * 128], T["id_sb"][:, :]),
            wait=(xln_done + [pb.act.now()] if idx == 0 else [pb.act.now()]), inc=pb.pe)
        pb.op("scalar", lambda s, j=j, i=i: s.copy(
            T["xlnT"][:, i, 1 + j * 128:1 + (j + 1) * 128], T["ps_tr"][:, :]),
            wait=[pb.pe.now()], inc=pb.act)
    xlnT_done = [pb.act.now(), pb.dmag.now()]

    pb.dmaop("gpsimd", lambda g: g.dma_start(
        out=T["wbuf"][:, :, :], in_=T["WwT"].rearrange("(i p) f -> p i f", p=128)))
    wwt = [pb.dmag.now()]
    for j in range(NT):
        emit_mm_tiled(pb, T["ps_c"],
                      lambda i, j=j: T["xlnT"][:, i, 1 + j * 128:1 + (j + 1) * 128],
                      lambda i, n: T["wbuf"][:, i, n * 512:(n + 1) * 512],
                      NE, 2, xlnT_done + wwt + [pb.act.now()],
                      evict=lambda s, j=j: s.copy(T["v_sb"][:, j, :], T["ps_c"][:, :]))
    v_done = [pb.act.now()]

    pb.op("vector", lambda v: v.memset(T["wt_sb"][:, :, :], 0.0), inc=pb.dve)
    for ub in range(NUB):
        for qc in range(NQC):
            jt = ub * NQC + qc
            col0 = 1 + jt * 128
            emit_mm_tiled(pb, T["ps_b"],
                          lambda i, col0=col0: T["xlnT"][:, i, col0:col0 + 128],
                          lambda i, n: T["wt_sb"][:, i, n * 512:(n + 1) * 512],
                          NE, 2,
                          xlnT_done + v_done + gamma_ready +
                          [pb.dve.now(), pb.act.now()])
            inter_mm = pb.pe.now()
            for kc in range(qc + 1):
                kcol0 = 1 + (ub * NQC + kc) * 128
                for i in range(NE):
                    pb.op("tensor", lambda t, kcol0=kcol0, col0=col0, i=i: t.matmul(
                        T["ps_s"][:, :],
                        T["xlnT"][:, i, kcol0 - 1:kcol0 + 127],
                        T["xlnT"][:, i, col0:col0 + 128],
                        start=(i == 0), stop=(i == NE - 1)),
                        wait=([pb.dve.now()] if i == 0 else ()), inc=pb.pe)
                pb.op("vector", lambda v, d=qc - kc: v.tensor_tensor(
                    T["smt"][:, :], T["ps_s"][:, :], T["m_full"][:, d, :], ALU.mult),
                    wait=[pb.pe.now()], inc=pb.dve)
                kt = ub * NQC + kc
                for n in range(2):
                    pb.op("tensor", lambda t, kt=kt, n=n, kc=kc, qc=qc: t.matmul(
                        T["ps_a"][:, n * 512:(n + 1) * 512],
                        T["smt"][:, :],
                        T["v_sb"][:, kt, n * 512:(n + 1) * 512],
                        start=(kc == 0), stop=(kc == qc)),
                        wait=([pb.dve.now()] if n == 0 else ()), inc=pb.pe)
            pb.op("scalar", lambda s, qc=qc: s.activation(
                T["ev1"][:, :], T["ps_b"][:, :], AF.Copy,
                scale=T["g_qb"][:, qc:qc + 1]), wait=[pb.pe.now()], inc=pb.act)
            pb.op("vector", lambda v, jt=jt: v.tensor_add(
                reads[:, jt, :], T["ev1"][:, :], T["ps_a"][:, :]),
                wait=[pb.act.now()], inc=pb.dve)
        blk0 = ub * UB
        pb.dmaop("sync", lambda s, blk0=blk0: s.dma_start(
            out=T["wbuf"][:, 0:NQC, :],
            in_=T["wdram"].ap()[blk0:blk0 + UB, :].rearrange("(c p) f -> p c f", p=128)),
            wait=wdram_done + v_done + [pb.dve.now()])
        for c in range(NQC):
            pb.op("vector", lambda v, c=c: v.tensor_scalar_mul(
                T["wbuf"][:, c, :], T["wbuf"][:, c, :], T["g_upd"][:, c:c + 1]),
                wait=[pb.dma.now()], inc=pb.dve)
        wsc = [pb.dve.now()]
        for et in range(NE):
            for n in range(2):
                for c in range(NQC):
                    pb.op("tensor", lambda t, et=et, n=n, c=c, ub=ub: t.matmul(
                        T["ps_c"][:, n * 512:(n + 1) * 512],
                        T["wbuf"][:, c, et * 128:(et + 1) * 128],
                        T["v_sb"][:, ub * NQC + c, n * 512:(n + 1) * 512],
                        start=(c == 0), stop=(c == NQC - 1)),
                        wait=(wsc + [pb.dve.now()] if (n == 0 and c == 0) else ()),
                        inc=pb.pe)
            pb.op("vector", lambda v, et=et: v.scalar_tensor_tensor(
                T["wt_sb"][:, et, :], T["wt_sb"][:, et, :], T["gcol"][:, 2:3],
                T["ps_c"][:, :], ALU.mult, ALU.add), wait=[pb.pe.now()], inc=pb.dve)
    scan_done = [pb.dve.now()]

    pb.dmaop("gpsimd", lambda g: g.dma_start(
        out=T["ccA_in"].ap().rearrange("(i p) f -> p i f", p=128),
        in_=T["wt_sb"][:, :, :]), wait=scan_done)
    pb.op("gpsimd", lambda g: g.collective_compute(
        "AllGather", ALU.bypass, replica_groups=GROUPS,
        ins=[T["ccA_in"].ap().opt()], outs=[T["ccA_out"].ap().opt()]),
        wait=[pb.dmag.now()], inc=pb.cc, inck=None)
    win = T["wt_sb"]
    for j in range(4):
        pb.dmaop("sync", lambda s, j=j: s.dma_start(
            out=T["wbuf"][:, :, :],
            in_=T["ccA_out"].ap()[j * D:(j + 1) * D, :].rearrange("(i p) f -> p i f", p=128)),
            wait=[pb.cc.now(), pb.dve.now()])
        if j == 0:
            pb.op("vector", lambda v: v.tensor_scalar_mul(
                win[:, :, :], T["wbuf"][:, :, :], T["g_pw"][:, 0:1]),
                wait=[pb.dma.now()], inc=pb.dve)
        else:
            pb.op("vector", lambda v, j=j: v.scalar_tensor_tensor(
                win[:, :, :], T["wbuf"][:, :, :], T["g_pw"][:, j:j + 1],
                win[:, :, :], ALU.mult, ALU.add), wait=[pb.dma.now()], inc=pb.dve)
    win_done = [pb.dve.now()]

    for j in range(NT):
        emit_mm_tiled(pb, T["ps_c"],
                      lambda i, j=j: T["xlnT"][:, i, 1 + j * 128:1 + (j + 1) * 128],
                      lambda i, n: win[:, i, n * 512:(n + 1) * 512],
                      NE, 2, win_done + [pb.dve.now()])
        pb.op("vector", lambda v, j=j: v.scalar_tensor_tensor(
            reads[:, j, :], T["ps_c"][:, :], T["g_glob"][:, j:j + 1],
            reads[:, j, :], ALU.mult, ALU.add), wait=[pb.pe.now()], inc=pb.dve)
    ginter_done = [pb.dve.now()]

    for idx, (j, i) in enumerate([(j, i) for j in range(NT) for i in range(NE)]):
        pb.op("tensor", lambda t, j=j, i=i: t.transpose(
            T["ps_tr"][:, :], reads[:, j, i * 128:(i + 1) * 128], T["id_sb"][:, :]),
            wait=(ginter_done + [pb.act.now()] if idx == 0 else [pb.act.now()]), inc=pb.pe)
        pb.op("scalar", lambda s, j=j, i=i: s.copy(
            readsT[:, i, j * 128:(j + 1) * 128], T["ps_tr"][:, :]),
            wait=[pb.pe.now()], inc=pb.act)
    readsT_done = [pb.act.now()]

    pb.dmaop("gpsimd", lambda g: g.dma_start(
        out=T["wbuf"][:, :, :], in_=T["WrT"].rearrange("(i p) f -> p i f", p=128)),
        wait=win_done)
    wrt = [pb.dmag.now()]
    for j in range(NT):
        pb.dmaop("sync", lambda s, j=j: s.dma_start(
            out=T["xt"][:, :], in_=T["x_in"][j * 128:(j + 1) * 128, :]),
            wait=[pb.dve.now()])
        pb.dmaop("sync", lambda s, j=j: s.dma_start(
            out=T["zl"][:, :], in_=T["wdram"].ap()[1 + j * 128:1 + (j + 1) * 128, :]))
        emit_mm_tiled(pb, T["ps_c"],
                      lambda i, j=j: readsT[:, i, j * 128:(j + 1) * 128],
                      lambda i, n: T["wbuf"][:, i, n * 512:(n + 1) * 512],
                      NE, 2, readsT_done + wrt + [pb.dve.now()])
        pb.op("vector", lambda v: v.scalar_tensor_tensor(
            T["ev1"][:, :], T["ps_c"][:, :], T["al"][:, 0:1], T["zl"][:, :],
            ALU.mult, ALU.add), wait=[pb.pe.now(), pb.dma.now()], inc=pb.dve)
        emit_ln_tile(pb, T["ev1"][:, :], T["ev1"][:, :], T["stats"][:, j, :],
                     [pb.dve.now()])
        pb.op("vector", lambda v: v.tensor_add(T["xt"][:, :], T["xt"][:, :],
                                               T["ev1"][:, :]),
              wait=[pb.dve.now()], inc=pb.dve)
        pb.dmaop("sync", lambda s, j=j: s.dma_start(
            out=T["x_out"][j * 128:(j + 1) * 128, :], in_=T["xt"][:, :]),
            wait=[pb.dve.now()])
    pb.op("sync", None, wait=[pb.dma.now()])


ARENA_N = 106000


def psum_std(nc, T, split_ab, ps=None):
    if ps is None:
        ps = dict(ab=nc.alloc_psum_tensor("ps_ab", [128, 2048], F32),
                  c=nc.alloc_psum_tensor("ps_c", [128, D], F32),
                  s=nc.alloc_psum_tensor("ps_s", [128, 128], F32),
                  tr=nc.alloc_psum_tensor("ps_tr", [128, 128], BF16))
    if split_ab:
        T["ps_a"] = V(ps["ab"].ap()[:, 0:1024])
        T["ps_b"] = V(ps["ab"].ap()[:, 1024:2048])
    else:
        T["ps_ab"] = V(ps["ab"].ap())
    T["ps_c"] = V(ps["c"].ap())
    T["ps_s"] = V(ps["s"].ap())
    T["ps_tr"] = V(ps["tr"].ap())
    return ps


def decl_shared(nc, T):
    if "ident" in T:
        return
    T["ident"] = nc.inline_tensor(np.eye(128, dtype=BFNP), name="ident").ap()
    T["ones_c"] = nc.inline_tensor(np.ones((128, 512), dtype=BFNP),
                                   name="ones_c").ap()
    T["cpool_c"] = nc.inline_tensor(
        np.broadcast_to(np.asarray(CPOOL_VALS, np.float32), (128, 8)).copy(),
        name="cpool_c").ap()


HW = 514
NTX = 20
HIDT = 22


def p2_views(T):
    V = {}
    V["uT"] = T["uT"]
    av = T["av"].ap()
    V["vn"] = av.rearrange("p (a b) -> p a b", b=512)
    V["actT"] = V["vn"]
    qk = T["qk"].ap()
    V["qn"] = qk[:, 0:16 * 512].rearrange("p (a b) -> p a b", b=512)
    V["kn"] = qk[:, 16 * 512:36 * 512].rearrange("p (a b) -> p a b", b=512)
    V["o_n"] = V["qn"]
    V["oT"] = qk[:, 16 * 512:16 * 512 + 4 * 2048].rearrange("p (a b) -> p a b", b=2048)
    tq = T["tq"].ap()
    V["kT"] = tq[:, 0:4 * 2560].rearrange("p (a b) -> p a b", b=2560)
    V["qT"] = tq[:, 4 * 2560:4 * 2560 + 4 * 2048].rearrange("p (a b) -> p a b", b=2048)
    V["hln"] = tq[:, 0:8 * 2048].rearrange("p (a b) -> p a b", b=2048)
    wz = T["wz"].ap()
    V["wq3"] = wz[:, 0:8 * 512].rearrange("p (a b) -> p a b", b=512)
    V["wo_sb"] = wz[:, 0:4 * 1024].rearrange("p (a b) -> p a b", b=1024)
    V["hbuf"] = wz[:, 0:8 * 514].rearrange("p (a b) -> p a b", b=514)
    V["lastrow"] = wz[0:1, 0:1024]
    V["selrow"] = wz[0:1, 1024:1024 + 4 * 1024].rearrange("o (r f) -> o r f", f=D)
    V["w1d"] = wz[:, 0:2 * 2 * 8 * 128].rearrange("p (d ag i f) -> p d ag i f",
                                                  d=2, ag=2, i=8)
    off = 2 * 2 * 8 * 128
    V["w2d"] = wz[:, off:off + 2 * HIDT * 128].rearrange("p (d k f) -> p d k f",
                                                         d=2, k=HIDT)
    V["tbl"] = T["tbl"]
    return V


def emit_halo_ag(pb, T, first):
    w = [pb.dve.now(), pb.act.now()]
    pb.dmaop("gpsimd", lambda g: g.dma_start(
        out=T["ccH_in"].ap().rearrange("(i p) t -> p i t", p=128),
        in_=T["uT"][:, :, HW + TCORE - HW:HW + TCORE]), wait=w)
    pb.op("gpsimd", lambda g: g.collective_compute(
        "AllGather", ALU.bypass, replica_groups=GROUPS,
        ins=[T["ccH_in"].ap().opt()], outs=[T["ccH_out"].ap().opt()]),
        wait=[pb.dmag.now()], inc=pb.cc, inck=None)
    for r in range(4):
        pb.dmaop("sync", lambda s, r=r: s.dma_start(
            out=T["hbuf_v"],
            in_=T["ccH_out"].ap()[r * D:(r + 1) * D, :].rearrange("(i p) t -> p i t", p=128)),
            wait=[pb.cc.now()] + ([pb.dve.now()] if r > 0 else []))
        if r == 0:
            pb.op("vector", lambda v: v.tensor_scalar_mul(
                T["uT"][:, :, 0:HW], T["hbuf_v"], T["selprev"][:, 0:1]),
                wait=[pb.dma.now()], inc=pb.dve)
        else:
            pb.op("vector", lambda v, r=r: v.scalar_tensor_tensor(
                T["uT"][:, :, 0:HW], T["hbuf_v"], T["selprev"][:, r:r + 1],
                T["uT"][:, :, 0:HW], ALU.mult, ALU.add),
                wait=[pb.dma.now()], inc=pb.dve)


def emit_mid_halo(pb, T, V):
    for i in range(NE):
        pb.op("tensor", lambda t, i=i: t.transpose(
            T["ps_tr"][:, 0:1] if False else T["ps_tr"][0:1, 0:128],
            T["uT"][:, i, HW + TCORE - 1:HW + TCORE], T["id_sb"][:, :]),
            wait=[pb.dve.now(), pb.act.now()] if i == 0 else [pb.act.now()], inc=pb.pe)
        pb.op("scalar", lambda s, i=i: s.copy(V["lastrow"][0:1, i * 128:(i + 1) * 128],
                                              T["ps_tr"][0:1, 0:128]),
              wait=[pb.pe.now()], inc=pb.act)
    pb.dmaop("gpsimd", lambda g: g.dma_start(out=T["ccM_in"].ap(),
                                             in_=V["lastrow"][0:1, :]),
             wait=[pb.act.now()])
    pb.op("gpsimd", lambda g: g.collective_compute(
        "AllGather", ALU.bypass, replica_groups=GROUPS,
        ins=[T["ccM_in"].ap().opt()], outs=[T["ccM_out"].ap().opt()]),
        wait=[pb.dmag.now()], inc=pb.cc, inck=None)
    pb.dmaop("sync", lambda s: s.dma_start(out=V["selrow"][0:1, :, :],
                                           in_=T["ccM_out"].ap()), wait=[pb.cc.now()])
    pb.op("vector", lambda v: v.tensor_scalar_mul(
        V["lastrow"][0:1, :], V["selrow"][0:1, 0, :], T["selprev1"][0:1, 0:1]),
        wait=[pb.dma.now()], inc=pb.dve)
    for r in range(1, 4):
        pb.op("vector", lambda v, r=r: v.scalar_tensor_tensor(
            V["lastrow"][0:1, :], V["selrow"][0:1, r, :], T["selprev1"][0:1, r:r + 1],
            V["lastrow"][0:1, :], ALU.mult, ALU.add), inc=pb.dve)
    for i in range(NE):
        pb.op("tensor", lambda t, i=i: t.transpose(
            T["ps_tr"][:, 0:1], V["lastrow"][0:1, i * 128:(i + 1) * 128],
            T["id_sb"][0:1, 0:1]),
            wait=[pb.dve.now()] if i == 0 else [pb.act.now()], inc=pb.pe)
        pb.op("scalar", lambda s, i=i: s.copy(T["uT"][:, i, HW - 1:HW],
                                              T["ps_tr"][:, 0:1]),
              wait=[pb.pe.now()], inc=pb.act)


def emit_inner_layer(pb, T, l):
    V = p2_views(T)
    uT, vn, qn, kn = V["uT"], V["vn"], V["qn"], V["kn"]
    cp = pb.cpool

    emit_halo_ag(pb, T, l == 0)
    halo_done = [pb.dve.now()]

    for nm in ("qs", "ks", "lg", "lb", "b1a", "b1g", "b2"):
        pb.dmaop("sync", lambda s, nm=nm: s.dma_start(
            out=T[nm + "_sb"][:, :], in_=T[f"{nm}{l}"]))
    prm = [pb.dma.now()]

    for n3 in range(3):
        pb.dmaop("gpsimd", lambda g, l=l, n3=n3: g.dma_start(
            out=V["wq3"][:, :, :],
            in_=T[f"wqkv{l}"][:, n3 * 512:(n3 + 1) * 512]
            .rearrange("(i p) f -> p i f", p=128)),
            wait=[pb.pe.now()] + halo_done)
        wq = [pb.dmag.now()]
        dstbuf = (qn, kn, vn)[n3]
        for jx in range(NTX):
            c0 = 2 + jx * 128
            if n3 == 0 and jx < 4:
                continue
            ps = T["ps_a"][:, 0:512] if jx % 2 == 0 else T["ps_a"][:, 512:1024]
            w0 = halo_done + wq + [pb.act.now()]
            for i in range(NE):
                pb.op("tensor", lambda t, c0=c0, i=i, ps=ps: t.matmul(
                    ps, T["uT"][:, i, c0 - (0 if i < 4 else 1):
                                c0 - (0 if i < 4 else 1) + 128],
                    V["wq3"][:, i, :],
                    start=(i == 0), stop=(i == NE - 1)),
                    wait=(w0 if i == 0 else ()), inc=pb.pe)
            pb.op("scalar", lambda s, jx=jx, n3=n3, ps=ps, dstbuf=dstbuf: s.copy(
                dstbuf[:, jx - (4 if n3 == 0 else 0), :], ps),
                wait=[pb.pe.now()], inc=pb.act)
    qkv_done = [pb.act.now()]

    def norm_rot(buf, jx, tblc, tbls, gain, wait0):
        b3 = buf[:, jx, :].rearrange("p (h d) -> p h d", d=64)
        pb.op("vector", lambda v: v.tensor_mul(T["sq"][:, :], buf[:, jx, :], buf[:, jx, :]),
              wait=wait0, inc=pb.dve)
        pb.op("vector", lambda v: v.tensor_reduce(
            T["nrm"][:, 0:8].rearrange("p (h o) -> p h o", o=1),
            T["sq"][:, :].rearrange("p (h d) -> p h d", d=64), AX.X, ALU.add),
            wait=[pb.dve.now()], inc=pb.dve)
        pb.op("scalar", lambda s: s.activation(T["nrm"][:, 0:8], T["nrm"][:, 0:8], AF.Sqrt),
              wait=[pb.dve.now()], inc=pb.act)
        pb.op("vector", lambda v: v.tensor_scalar_max(T["nrm"][:, 0:8], T["nrm"][:, 0:8],
                                                      1e-12), wait=[pb.act.now()], inc=pb.dve)
        pb.op("vector", lambda v: v.reciprocal(T["nrm"][:, 0:8], T["nrm"][:, 0:8]),
              wait=[pb.dve.now()], inc=pb.dve)
        pb.op("vector", lambda v: v.tensor_tensor(
            b3, b3, T["nrm"][:, 0:8].rearrange("p (h o) -> p h o", o=1)
            .to_broadcast([128, 8, 64]), ALU.mult), wait=[pb.dve.now()], inc=pb.dve)
        pb.op("vector", lambda v: v.tensor_tensor(
            b3, b3, gain.unsqueeze(1).to_broadcast([128, 8, 64]), ALU.mult),
            wait=[pb.dve.now()], inc=pb.dve)
        pb.op("vector", lambda v: v.tensor_tensor(
            T["t1"][:, :].rearrange("p (h d) -> p h d", d=64), b3,
            tblc.unsqueeze(1).to_broadcast([128, 8, 64]), ALU.mult),
            wait=[pb.dve.now()], inc=pb.dve)
        pb.op("vector", lambda v: v.tensor_tensor(
            T["t2"][:, :].rearrange("p (h d) -> p h d", d=64), b3,
            tbls.unsqueeze(1).to_broadcast([128, 8, 64]), ALU.mult), inc=pb.dve)
        t1v = T["t1"][:, :].rearrange("p (h d) -> p h d", d=64)
        t2v = T["t2"][:, :].rearrange("p (h d) -> p h d", d=64)
        pb.op("vector", lambda v: v.tensor_sub(b3[:, :, 0:32], t1v[:, :, 0:32],
                                               t2v[:, :, 32:64]),
              wait=[pb.dve.now()], inc=pb.dve)
        pb.op("vector", lambda v: v.tensor_add(b3[:, :, 32:64], t1v[:, :, 32:64],
                                               t2v[:, :, 0:32]), inc=pb.dve)

    for jx in range(NTX):
        norm_rot(kn, jx, T["tbl"][:, 2, jx, :], T["tbl"][:, 3, jx, :],
                 T["ks_sb"][:, :], qkv_done + prm)
    for jq in range(NT):
        norm_rot(qn, jq, T["tbl"][:, 0, jq + 4, :], T["tbl"][:, 1, jq + 4, :],
                 T["qs_sb"][:, :], [])
    rot_done = [pb.dve.now()]

    for idx, (buf, n_t, dstv) in enumerate(((kn, NTX, V["kT"]), (qn, NT, V["qT"]))):
        for j in range(n_t):
            for fb in range(4):
                pb.op("tensor", lambda t, buf=buf, j=j, fb=fb: t.transpose(
                    T["ps_tr"][:, :], buf[:, j, fb * 128:(fb + 1) * 128], T["id_sb"][:, :]),
                    wait=(rot_done + [pb.act.now()]), inc=pb.pe)
                pb.op("scalar", lambda s, dstv=dstv, j=j, fb=fb: s.copy(
                    dstv[:, fb, j * 128:(j + 1) * 128], T["ps_tr"][:, :]),
                    wait=[pb.pe.now()], inc=pb.act)
    tq_done = [pb.act.now()]

    for qt in range(NT):
        for h in range(8):
            pbase = (h % 2) * 64
            fb = h // 2
            w0 = tq_done + [pb.act.now(), pb.dve.now()]
            for s5 in range(5):
                pb.op("tensor", lambda t, qt=qt, s5=s5, pbase=pbase, fb=fb: t.matmul(
                    T["ps_a"][:, s5 * 128:(s5 + 1) * 128],
                    V["kT"][pbase:pbase + 64, fb, (qt + s5) * 128:(qt + s5 + 1) * 128],
                    V["qT"][pbase:pbase + 64, fb, qt * 128:(qt + 1) * 128],
                    start=True, stop=True),
                    wait=(w0 if s5 == 0 else ()), inc=pb.pe)
            mw = [pb.pe.now()]
            for s5 in range(5):
                msk = (T["mskb"][:, qt, s5, :] if qt < 4 else T["msk5"][:, s5, :])
                pb.op("vector", lambda v, s5=s5, msk=msk: v.tensor_tensor(
                    T["ps_a"][:, s5 * 128:(s5 + 1) * 128],
                    T["ps_a"][:, s5 * 128:(s5 + 1) * 128], msk, ALU.add),
                    wait=(mw if s5 == 0 else ()), inc=pb.dve)
            pb.op("scalar", lambda s: s.activation(T["P_sb"][:, :], T["ps_a"][:, 0:640],
                                                   AF.Exp), wait=[pb.dve.now()], inc=pb.act)
            pw = [pb.act.now()]
            for s5 in range(5):
                pb.op("tensor", lambda t, qt=qt, s5=s5, h=h: t.matmul(
                    T["ps_s"][:, 0:64], T["P_sb"][:, s5 * 128:(s5 + 1) * 128],
                    vn[:, qt + s5, h * 64:(h + 1) * 64],
                    start=(s5 == 0), stop=(s5 == 4)),
                    wait=(pw if s5 == 0 else ()), inc=pb.pe)
                pb.op("tensor", lambda t, s5=s5: t.matmul(
                    T["ps_c"][:, 0:1], T["P_sb"][:, s5 * 128:(s5 + 1) * 128],
                    T["ones_sb"][:, 0:1], start=(s5 == 0), stop=(s5 == 4)),
                    wait=([pb.dve.now()] if s5 == 0 else ()), inc=pb.pe)
            pb.op("vector", lambda v: v.reciprocal(T["nrm"][:, 8:9], T["ps_c"][:, 0:1]),
                  wait=[pb.pe.now()], inc=pb.dve)
            pb.op("vector", lambda v, qt=qt, h=h: v.tensor_scalar_mul(
                V["o_n"][:, qt, h * 64:(h + 1) * 64], T["ps_s"][:, 0:64],
                T["nrm"][:, 8:9]), wait=[pb.dve.now()], inc=pb.dve)
    attn_done = [pb.dve.now()]

    for j in range(NT):
        for fb in range(4):
            pb.op("tensor", lambda t, j=j, fb=fb: t.transpose(
                T["ps_tr"][:, :], V["o_n"][:, j, fb * 128:(fb + 1) * 128], T["id_sb"][:, :]),
                wait=(attn_done + [pb.act.now()]), inc=pb.pe)
            pb.op("scalar", lambda s, j=j, fb=fb: s.copy(
                V["oT"][:, fb, j * 128:(j + 1) * 128], T["ps_tr"][:, :]),
                wait=[pb.pe.now()], inc=pb.act)
    oT_done = [pb.act.now()]

    pb.dmaop("gpsimd", lambda g, l=l: g.dma_start(
        out=V["wo_sb"][:, :, :],
        in_=T[f"wo{l}"].rearrange("k p m f -> p k (m f)", p=128)), wait=oT_done)
    wo_w = [pb.dmag.now()]
    for m in range(NE):
        for n in range(4):
            w0 = oT_done + wo_w + [pb.dve.now()]
            for k in range(4):
                pb.op("tensor", lambda t, m=m, n=n, k=k: t.matmul(
                    T["ps_c"][:, 0:512], V["wo_sb"][:, k, m * 128:(m + 1) * 128],
                    V["oT"][:, k, n * 512:(n + 1) * 512],
                    start=(k == 0), stop=(k == 3)),
                    wait=(w0 if k == 0 else ()), inc=pb.pe)
            pb.op("vector", lambda v, m=m, n=n: v.tensor_tensor(
                T["uT"][:, m, HW + n * 512:HW + (n + 1) * 512],
                T["uT"][:, m, HW + n * 512:HW + (n + 1) * 512],
                T["ps_c"][:, 0:512], ALU.add), wait=[pb.pe.now()], inc=pb.dve)
    wo_done = [pb.dve.now()]

    emit_mid_halo(pb, T, V)
    mh_done = [pb.act.now(), pb.dve.now()]

    def h2T(i, c0, width):
        off = HW + c0 - (0 if i < 4 else 1)
        return T["uT"][:, i, off:off + width]

    for n in range(4):
        w0 = mh_done + [pb.act.now(), pb.dve.now()]
        for i in range(NE):
            pb.op("tensor", lambda t, n=n, i=i: t.matmul(
                T["ps_b"][0:1, 0:512], T["ones_sb"][:, 0:1], h2T(i, n * 512, 512),
                start=(i == 0), stop=(i == NE - 1)),
                wait=(w0 if i == 0 else ()), inc=pb.pe)
        pb.op("scalar", lambda s, n=n: s.activation(
            T["h2sq"][:, 0:4, :],
            T["uT"][:, 0:4, HW + n * 512:HW + (n + 1) * 512], AF.Square),
            wait=mh_done + [pb.pe.now()], inc=pb.act)
        pb.op("scalar", lambda s, n=n: s.activation(
            T["h2sq"][:, 4:8, :],
            T["uT"][:, 4:8, HW + n * 512 - 1:HW + (n + 1) * 512 - 1], AF.Square),
            inc=pb.act)
        sq_w = [pb.act.now()]
        for i in range(NE):
            pb.op("tensor", lambda t, n=n, i=i: t.matmul(
                T["ps_b"][0:1, 512:1024], T["ones_sb"][:, 0:1], T["h2sq"][:, i, :],
                start=(i == 0), stop=(i == NE - 1)),
                wait=(sq_w if i == 0 else ()), inc=pb.pe)
        pb.op("vector", lambda v: v.tensor_scalar_mul(
            T["mv"][0:1, 0, :], T["ps_b"][0:1, 0:512], 1.0 / D),
            wait=[pb.pe.now()], inc=pb.dve)
        pb.op("vector", lambda v: v.tensor_scalar_mul(
            T["mv"][0:1, 1, :], T["ps_b"][0:1, 512:1024], 1.0 / D), inc=pb.dve)
        pb.op("vector", lambda v: v.tensor_mul(T["mv"][0:1, 2, :], T["mv"][0:1, 0, :],
                                               T["mv"][0:1, 0, :]),
              wait=[pb.dve.now()], inc=pb.dve)
        pb.op("vector", lambda v: v.tensor_sub(T["mv"][0:1, 1, :], T["mv"][0:1, 1, :],
                                               T["mv"][0:1, 2, :]),
              wait=[pb.dve.now()], inc=pb.dve)
        pb.op("scalar", lambda s: s.activation(T["mv"][0:1, 1, :], T["mv"][0:1, 1, :],
                                               AF.Ln, bias=cp[0:1, CP_EPS:CP_EPS + 1]),
              wait=[pb.dve.now()], inc=pb.act)
        pb.op("scalar", lambda s: s.activation(T["mv"][0:1, 1, :], T["mv"][0:1, 1, :],
                                               AF.Exp, scale=cp[0:1, CP_NEGH:CP_NEGH + 1]),
              wait=[pb.act.now()], inc=pb.act)
        pb.op("scalar", lambda s: s.copy(T["mvb"][0:1, 0, :], T["mv"][0:1, 0, :]),
              wait=[pb.act.now()], inc=pb.act)
        pb.op("scalar", lambda s: s.copy(T["mvb"][0:1, 1, :], T["mv"][0:1, 1, :]),
              inc=pb.act)
        mv_done = [pb.act.now()]
        for which, dst in ((0, "mub"), (1, "rsb")):
            pb.op("tensor", lambda t, which=which: t.matmul(
                T["ps_b"][:, 0:512], T["ones_sb"][0:1, 0:128],
                T["mvb"][0:1, which, :], start=True, stop=True),
                wait=mv_done + [pb.act.now()], inc=pb.pe)
            pb.op("scalar", lambda s, dst=dst: s.copy(T[dst][:, :], T["ps_b"][:, 0:512]),
                  wait=[pb.pe.now()], inc=pb.act)
        bc_done = [pb.act.now()]
        for i in range(NE):
            pb.op("vector", lambda v, i=i, n=n: v.tensor_tensor(
                V["hln"][:, i, n * 512:(n + 1) * 512], h2T(i, n * 512, 512),
                T["mub"][:, :], ALU.subtract),
                wait=bc_done + ([pb.pe.now()] if i == 0 else []), inc=pb.dve)
            pb.op("vector", lambda v, i=i, n=n: v.tensor_tensor(
                V["hln"][:, i, n * 512:(n + 1) * 512],
                V["hln"][:, i, n * 512:(n + 1) * 512], T["rsb"][:, :], ALU.mult),
                wait=[pb.dve.now()], inc=pb.dve)
            pb.op("vector", lambda v, i=i, n=n: v.tensor_scalar(
                V["hln"][:, i, n * 512:(n + 1) * 512],
                V["hln"][:, i, n * 512:(n + 1) * 512], T["lg_sb"][:, i:i + 1],
                T["lb_sb"][:, i:i + 1], ALU.mult, ALU.add),
                wait=[pb.dve.now()], inc=pb.dve)
    hln_done = [pb.dve.now()]

    w14 = T["wz"].ap()[:, 0:12288].rearrange(
        "p (d ag mi f) -> p d ag mi f", d=2, ag=2, mi=24)
    w24 = T["wz"].ap()[:, 0:11264].rearrange(
        "p (d m k f) -> p d m k f", d=2, m=2, k=HIDT)
    mgroups = [(g * 3, min(HIDT, g * 3 + 3)) for g in range((HIDT + 2) // 3)]
    w1_lastuse = [0, 0]
    w2_lastuse = [0, 0]
    for n in range(4):
        for gi, (m0, m1) in enumerate(mgroups):
            db = gi % 2
            pb.dmaop("gpsimd", lambda g, l=l, m0=m0, m1=m1, db=db: g.dma_start(
                out=w14[:, db, 0, 0:(m1 - m0) * 8, :],
                in_=T[f"w1a{l}"][m0:m1].rearrange("m i p f -> p (m i) f")),
                wait=[(pb.pe, w1_lastuse[db])] + (hln_done if (n == 0 and gi < 2)
                                                 else []))
            pb.dmaop("gpsimd", lambda g, l=l, m0=m0, m1=m1, db=db: g.dma_start(
                out=w14[:, db, 1, 0:(m1 - m0) * 8, :],
                in_=T[f"w1g{l}"][m0:m1].rearrange("m i p f -> p (m i) f")))
            w1w = [pb.dmag.now()]
            for mi, m in enumerate(range(m0, m1)):
                for ag in range(2):
                    ps = T["ps_a"][:, ag * 512:(ag + 1) * 512]
                    for i in range(NE):
                        pb.op("tensor", lambda t, n=n, i=i, ag=ag, db=db, mi=mi,
                              ps=ps: t.matmul(
                            ps, w14[:, db, ag, mi * 8 + i, :],
                            V["hln"][:, i, n * 512:(n + 1) * 512],
                            start=(i == 0), stop=(i == NE - 1)),
                            wait=(hln_done + w1w + [pb.dve.now(), pb.act.now()]
                                  if (ag == 0 and i == 0) else ()), inc=pb.pe)
                pb.op("scalar", lambda s, m=m: s.activation(
                    T["gsc"][:, :], T["ps_a"][:, 512:1024], AF.Gelu,
                    bias=T["b1g_sb"][:, m:m + 1]), wait=[pb.pe.now(), pb.dve.now()],
                      inc=pb.act)
                pb.op("vector", lambda v, m=m: v.scalar_tensor_tensor(
                    V["actT"][:, m, :], T["ps_a"][:, 0:512], T["b1a_sb"][:, m:m + 1],
                    T["gsc"][:, :], ALU.add, ALU.mult), wait=[pb.act.now()],
                    inc=pb.dve)
            w1_lastuse[db] = pb.pe.n
        act_done = [pb.dve.now()]
        for gi in range(4):
            mo0 = gi * 2
            db = gi % 2
            pb.dmaop("gpsimd", lambda g, l=l, mo0=mo0, db=db: g.dma_start(
                out=w24[:, db, :, :, :].rearrange("p m k f -> p (m k) f"),
                in_=T[f"w2{l}"][mo0:mo0 + 2].rearrange("m k p f -> p (m k) f")),
                wait=[(pb.pe, w2_lastuse[db])])
            w2w = [pb.dmag.now()]
            for moi in range(2):
                mo = mo0 + moi
                for k in range(HIDT):
                    pb.op("tensor", lambda t, k=k, db=db, moi=moi: t.matmul(
                        T["ps_b"][:, 0:512], w24[:, db, moi, k, :], V["actT"][:, k, :],
                        start=(k == 0), stop=(k == HIDT - 1)),
                        wait=(act_done + w2w + [pb.dve.now()] if k == 0 else ()),
                        inc=pb.pe)
                pb.op("vector", lambda v, mo=mo, n=n: v.scalar_tensor_tensor(
                    T["uT"][:, mo, HW + n * 512:HW + (n + 1) * 512], T["ps_b"][:, 0:512],
                    T["b2_sb"][:, mo:mo + 1],
                    T["uT"][:, mo, HW + n * 512:HW + (n + 1) * 512], ALU.add, ALU.add),
                    wait=[pb.pe.now()], inc=pb.dve)
            w2_lastuse[db] = pb.pe.n


def emit_phase2(pb, T):
    cpv = pb.cpool
    pb.dmaop("sync", lambda s: s.dma_start(out=T["cpool"][:, :], in_=T["cpool_c"]))
    pb.dmaop("sync", lambda s: s.dma_start(out=T["id_sb"][:, :], in_=T["ident"]))
    pb.dmaop("sync", lambda s: s.dma_start(out=T["ones_sb"][:, :], in_=T["ones_c"][:, 0:128]))
    pb.dmaop("sync", lambda s: s.dma_start(out=T["selprev"][:, :], in_=T["selprev_i"]))
    pb.dmaop("sync", lambda s: s.dma_start(out=T["selprev1"][0:1, :], in_=T["selprev1_i"]))
    for tn, ti in (("tcs", 0), ("tss", 1), ("tcd", 2), ("tsd", 3)):
        pb.dmaop("gpsimd", lambda g, tn=tn, ti=ti: g.dma_start(
            out=T["tbl"][:, ti, :, :],
            in_=T[tn + "_i"].rearrange("(j p) f -> p j f", p=128)))
    pb.dmaop("gpsimd", lambda g: g.dma_start(
        out=T["msk5"][:, :, :], in_=T["mask5_i"].rearrange("s p f -> p s f")))
    pb.dmaop("gpsimd", lambda g: g.dma_start(
        out=T["mskb"][:, :, :, :], in_=T["maskb_i"].rearrange("q s p f -> p q s f")))
    ld0 = [pb.dma.now(), pb.dmag.now()]

    for j in range(NT):
        pb.dmaop("sync", lambda s, j=j: s.dma_start(
            out=T["xt"][:, :], in_=T["x_in"][j * 128:(j + 1) * 128, :]),
            wait=[pb.dve.now()])
        emit_ln_tile(pb, T["xt"][:, :], T["un"][:, :], T["stats"][:, 0, :],
                     ld0 + [pb.dma.now(), pb.act.now()])
        for i in range(NE):
            pb.op("tensor", lambda t, i=i: t.transpose(
                T["ps_tr"][:, :], T["un"][:, i * 128:(i + 1) * 128], T["id_sb"][:, :]),
                wait=[pb.dve.now(), pb.act.now()], inc=pb.pe)
            pb.op("scalar", lambda s, j=j, i=i: s.copy(
                T["uT"][:, i, HW + j * 128:HW + (j + 1) * 128], T["ps_tr"][:, :]),
                wait=[pb.pe.now()], inc=pb.act)

    emit_inner_layer(pb, T, 0)
    emit_inner_layer(pb, T, 1)

    for j in range(NT):
        for i in range(NE):
            pb.op("tensor", lambda t, j=j, i=i: t.transpose(
                T["ps_tr"][:, :], T["uT"][:, i, HW + j * 128:HW + (j + 1) * 128],
                T["id_sb"][:, :]),
                wait=[pb.dve.now(), pb.act.now()], inc=pb.pe)
            pb.op("scalar", lambda s, i=i: s.copy(
                T["un"][:, i * 128:(i + 1) * 128], T["ps_tr"][:, :]),
                wait=[pb.pe.now()], inc=pb.act)
        pb.dmaop("sync", lambda s, j=j: s.dma_start(
            out=T["xt"][:, :], in_=T["x_in"][j * 128:(j + 1) * 128, :]),
            wait=[pb.dve.now()])
        emit_ln_tile(pb, T["un"][:, :], T["un"][:, :], T["stats"][:, 0, :],
                     [pb.act.now()])
        pb.op("vector", lambda v: v.tensor_add(T["xt"][:, :], T["xt"][:, :],
                                               T["un"][:, :]),
              wait=[pb.dve.now(), pb.dma.now()], inc=pb.dve)
        pb.dmaop("sync", lambda s, j=j: s.dma_start(
            out=T["x_out"][j * 128:(j + 1) * 128, :], in_=T["xt"][:, :]),
            wait=[pb.dve.now()])
    pb.op("sync", None, wait=[pb.dma.now()])


def sbuf_p2(nc, T, ar, ps=None):
    sb = lambda n, shp, dt: T.__setitem__(n, ar.take(shp, dt))
    sb("uT", [128, NE, HW + TCORE], BF16)
    sb("av", [128, HIDT * 512], BF16)
    sb("qk", [128, 36 * 512], BF16)
    sb("tq", [128, 4 * 2560 + 4 * 2048], BF16)
    sb("wz", [128, 12288], BF16)
    sb("tbl", [128, 4, NTX, 64], BF16)
    sb("msk5", [128, 5, 128], BF16)
    sb("mskb", [128, 4, 5, 128], BF16)
    sb("P_sb", [128, 640], BF16)
    sb("sq", [128, 512], BF16)
    sb("t1", [128, 512], BF16)
    sb("t2", [128, 512], BF16)
    sb("nrm", [128, 16], F32)
    sb("h2sq", [128, NE, 512], BF16)
    sb("mv", [1, 3, 512], F32)
    sb("mvb", [1, 2, 512], BF16)
    sb("mub", [128, 512], BF16)
    sb("rsb", [128, 512], BF16)
    sb("gsc", [128, 512], BF16)
    sb("selprev", [128, 4], F32)
    sb("selprev1", [1, 4], F32)
    sb("qs_sb", [128, 64], F32); sb("ks_sb", [128, 64], F32)
    sb("lg_sb", [128, 8], F32); sb("lb_sb", [128, 8], F32)
    sb("b1a_sb", [128, HIDT], F32); sb("b1g_sb", [128, HIDT], F32)
    sb("b2_sb", [128, 8], F32)
    sb("stats", [128, 1, 16], F32)
    sb("un", [128, D], BF16)
    sb("xt", [128, D], F32)
    sb("id_sb", [128, 128], BF16)
    sb("ones_sb", [128, 128], BF16)
    sb("cpool", [128, 8], F32)
    return psum_std(nc, T, split_ab=True, ps=ps)


def decls_p2(nc, T, xio=True):
    def inp(name, shape, dtype=F32):
        T[name] = nc.dram_tensor(name, list(shape), dtype, kind="ExternalInput").ap()

    if xio:
        inp("x_in", [TCORE, D])
        T["x_out"] = nc.dram_tensor("x_out", [TCORE, D], F32,
                                    kind="ExternalOutput").ap()
    inp("selprev_i", [128, 4]); inp("selprev1_i", [1, 4])
    inp("mask5_i", [5, 128, 128]); inp("maskb_i", [4, 5, 128, 128])
    for tn in ("tcs", "tss", "tcd", "tsd"):
        inp(tn + "_i", [NTX * 128, 64])
    for l in range(2):
        inp(f"wqkv{l}", [D, 1536]); inp(f"wo{l}", [4, 128, 8, 128])
        inp(f"w1a{l}", [HIDT, 8, 128, 128]); inp(f"w1g{l}", [HIDT, 8, 128, 128])
        inp(f"w2{l}", [8, HIDT, 128, 128])
        inp(f"qs{l}", [128, 64]); inp(f"ks{l}", [128, 64])
        inp(f"lg{l}", [128, 8]); inp(f"lb{l}", [128, 8])
        inp(f"b1a{l}", [128, HIDT]); inp(f"b1g{l}", [128, HIDT])
        inp(f"b2{l}", [128, 8])
    T["ccH_in"] = nc.dram_tensor("ccH_in", [D, HW], BF16, kind="Internal")
    T["ccH_out"] = nc.dram_tensor("ccH_out", [4 * D, HW], BF16, kind="Internal")
    T["ccM_in"] = nc.dram_tensor("ccM_in", [1, D], BF16, kind="Internal")
    T["ccM_out"] = nc.dram_tensor("ccM_out", [4, D], BF16, kind="Internal")
    decl_shared(nc, T)


def build_p2(limit=None):
    nc = bass.Bass("TRN2", target_bir_lowering=False)
    T = {}
    decls_p2(nc, T)
    sbuf_p2(nc, T, Arena(nc, "G", ARENA_N))
    T["hbuf_v"] = T["wz"].ap()[:, 0:8 * HW].rearrange("p (a b) -> p a b", b=HW)

    pb = PB(nc)
    pb.limit = limit
    pb.cpool = T["cpool"][:, :]
    emit_phase2(pb, T)
    with nc.Block() as blk:
        pb.replay(blk)
    return nc


def host_inputs_p2(full, x_stream=None):
    DIM_HEAD, WINDOW = 64, 512
    half = np.arange(0, DIM_HEAD, 2, dtype=np.float64)
    inv = 1.0 / 10000.0 ** (half / DIM_HEAD)
    basev = (half + 0.4 * DIM_HEAD) / (1.4 * DIM_HEAD)

    wqkv = np.asarray(full["L0_ffn_wqkv"], np.float32)
    wo = np.asarray(full["L0_ffn_wo"], np.float32)
    w1 = np.asarray(full["L0_ffn_w1"], np.float32)
    w2 = np.asarray(full["L0_ffn_w2"], np.float32)
    qsg = np.asarray(full["L0_ffn_q_scale"], np.float32)
    ksg = np.asarray(full["L0_ffn_k_scale"], np.float32)
    lng = np.asarray(full["L0_ffn_ln_g"], np.float32)
    lnb = np.asarray(full["L0_ffn_ln_b"], np.float32)
    b1 = np.asarray(full["L0_ffn_b1"], np.float32)
    b2 = np.asarray(full["L0_ffn_b2"], np.float32)

    per_layer = {}
    for l in range(2):
        wov = wo[l].reshape(4, 128, 8, 128)
        w1a = np.zeros((D, HIDT * 128), np.float32); w1a[:, 0:2730] = w1[l][:, 0:2730]
        w1g = np.zeros((D, HIDT * 128), np.float32); w1g[:, 0:2730] = w1[l][:, 2730:5460]
        w1ar = w1a.reshape(D, HIDT, 128).transpose(1, 0, 2)
        w1gr = w1g.reshape(D, HIDT, 128).transpose(1, 0, 2)
        w2p = np.zeros((HIDT * 128, D), np.float32); w2p[0:2730] = w2[l]
        w2r = w2p.reshape(HIDT, 128, 8, 128).transpose(2, 0, 1, 3)
        b1a = np.zeros(HIDT * 128, np.float32); b1a[0:2730] = b1[l][0:2730]
        b1g = np.zeros(HIDT * 128, np.float32); b1g[0:2730] = b1[l][2730:5460]
        per_layer[l] = dict(
            wqkv=np.ascontiguousarray(wqkv[l]),
            wo=np.ascontiguousarray(wov),
            w1a=np.ascontiguousarray(w1ar.reshape(HIDT, 8, 128, 128)),
            w1g=np.ascontiguousarray(w1gr.reshape(HIDT, 8, 128, 128)),
            w2=np.ascontiguousarray(w2r),
            qs=bcast_np(qsg[l]), ks=bcast_np(ksg[l]),
            lg=np.ascontiguousarray(lng[l].reshape(8, 128).T),
            lb=np.ascontiguousarray(lnb[l].reshape(8, 128).T),
            b1a=np.ascontiguousarray(b1a.reshape(HIDT, 128).T),
            b1g=np.ascontiguousarray(b1g.reshape(HIDT, 128).T),
            b2=np.ascontiguousarray(b2[l].reshape(8, 128).T),
        )

    m = np.arange(128)
    base = np.zeros((5, 128, 128), np.float32)
    for s in range(5):
        mt = s * 128 + m
        allowed = (mt[:, None] >= np.arange(128)[None, :]) & \
                  (mt[:, None] <= np.arange(128)[None, :] + 512)
        base[s] = np.where(allowed, -30.0, -1e30)

    maps = []
    for c in range(8):
        b, q = c // 4, c % 4
        t0 = q * TCORE
        xs = (np.ascontiguousarray(x_stream[b, t0:t0 + TCORE])
              if x_stream is not None else None)
        selprev = np.zeros((128, 4), np.float32)
        selprev1 = np.zeros((1, 4), np.float32)
        if q > 0:
            selprev[:, q - 1] = 1.0
            selprev1[0, q - 1] = 1.0
        maskb = np.broadcast_to(base, (4, 5, 128, 128)).copy()
        if q == 0:
            for qt in range(4):
                for s in range(5):
                    tkey = (qt + s) * 128 + m - 512
                    maskb[qt, s][tkey < 0, :] = -1e30
        t = (t0 - 512 + np.arange(NTX * 128)).astype(np.float64)
        fr = t[:, None] * inv[None, :]
        power = (t - 8192 // 2) / float(WINDOW // 2)
        sc = basev[None, :] ** power[:, None]
        tcs = np.concatenate([np.cos(fr) * sc] * 2, axis=1).astype(np.float32)
        tss = np.concatenate([np.sin(fr) * sc] * 2, axis=1).astype(np.float32)
        tcd = np.concatenate([np.cos(fr) / sc] * 2, axis=1).astype(np.float32)
        tsd = np.concatenate([np.sin(fr) / sc] * 2, axis=1).astype(np.float32)
        mp = dict(selprev_i=selprev, selprev1_i=selprev1,
                  mask5_i=base, maskb_i=maskb,
                  tcs_i=tcs, tss_i=tss, tcd_i=tcd, tsd_i=tsd)
        if x_stream is not None:
            mp["x_in"] = xs
        for l in range(2):
            for k, v in per_layer[l].items():
                mp[f"{k}{l}"] = v
        maps.append(mp)
    return maps


NKB = 16
NQB = 16


def emit_phase3(pb, T):
    cp = pb.cpool
    uT = T["a1"].ap().rearrange("p (i t) -> p i t", t=TCORE)
    wband = T["a1"].ap().rearrange("p (i t) -> p i t", t=TCORE)
    krp = T["a1"].ap()[:, 0:8192]
    krm = T["a1"].ap()[:, 8192:16384]
    qT_loc = T["a2"].ap().rearrange("p (i t) -> p i t", t=TCORE)
    qTm = T["a2"].ap().rearrange("p (i g t) -> p i g t", i=2, g=4)
    kvT = T["a3"].ap().rearrange("p (i t) -> p i t", t=TCORE)
    oT = T["a3"].ap().rearrange("p (i t) -> p i t", t=8192)
    latT = T["latT"].ap().rearrange("p (i t) -> p i t", t=8192)
    krT = T["krT"].ap()
    vrec = T["vrec"].ap().rearrange("p (k w) -> p k w", w=65)

    pb.dmaop("sync", lambda s: s.dma_start(out=T["cpool"][:, :], in_=T["cpool_c"]))
    pb.dmaop("sync", lambda s: s.dma_start(out=T["id_sb"][:, :], in_=T["ident"]))
    pb.dmaop("sync", lambda s: s.dma_start(out=T["ones_sb"][:, :], in_=T["ones_c"]))
    for nm in ("rowqp", "rowqm", "rowkp", "rowkm"):
        pb.dmaop("gpsimd", lambda g, nm=nm: g.dma_start(out=T[nm + "_sb"][:, :],
                                                        in_=T[nm + "_i"]))
    for nm in ("creltab", "dconst", "selg"):
        pb.dmaop("sync", lambda s, nm=nm: s.dma_start(out=T[nm + "_sb"][:, :],
                                                      in_=T[nm + "_i"]))
    ld0 = [pb.dma.now()]

    for j in range(NT):
        pb.dmaop("sync", lambda s, j=j: s.dma_start(
            out=T["xt"][:, :], in_=T["x_in"][j * 128:(j + 1) * 128, :]),
            wait=[pb.dve.now()])
        emit_ln_tile(pb, T["xt"][:, :], T["un"][:, :], T["stats"][:, :],
                     ld0 + [pb.dma.now(), pb.act.now()])
        for i in range(NE):
            pb.op("tensor", lambda t, i=i: t.transpose(
                T["ps_tr"][:, :], T["un"][:, i * 128:(i + 1) * 128], T["id_sb"][:, :]),
                wait=[pb.dve.now(), pb.act.now()], inc=pb.pe)
            pb.op("scalar", lambda s, j=j, i=i: s.copy(
                uT[:, i, j * 128:(j + 1) * 128], T["ps_tr"][:, :]),
                wait=[pb.pe.now()], inc=pb.act)
    xlnT_done = [pb.act.now()]

    def proj(wname, out_view, acc_with=None, done_wait=()):
        pb.dmaop("gpsimd", lambda g: g.dma_start(
            out=T["wbuf"][:, :, :],
            in_=T[wname].rearrange("(i p) f -> p i f", p=128)),
            wait=list(done_wait))
        ww = [pb.dmag.now()]
        for m in range(NE):
            for n in range(4):
                first = not acc_with
                ps = T["ps_ab"][:, ((m + n) % 2) * 512:((m + n) % 2) * 512 + 512]
                for i in range(NE):
                    pb.op("tensor", lambda t, m=m, n=n, i=i, ps=ps: t.matmul(
                        ps, T["wbuf"][:, i, m * 128:(m + 1) * 128],
                        uT[:, i, n * 512:(n + 1) * 512],
                        start=(i == 0), stop=(i == NE - 1)),
                        wait=(xlnT_done + ww + [pb.act.now()] if i == 0 else ()),
                        inc=pb.pe)
                pb.op("scalar", lambda s, m=m, n=n, ps=ps: s.copy(
                    out_view[:, m, n * 512:(n + 1) * 512], ps),
                    wait=[pb.pe.now()], inc=pb.act)
    proj("Wq_i", qT_loc)
    pb.dmaop("gpsimd", lambda g: g.dma_start(
        out=T["ccQ_in"].ap().rearrange("(i p) t -> p i t", p=128), in_=qT_loc[:, :, :]),
        wait=[pb.act.now()])
    pb.op("gpsimd", lambda g: g.collective_compute(
        "AllGather", ALU.bypass, replica_groups=GROUPS,
        ins=[T["ccQ_in"].ap().opt()], outs=[T["ccQ_out"].ap().opt()]),
        wait=[pb.dmag.now()], inc=pb.cc, inck=None)

    proj("Wk_i", kvT, done_wait=[pb.pe.now()])
    pb.dmaop("gpsimd", lambda g: g.dma_start(
        out=T["wbuf"][:, :, :], in_=T["Wv_i"].rearrange("(i p) f -> p i f", p=128)),
        wait=[pb.pe.now()])
    wv = [pb.dmag.now()]
    for m in range(NE):
        for n in range(4):
            ps = T["ps_ab"][:, 1024 + ((m + n) % 2) * 512:1536 + ((m + n) % 2) * 512]
            for i in range(NE):
                pb.op("tensor", lambda t, m=m, n=n, i=i, ps=ps: t.matmul(
                    ps, T["wbuf"][:, i, m * 128:(m + 1) * 128],
                    uT[:, i, n * 512:(n + 1) * 512],
                    start=(i == 0), stop=(i == NE - 1)),
                    wait=(wv + [pb.dve.now(), pb.act.now()] if i == 0 else ()),
                    inc=pb.pe)
            pb.op("vector", lambda v, m=m, n=n, ps=ps: v.tensor_add(
                kvT[:, m, n * 512:(n + 1) * 512],
                kvT[:, m, n * 512:(n + 1) * 512], ps), wait=[pb.pe.now()], inc=pb.dve)
    kv_done = [pb.dve.now()]

    wdv = T["wbuf"].ap().rearrange("p i f -> p (i f)").rearrange(
        "p (i f) -> p i f", f=256)
    pb.dmaop("gpsimd", lambda g: g.dma_start(
        out=wdv[:, 0:8, :],
        in_=T["Wd_i"].rearrange("(i p) f -> p i f", p=128)), wait=kv_done)
    wd = [pb.dmag.now()]
    for m in range(2):
        for n in range(4):
            for i in range(NE):
                pb.op("tensor", lambda t, m=m, n=n, i=i: t.matmul(
                    T["ps_ab"][:, 0:512], wdv[:, i, m * 128:(m + 1) * 128],
                    kvT[:, i, n * 512:(n + 1) * 512],
                    start=(i == 0), stop=(i == NE - 1)),
                    wait=(kv_done + wd + [pb.act.now()] if i == 0 else ()), inc=pb.pe)
            pb.op("scalar", lambda s, m=m, n=n: s.copy(
                T["dxs"][:, :], T["ps_ab"][:, 0:512]),
                wait=[pb.pe.now(), pb.dma.now()], inc=pb.act)
            pb.dmaop("sync", lambda s, m=m, n=n: s.dma_start(
                out=T["ccL_in"].ap()[m * 128:(m + 1) * 128, n * 512:(n + 1) * 512],
                in_=T["dxs"][:, :]), wait=[pb.act.now()])
    pb.op("gpsimd", lambda g: g.collective_compute(
        "AllGather", ALU.bypass, replica_groups=GROUPS,
        ins=[T["ccL_in"].ap().opt()], outs=[T["ccL_out"].ap().opt()]),
        wait=[pb.dma.now()], inc=pb.cc, inck=None)
    cc2 = [pb.cc.now()]

    for p4 in range(4):
        pb.dmaop("sync", lambda s, p4=p4: s.dma_start(
            out=latT[:, :, p4 * TCORE:(p4 + 1) * TCORE],
            in_=T["ccL_out"].ap()[p4 * 256:(p4 + 1) * 256, :]
            .rearrange("(i p) t -> p i t", p=128)), wait=cc2)
    for p4 in range(4):
        pb.dmaop("sync", lambda s, p4=p4: s.dma_start(
            out=wband[:, :, :],
            in_=T["ccQ_out"].ap()[p4 * D:(p4 + 1) * D, :]
            .rearrange("(i p) t -> p i t", p=128)),
            wait=cc2 + [pb.dve.now()])
        for g4 in range(4):
            if g4 == 0:
                pb.op("vector", lambda v, p4=p4: v.tensor_scalar_mul(
                    qTm[:, :, p4, :], wband[:, 0:2, :], T["selg_sb"][:, 0:1]),
                    wait=[pb.dma.now()], inc=pb.dve)
            else:
                pb.op("vector", lambda v, p4=p4, g4=g4: v.scalar_tensor_tensor(
                    qTm[:, :, p4, :], wband[:, 2 * g4:2 * g4 + 2, :],
                    T["selg_sb"][:, g4:g4 + 1], qTm[:, :, p4, :], ALU.mult, ALU.add),
                    inc=pb.dve)
    asm_done = [pb.dve.now(), pb.dma.now()]

    pb.dmaop("gpsimd", lambda g: g.dma_start(
        out=T["wupk_sb"][:, :, :],
        in_=T["wupk_i"].rearrange("(i p) f -> p i f", p=128)))
    pb.dmaop("gpsimd", lambda g: g.dma_start(
        out=T["wupv_sb"][:, :, :],
        in_=T["wupv_i"].rearrange("(i p) f -> p i f", p=128)))
    wup = [pb.dmag.now()]

    for ht in range(2):
        for n in range(NKB):
            for i in range(2):
                pb.op("tensor", lambda t, ht=ht, n=n, i=i: t.matmul(
                    T["ps_ab"][:, 0:512],
                    T["wupk_sb"][:, i, ht * 128:(ht + 1) * 128],
                    latT[:, i, n * 512:(n + 1) * 512],
                    start=(i == 0), stop=(i == 1)),
                    wait=(asm_done + wup + [pb.act.now(), pb.dve.now()]
                          if (n == 0 and i == 0) else
                          ([pb.act.now()] if i == 0 else ())), inc=pb.pe)
            pb.op("scalar", lambda s, n=n: s.copy(krT[:, n * 512:(n + 1) * 512],
                                                  T["ps_ab"][:, 0:512]),
                  wait=[pb.pe.now()], inc=pb.act)
        krw = [pb.act.now()]
        pb.op("vector", lambda v: v.tensor_tensor(
            krp.rearrange("p (n t) -> p n t", t=512),
            krT[:, :].rearrange("p (n t) -> p n t", t=512),
            T["rowkp_sb"][:, :].unsqueeze(1).to_broadcast([128, NKB, 512]),
            ALU.mult), wait=krw, inc=pb.dve)
        pb.op("vector", lambda v: v.tensor_tensor(
            krm.rearrange("p (n t) -> p n t", t=512),
            krT[:, :].rearrange("p (n t) -> p n t", t=512),
            T["rowkm_sb"][:, :].unsqueeze(1).to_broadcast([128, NKB, 512]),
            ALU.mult), inc=pb.dve)
        kr_done = [pb.dve.now()]
        for hsub in range(2):
            h = 2 * ht + hsub
            hb = hsub * 64
            pb.op("vector", lambda v: v.memset(vrec[:, :, 64:65], 1.0),
                  wait=[pb.pe.now()], inc=pb.dve)
            for kt in range(64):
                for i in range(2):
                    pb.op("tensor", lambda t, h=h, kt=kt, i=i: t.matmul(
                        T["ps_ab"][:, 1024:1088], latT[:, i, kt * 128:(kt + 1) * 128],
                        T["wupv_sb"][:, i, h * 64:(h + 1) * 64],
                        start=(i == 0), stop=(i == 1)),
                        wait=(kr_done + [pb.act.now(), pb.dve.now()]
                              if kt == 0 else
                              ([pb.act.now()] if i == 0 else ())), inc=pb.pe)
                pb.op("scalar", lambda s, kt=kt: s.copy(vrec[:, kt, 0:64],
                                                        T["ps_ab"][:, 1024:1088]),
                      wait=[pb.pe.now()], inc=pb.act)
            vr_done = [pb.act.now(), pb.dve.now()]
            for kb in range(NKB):
                for kt in range(4):
                    pb.op("tensor", lambda t, kb=kb, kt=kt: t.matmul(
                        T["ps_s"][0:65, 0:1], vrec[:, kb * 4 + kt, :],
                        T["ones_sb"][:, 0:1], start=(kt == 0), stop=(kt == 3)),
                        wait=(vr_done if (kb == 0 and kt == 0) else [pb.dve.now()]),
                        inc=pb.pe)
                pb.op("vector", lambda v, kb=kb: v.tensor_copy(
                    T["vs"][0:65, kb:kb + 1], T["ps_s"][0:65, 0:1]),
                    wait=[pb.pe.now()], inc=pb.dve)
            cur = "vs"
            for st in (1, 2, 4, 8):
                nxt = "vp2" if cur == "vs" else "vs"
                pb.op("vector", lambda v, cur=cur, nxt=nxt, st=st: v.tensor_copy(
                    T[nxt][0:65, 0:st], T[cur][0:65, 0:st]),
                    wait=[pb.dve.now()], inc=pb.dve)
                pb.op("vector", lambda v, cur=cur, nxt=nxt, st=st: v.tensor_add(
                    T[nxt][0:65, st:16], T[cur][0:65, st:16], T[cur][0:65, 0:16 - st]),
                    inc=pb.dve)
                cur = nxt
            vpref = T[cur]
            pref_done = [pb.dve.now()]

            for qb in range(NQB):
                qsl = qTm[hb:hb + 64, ht, qb // 4,
                          (qb % 4) * 512:(qb % 4) * 512 + 512]
                pb.op("vector", lambda v, qsl=qsl, hb=hb: v.tensor_mul(
                    T["qp"][hb:hb + 64, :], qsl, T["rowqp_sb"][hb:hb + 64, :]),
                    wait=pref_done + [pb.act.now()], inc=pb.dve)
                pb.op("vector", lambda v, qsl=qsl, hb=hb: v.tensor_mul(
                    T["qm"][hb:hb + 64, :], qsl, T["rowqm_sb"][hb:hb + 64, :]),
                    inc=pb.dve)
                q_done = [pb.dve.now()]
                kbs = [kb for kb in range(NKB) if abs(kb - qb) <= 4]
                started = False
                for kb in kbs:
                    dB = abs(kb - qb)
                    if kb != qb:
                        kr_use, q_use = (krm, "qp") if kb > qb else (krp, "qm")
                        for kt in range(4):
                            pb.op("tensor", lambda t, kr_use=kr_use, q_use=q_use,
                                  kb=kb, kt=kt, hb=hb: t.matmul(
                                T["ps_ab"][:, kt * 512:(kt + 1) * 512],
                                kr_use[hb:hb + 64,
                                       (kb * 4 + kt) * 128:(kb * 4 + kt + 1) * 128],
                                T[q_use][hb:hb + 64, :], start=True, stop=True),
                                wait=(q_done + [pb.act.now(), pb.dve.now()]
                                      if kt == 0 else ()), inc=pb.pe)
                        if dB <= 2:
                            pb.op("scalar", lambda s, dB=dB: s.activation(
                                T["Pv"][:, :], T["ps_ab"][:, :], AF.Exp,
                                scale=T["creltab_sb"][:, dB:dB + 1]),
                                wait=[pb.pe.now()], inc=pb.act)
                        else:
                            pb.op("vector", lambda v, dB=dB: v.tensor_scalar(
                                T["Pv"][:, :], T["ps_ab"][:, :],
                                T["creltab_sb"][:, dB:dB + 1],
                                1.0, ALU.mult, ALU.add), wait=[pb.pe.now()],
                                inc=pb.dve)
                        pdone = [pb.act.now(), pb.dve.now()]
                        for kt in range(4):
                            pb.op("tensor", lambda t, kb=kb, kt=kt,
                                  started=started: t.matmul(
                                T["ps_c"][0:65, 0:512], vrec[:, kb * 4 + kt, :],
                                T["Pv"][:, kt * 512:(kt + 1) * 512],
                                start=(not started and kt == 0), stop=False,
                                skip_group_check=True),
                                wait=(pdone if kt == 0 else ()), inc=pb.pe)
                        started = True
                    else:
                        for variant, kr_use, q_use in (("Pv", krm, "qp"),
                                                       ("Pv2", krp, "qm")):
                            for kt in range(4):
                                pb.op("tensor", lambda t, kr_use=kr_use, q_use=q_use,
                                      kb=kb, kt=kt, hb=hb: t.matmul(
                                    T["ps_ab"][:, kt * 512:(kt + 1) * 512],
                                    kr_use[hb:hb + 64,
                                           (kb * 4 + kt) * 128:(kb * 4 + kt + 1) * 128],
                                    T[q_use][hb:hb + 64, :], start=True, stop=True),
                                    wait=(q_done + [pb.act.now(), pb.dve.now()]
                                          if kt == 0 else ()), inc=pb.pe)
                            wexp = [pb.pe.now()]
                            for kt in range(4):
                                if variant == "Pv":
                                    lo2, hi2 = kt * 512, kt * 512 + kt * 128
                                else:
                                    lo2, hi2 = kt * 512 + (kt + 1) * 128, (kt + 1) * 512
                                if hi2 > lo2:
                                    pb.op("scalar", lambda s, variant=variant,
                                          lo2=lo2, hi2=hi2: s.activation(
                                        T[variant][:, lo2:hi2],
                                        T["ps_ab"][:, lo2:hi2], AF.Exp,
                                        scale=T["creltab_sb"][:, 0:1]),
                                        wait=wexp, inc=pb.act)
                        for kt in range(4):
                            qsl2 = qTm[hb:hb + 64, ht, qb // 4,
                                       (qb % 4) * 512 + kt * 128:
                                       (qb % 4) * 512 + kt * 128 + 128]
                            pb.op("tensor", lambda t, kb=kb, kt=kt, qsl2=qsl2,
                                  hb=hb: t.matmul(
                                T["ps_c"][:, 512 + kt * 128:512 + (kt + 1) * 128],
                                krT[hb:hb + 64,
                                    (kb * 4 + kt) * 128:(kb * 4 + kt + 1) * 128],
                                qsl2, start=True, stop=True),
                                wait=([pb.act.now()] if kt == 0 else ()), inc=pb.pe)
                        pb.op("vector", lambda v: v.tensor_tensor(
                            T["Pd"][:, :].rearrange("p (k f) -> p k f", f=128),
                            T["ps_c"][:, 512:1024].rearrange("p (k f) -> p k f", f=128),
                            T["dconst_sb"][:, :].unsqueeze(1)
                            .to_broadcast([128, 4, 128]),
                            ALU.mult), wait=[pb.pe.now()], inc=pb.dve)
                        pb.op("scalar", lambda s: s.activation(
                            T["Pd"][:, :], T["Pd"][:, :], AF.Exp),
                            wait=[pb.dve.now()], inc=pb.act)
                        pdone = [pb.act.now()]
                        for kt in range(4):
                            for qs4 in range(4):
                                src2 = ("Pd" if kt == qs4 else
                                        ("Pv" if kt > qs4 else "Pv2"))
                                off = (kt * 128 if src2 == "Pd"
                                       else kt * 512 + qs4 * 128)
                                pb.op("tensor", lambda t, kb=kb, kt=kt, qs4=qs4,
                                      src2=src2, off=off, started=started: t.matmul(
                                    T["ps_c"][0:65, qs4 * 128:(qs4 + 1) * 128],
                                    vrec[:, kb * 4 + kt, :],
                                    T[src2][:, off:off + 128],
                                    start=(not started and kt == 0 and qs4 == 0),
                                    stop=False, skip_group_check=True),
                                    wait=(pdone if (kt == 0 and qs4 == 0) else ()),
                                    inc=pb.pe)
                        started = True
                lo, hi = qb - 5, qb + 4
                pb.op("vector", lambda v: v.memset(T["farc"][0:65, 0:1], 0.0),
                      wait=[pb.pe.now()], inc=pb.dve)
                if lo >= 0:
                    pb.op("vector", lambda v, lo=lo, vpref=vpref: v.tensor_add(
                        T["farc"][0:65, 0:1], T["farc"][0:65, 0:1],
                        vpref[0:65, lo:lo + 1]), inc=pb.dve)
                if hi < 15:
                    pb.op("vector", lambda v, vpref=vpref: v.tensor_add(
                        T["farc"][0:65, 0:1], T["farc"][0:65, 0:1],
                        vpref[0:65, 15:16]), inc=pb.dve)
                    pb.op("vector", lambda v, hi=hi, vpref=vpref: v.tensor_sub(
                        T["farc"][0:65, 0:1], T["farc"][0:65, 0:1],
                        vpref[0:65, hi:hi + 1]), inc=pb.dve)
                pb.op("scalar", lambda s: s.copy(T["farb"][0:65, 0:1],
                                                 T["farc"][0:65, 0:1]),
                      wait=[pb.dve.now()], inc=pb.act)
                pb.op("tensor", lambda t: t.transpose(
                    T["ps_tr"][0:1, 0:65], T["farb"][0:65, 0:1],
                    T["id_sb"][0:65, 0:65]), wait=[pb.act.now()], inc=pb.pe)
                pb.op("scalar", lambda s: s.copy(T["farrow"][0:1, 0:65],
                                                 T["ps_tr"][0:1, 0:65]),
                      wait=[pb.pe.now()], inc=pb.act)
                pb.op("tensor", lambda t: t.matmul(
                    T["ps_c"][0:65, 0:512], T["farrow"][0:1, 0:65],
                    T["ones_sb"][0:1, 0:512], start=False, stop=True,
                    skip_group_check=True),
                    wait=[pb.act.now()], inc=pb.pe)
                pb.op("vector", lambda v: v.reciprocal(T["lrow"][64:65, 0:512],
                                                       T["ps_c"][64:65, 0:512]),
                      wait=[pb.pe.now()], inc=pb.dve)
                pb.op("scalar", lambda s: s.copy(T["lrowb"][64:65, 0:512],
                                                 T["lrow"][64:65, 0:512]),
                      wait=[pb.dve.now()], inc=pb.act)
                pb.op("tensor", lambda t: t.matmul(
                    T["ps_ab"][0:64, 0:512], T["ones_sb"][64:65, 0:64],
                    T["lrowb"][64:65, 0:512], start=True, stop=True),
                    wait=[pb.act.now(), pb.dve.now()], inc=pb.pe)
                pb.op("scalar", lambda s: s.copy(T["osb"][0:64, 0:512],
                                                 T["ps_c"][0:64, 0:512]),
                      wait=[pb.pe.now()], inc=pb.act)
                pb.op("vector", lambda v: v.tensor_tensor(
                    T["ob0"][0:64, 0:512], T["osb"][0:64, 0:512],
                    T["ps_ab"][0:64, 0:512], ALU.mult),
                    wait=[pb.act.now(), pb.dma.now()], inc=pb.dve)
                pb.dmaop("sync", lambda s, h=h, qb=qb: s.dma_start(
                    out=T["oD"].ap()[h * 64:(h + 1) * 64,
                                     qb * 512:(qb + 1) * 512],
                    in_=T["ob0"][0:64, 0:512]), wait=[pb.dve.now()])
    attn_done = [pb.dma.now()]
    pb.dmaop("sync", lambda s: s.dma_start(
        out=oT[:, :, :], in_=T["oD"].ap().rearrange("(i p) t -> p i t", p=128)),
        wait=attn_done)
    attn_done = [pb.dma.now()]

    pb.dmaop("gpsimd", lambda g: g.dma_start(
        out=T["wbuf"][:, 0:2, :],
        in_=T["wo_i"].rearrange("k p m f -> p k (m f)", p=128)), wait=attn_done)
    wo_w = [pb.dmag.now()]
    for m in range(NE):
        for n in range(NKB):
            for k in range(2):
                pb.op("tensor", lambda t, m=m, n=n, k=k: t.matmul(
                    T["ps_ab"][:, 0:512], T["wbuf"][:, k, m * 128:(m + 1) * 128],
                    oT[:, k, n * 512:(n + 1) * 512], start=(k == 0), stop=(k == 1)),
                    wait=(attn_done + wo_w + [pb.act.now()] if (k == 0) else ()),
                    inc=pb.pe)
            pb.op("scalar", lambda s, m=m, n=n: s.copy(T["dxs"][:, :],
                                                       T["ps_ab"][:, 0:512]),
                  wait=[pb.pe.now(), pb.dma.now()], inc=pb.act)
            pb.dmaop("sync", lambda s, m=m, n=n: s.dma_start(
                out=T["ccR_in"].ap()[(n // 4) * D + m * 128:(n // 4) * D + (m + 1) * 128,
                                     (n % 4) * 512:(n % 4) * 512 + 512],
                in_=T["dxs"][:, :]), wait=[pb.act.now()])
    pb.op("gpsimd", lambda g: g.collective_compute(
        "ReduceScatter", ALU.add, replica_groups=GROUPS,
        ins=[T["ccR_in"].ap().opt()], outs=[T["ccR_out"].ap().opt()]),
        wait=[pb.dma.now(), pb.dmag.now()], inc=pb.cc, inck=None)
    pb.dmaop("sync", lambda s: s.dma_start(
        out=uT[:, :, :],
        in_=T["ccR_out"].ap().rearrange("(i p) t -> p i t", p=128)),
        wait=[pb.cc.now()])
    dx_done = [pb.dma.now()]

    for j in range(NT):
        for i in range(NE):
            pb.op("tensor", lambda t, j=j, i=i: t.transpose(
                T["ps_tr"][:, :], uT[:, i, j * 128:(j + 1) * 128], T["id_sb"][:, :]),
                wait=(dx_done + [pb.dve.now(), pb.act.now()]), inc=pb.pe)
            pb.op("scalar", lambda s, i=i: s.copy(
                T["un"][:, i * 128:(i + 1) * 128], T["ps_tr"][:, :]),
                wait=[pb.pe.now()], inc=pb.act)
        pb.dmaop("sync", lambda s, j=j: s.dma_start(
            out=T["xt"][:, :], in_=T["x_in"][j * 128:(j + 1) * 128, :]),
            wait=[pb.dve.now()])
        emit_ln_tile(pb, T["un"][:, :], T["un"][:, :], T["stats"][:, :],
                     [pb.act.now()])
        pb.op("vector", lambda v: v.tensor_add(T["xt"][:, :], T["xt"][:, :],
                                               T["un"][:, :]),
              wait=[pb.dve.now(), pb.dma.now()], inc=pb.dve)
        pb.dmaop("sync", lambda s, j=j: s.dma_start(
            out=T["x_out"][j * 128:(j + 1) * 128, :], in_=T["xt"][:, :]),
            wait=[pb.dve.now()])
    pb.op("sync", None, wait=[pb.dma.now()])


def sbuf_p3(nc, T, ar, ps=None):
    sb = lambda n, shp, dt: T.__setitem__(n, ar.take(shp, dt))
    sb("a1", [128, 16384], BF16)
    sb("a2", [128, 16384], BF16)
    sb("a3", [128, 16384], BF16)
    sb("latT", [128, 16384], BF16)
    sb("krT", [128, 8192], BF16)
    sb("vrec", [128, 64 * 65], BF16)
    sb("wbuf", [128, NE, D], BF16)
    sb("wupk_sb", [128, 2, 256], BF16)
    sb("wupv_sb", [128, 2, 256], BF16)
    sb("Pv", [128, 2048], BF16)
    sb("Pv2", [128, 2048], BF16)
    sb("Pd", [128, 512], BF16)
    sb("qp", [128, 512], BF16)
    sb("qm", [128, 512], BF16)
    sb("vs", [65, 16], F32)
    sb("vp2", [65, 16], F32)
    sb("farc", [65, 1], F32)
    sb("farb", [65, 1], BF16)
    sb("farrow", [1, 65], BF16)
    sb("lrow", [128, 512], F32)
    sb("lrowb", [128, 512], BF16)
    sb("osb", [64, 512], BF16)
    sb("ob0", [64, 512], BF16)
    sb("dxs", [128, 512], BF16)
    sb("rowqp_sb", [128, 512], BF16)
    sb("rowqm_sb", [128, 512], BF16)
    sb("rowkp_sb", [128, 512], BF16)
    sb("rowkm_sb", [128, 512], BF16)
    sb("creltab_sb", [128, 8], F32)
    sb("dconst_sb", [128, 128], F32)
    sb("selg_sb", [128, 4], F32)
    sb("stats", [128, 16], F32)
    sb("un", [128, D], BF16)
    sb("xt", [128, D], F32)
    sb("id_sb", [128, 128], BF16)
    sb("ones_sb", [128, 512], BF16)
    sb("cpool", [128, 8], F32)
    return psum_std(nc, T, split_ab=False, ps=ps)


def decls_p3(nc, T, xio=True):
    def inp(name, shape, dtype=F32):
        T[name] = nc.dram_tensor(name, list(shape), dtype, kind="ExternalInput").ap()

    if xio:
        inp("x_in", [TCORE, D])
        T["x_out"] = nc.dram_tensor("x_out", [TCORE, D], F32,
                                    kind="ExternalOutput").ap()
    inp("Wq_i", [D, D]); inp("Wk_i", [D, D]); inp("Wv_i", [D, D])
    inp("Wd_i", [D, 256])
    inp("wupk_i", [256, 256]); inp("wupv_i", [256, 256])
    inp("wo_i", [2, 128, 8, 128])
    inp("rowqp_i", [128, 512]); inp("rowqm_i", [128, 512])
    inp("rowkp_i", [128, 512]); inp("rowkm_i", [128, 512])
    inp("creltab_i", [128, 8]); inp("dconst_i", [128, 128]); inp("selg_i", [128, 4])
    T["ccQ_in"] = nc.dram_tensor("ccQ_in", [D, TCORE], BF16, kind="Internal")
    T["ccQ_out"] = nc.dram_tensor("ccQ_out", [4 * D, TCORE], BF16, kind="Internal")
    T["ccL_in"] = nc.dram_tensor("ccL_in", [256, TCORE], BF16, kind="Internal")
    T["ccL_out"] = nc.dram_tensor("ccL_out", [4 * 256, TCORE], BF16, kind="Internal")
    T["ccR_in"] = nc.dram_tensor("ccR_in", [4 * D, TCORE], BF16, kind="Internal")
    T["ccR_out"] = nc.dram_tensor("ccR_out", [D, TCORE], BF16, kind="Internal")
    T["oD"] = nc.dram_tensor("oD", [256, 8192], BF16, kind="Internal")
    decl_shared(nc, T)


def build_p3(limit=None):
    nc = bass.Bass("TRN2", target_bir_lowering=False)
    T = {}
    decls_p3(nc, T)
    sbuf_p3(nc, T, Arena(nc, "G", ARENA_N))

    pb = PB(nc)
    pb.limit = limit
    pb.cpool = T["cpool"][:, :]
    emit_phase3(pb, T)
    with nc.Block() as blk:
        pb.replay(blk)
    return nc


def host_inputs_p3(full, x_stream=None):
    lam = 0.01
    j = np.arange(512, dtype=np.float64)
    rowqp = bcast_np(np.exp(lam * j) / 32.0)
    rowqm = bcast_np(np.exp(-lam * j) / 32.0)
    rowkp = bcast_np(np.exp(lam * j))
    rowkm = bcast_np(np.exp(-lam * j))
    crel = bcast_np(np.exp(-512.0 * lam * np.arange(8)))
    pk = np.arange(128, dtype=np.float64)
    dconst = (np.exp(-lam * np.abs(pk[:, None] - pk[None, :])) / 32.0).astype(np.float32)
    Wq = np.asarray(full["L1_attn_W_q"], np.float32)
    Wk = np.asarray(full["L1_attn_W_k"], np.float32)
    Wv = np.asarray(full["L1_attn_W_v"], np.float32)
    Wd = np.asarray(full["L1_attn_W_down_kv"], np.float32)
    Wuk = np.asarray(full["L1_attn_W_up_k"], np.float32)
    Wuv = np.asarray(full["L1_attn_W_up_v"], np.float32)
    Wo = np.asarray(full["L1_attn_W_o"], np.float32)
    maps = []
    for c in range(8):
        b, g = c // 4, c % 4
        t0 = g * TCORE
        selg = np.zeros((128, 4), np.float32)
        selg[:, g] = 1.0
        maps.append(dict(
            **({"x_in": np.ascontiguousarray(x_stream[b, t0:t0 + TCORE])}
               if x_stream is not None else {}),
            Wq_i=Wq, Wk_i=Wk, Wv_i=Wv, Wd_i=Wd,
            wupk_i=np.ascontiguousarray(Wuk[:, g * 256:(g + 1) * 256]),
            wupv_i=np.ascontiguousarray(Wuv[:, g * 256:(g + 1) * 256]),
            wo_i=np.ascontiguousarray(
                Wo[g * 256:(g + 1) * 256].reshape(2, 128, 8, 128)),
            rowqp_i=rowqp, rowqm_i=rowqm, rowkp_i=rowkp, rowkm_i=rowkm,
            creltab_i=crel, dconst_i=dconst, selg_i=selg,
        ))
    return maps


NPASS = [6, 6, 6, 6, 6, 6, 6, 6, 6, 6, 4]


def emit_phase4(pb, T):
    cp = pb.cpool
    xlnT = T["xlnT4"].ap().rearrange("p (i t) -> p i t", t=TCORE)
    wup = T["wz4"].ap()[:, 0:6 * 8 * 512].rearrange("p (e i f) -> p e i f", e=6, i=8)
    wdn = T["wz4"].ap()[:, 6 * 8 * 512:6 * 8 * 512 + 6 * 4 * 1024].rearrange(
        "p (e k f) -> p e k f", e=6, k=4)
    wsu = T["wz4"].ap()[:, 0:8 * 2048].rearrange("p (i f) -> p i f", f=2048)
    wsd = T["wz4"].ap()[:, 8 * 2048:8 * 2048 + 16 * 1024].rearrange(
        "p (k f) -> p k f", f=1024)
    act6 = T["act6"].ap().rearrange("p (e k t) -> p e k t", e=6, k=4)
    acts = T["act6"].ap().rearrange("p (k t) -> p k t", t=512)
    outb = T["outb"].ap().rearrange("p (j f) -> p j f", f=D)

    pb.dmaop("sync", lambda s: s.dma_start(out=T["cpool"][:, :], in_=T["cpool_c"]))
    pb.dmaop("sync", lambda s: s.dma_start(out=T["id_sb"][:, :], in_=T["ident"]))
    pb.dmaop("sync", lambda s: s.dma_start(out=T["ones_sb"][:, :], in_=T["ones_c"][:, 0:128]))
    pb.dmaop("sync", lambda s: s.dma_start(out=T["cbias_sb"][:, :], in_=T["cbias_i"]))
    pb.dmaop("gpsimd", lambda g: g.dma_start(
        out=T["rw_sb"][:, :, :], in_=T["rwT_i"].rearrange("(i p) f -> p i f", p=128)))
    ld0 = [pb.dma.now(), pb.dmag.now()]

    for j in range(NT):
        pb.dmaop("sync", lambda s, j=j: s.dma_start(
            out=T["xt"][:, :], in_=T["x_in"][j * 128:(j + 1) * 128, :]),
            wait=[pb.dve.now()])
        emit_ln_tile(pb, T["xt"][:, :], T["un"][:, :], T["stats"][:, :],
                     ld0 + [pb.dma.now(), pb.act.now()])
        for i in range(NE):
            pb.op("tensor", lambda t, i=i: t.transpose(
                T["ps_tr"][:, :], T["un"][:, i * 128:(i + 1) * 128], T["id_sb"][:, :]),
                wait=[pb.dve.now(), pb.act.now()], inc=pb.pe)
            pb.op("scalar", lambda s, j=j, i=i: s.copy(
                xlnT[:, i, j * 128:(j + 1) * 128], T["ps_tr"][:, :]),
                wait=[pb.pe.now()], inc=pb.act)
    xlnT_done = [pb.act.now()]

    for j in range(NT):
        for i in range(NE):
            pb.op("tensor", lambda t, j=j, i=i: t.matmul(
                T["ps_s"][:, 0:64], xlnT[:, i, j * 128:(j + 1) * 128],
                T["rw_sb"][:, i, :], start=(i == 0), stop=(i == NE - 1)),
                wait=(xlnT_done + [pb.dve.now()] if i == 0 else ()), inc=pb.pe)
        S_ = T["rsc"]
        pb.op("scalar", lambda s, j=j: s.activation(S_[:, j, :], T["ps_s"][:, 0:64],
                                                    AF.Sigmoid),
              wait=[pb.pe.now()], inc=pb.act)
        pb.op("vector", lambda v, j=j: v.tensor_add(T["sfc"][:, :], S_[:, j, :],
                                                    T["cbias_sb"][:, :]),
              wait=[pb.act.now()], inc=pb.dve)
        for g8 in range(8):
            pb.op("vector", lambda v, g8=g8: v.max(
                out=T["m8"][:, :], in_=T["sfc"][:, g8 * 8:(g8 + 1) * 8]), inc=pb.dve)
            pb.op("vector", lambda v, g8=g8: v.tensor_add(
                T["gs"][:, g8:g8 + 1], T["m8"][:, 0:1], T["m8"][:, 1:2]), inc=pb.dve)
        pb.op("vector", lambda v: v.max(out=T["m8"][:, :], in_=T["gs"][:, :]),
              inc=pb.dve)
        pb.op("vector", lambda v: v.tensor_single_scalar(
            T["gm"][:, :], T["gs"][:, :], T["m8"][:, 3:4], ALU.is_ge), inc=pb.dve)
        for g8 in range(8):
            pb.op("vector", lambda v, g8=g8: v.tensor_scalar_mul(
                T["msk"][:, g8 * 8:(g8 + 1) * 8], T["sfc"][:, g8 * 8:(g8 + 1) * 8],
                T["gm"][:, g8:g8 + 1]), inc=pb.dve)
        pb.op("vector", lambda v: v.max(out=T["m8"][:, :], in_=T["msk"][:, :]),
              inc=pb.dve)
        pb.op("vector", lambda v: v.tensor_single_scalar(
            T["sel"][:, :], T["msk"][:, :], T["m8"][:, 7:8], ALU.is_ge), inc=pb.dve)
        pb.op("vector", lambda v, j=j: v.tensor_mul(T["sel"][:, :], T["sel"][:, :],
                                                    S_[:, j, :]), inc=pb.dve)
        pb.op("vector", lambda v: v.tensor_reduce(
            T["m8"][:, 0:1], T["sel"][:, :], AX.X, ALU.add), inc=pb.dve)
        pb.op("vector", lambda v: v.tensor_scalar_add(
            T["m8"][:, 0:1], T["m8"][:, 0:1], 1e-20), inc=pb.dve)
        pb.op("vector", lambda v: v.reciprocal(T["m8"][:, 1:2], T["m8"][:, 0:1]),
              inc=pb.dve)
        pb.op("vector", lambda v, j=j: v.tensor_scalar(
            T["Cr"][:, j, :], T["sel"][:, :], T["m8"][:, 1:2], 2.5,
            ALU.mult, ALU.mult), inc=pb.dve)
    rt_done = [pb.dve.now()]

    for j in range(NT):
        pb.op("tensor", lambda t, j=j: t.transpose(
            T["ps_tr"][0:64, 0:128], T["Cr"][:, j, :], T["id_sb"][:, :]),
            wait=(rt_done + [pb.act.now(), pb.dma.now()]), inc=pb.pe)
        pb.op("scalar", lambda s, j=j: s.copy(T["ctb"][0:64, :],
                                              T["ps_tr"][0:64, 0:128]),
              wait=[pb.pe.now()], inc=pb.act)
        pb.dmaop("sync", lambda s, j=j: s.dma_start(
            out=T["CTd"].ap()[:, j * 128:(j + 1) * 128],
            in_=T["ctb"][0:64, :]), wait=[pb.act.now()])
    ct_done = [pb.dma.now()]


    pb.op("vector", lambda v: v.memset(T["outb"].ap()[:, :], 0.0), inc=pb.dve)

    e0 = 0
    for pi, psz in enumerate(NPASS):
        wl_e = []
        for ei in range(psz):
            pb.dmaop("gpsimd", lambda g, e=e0 + ei, ei=ei: g.dma_start(
                out=wup[:, ei, :, :],
                in_=T["upw_i"][e].rearrange("(i p) f -> p i f", p=128)),
                wait=[pb.pe.now()] if ei == 0 else ())
            pb.dmaop("gpsimd", lambda g, e=e0 + ei, ei=ei: g.dma_start(
                out=wdn[:, ei, :, :],
                in_=T["dnw_i"][e].rearrange("(k p) f -> p k f", p=128)))
            wl_e.append(pb.dmag.now())
        wl = [pb.dmag.now()]
        for ch in range(4):
            for ei in range(psz):
                pb.dmaop("sync", lambda s, e=e0 + ei, ch=ch: s.dma_start(
                    out=T["crow"][0:1, :],
                    in_=T["CTd"].ap()[e:e + 1, ch * 512:(ch + 1) * 512]),
                    wait=ct_done + [pb.pe.now()])
                pb.op("tensor", lambda t: t.matmul(
                    T["ps_c"][:, 0:512], T["ones_sb"][0:1, 0:128], T["crow"][0:1, :],
                    start=True, stop=True),
                    wait=[pb.dma.now(), pb.act.now(), pb.dve.now()], inc=pb.pe)
                pb.op("scalar", lambda s: s.copy(T["cbc"][:, :], T["ps_c"][:, 0:512]),
                      wait=[pb.pe.now(), pb.dve.now()], inc=pb.act)
                rlv = T["rl"].ap().rearrange("p (d f) -> p d f", d=2)[:, ei % 2, :]
                rl4 = rlv.rearrange("p (k t) -> p k t", t=512)
                for mt in range(4):
                    for i in range(NE):
                        pb.op("tensor", lambda t, ei=ei, ch=ch, mt=mt, i=i: t.matmul(
                            T["ps_ab"][:, mt * 512:(mt + 1) * 512],
                            wup[:, ei, i, mt * 128:(mt + 1) * 128],
                            xlnT[:, i, ch * 512:(ch + 1) * 512],
                            start=(i == 0), stop=(i == NE - 1)),
                            wait=(wl_e[ei] if False else wl_e[ei],) and
                                 ([wl_e[ei]] + xlnT_done + [pb.dve.now()]
                                  if (mt == 0 and i == 0) else
                                  ([pb.act.now()] if i == 0 else ())), inc=pb.pe)
                    pb.op("scalar", lambda s, mt=mt, rlv=rlv: s.activation(
                        rlv[:, mt * 512:(mt + 1) * 512],
                        T["ps_ab"][:, mt * 512:(mt + 1) * 512], AF.Relu),
                        wait=[pb.pe.now()], inc=pb.act)
                pb.op("vector", lambda v, rlv=rlv: v.tensor_mul(rlv, rlv, rlv),
                      wait=[pb.act.now()], inc=pb.dve)
                pb.op("vector", lambda v, ei=ei, rl4=rl4: v.tensor_tensor(
                    act6[:, ei, :, :], rl4,
                    T["cbc"][:, :].unsqueeze(1).to_broadcast([128, 4, 512]),
                    ALU.mult), wait=[pb.dve.now()], inc=pb.dve)
            up_done = [pb.dve.now()]
            for mt in range(4):
                for n in range(2):
                    for ei in range(psz):
                        for k in range(4):
                            pb.op("tensor", lambda t, ei=ei, k=k, mt=mt, n=n,
                                  psz=psz: t.matmul(
                                T["ps_c"][:, 0:512],
                                act6[:, ei, k, mt * 128:(mt + 1) * 128],
                                wdn[:, ei, k, n * 512:(n + 1) * 512],
                                start=(ei == 0 and k == 0),
                                stop=(ei == psz - 1 and k == 3)),
                                wait=(up_done + [pb.dve.now(), pb.act.now()]
                                      if (ei == 0 and k == 0) else ()), inc=pb.pe)
                    pb.op("vector", lambda v, ch=ch, mt=mt, n=n: v.tensor_tensor(
                        outb[:, ch * 4 + mt, n * 512:(n + 1) * 512],
                        outb[:, ch * 4 + mt, n * 512:(n + 1) * 512],
                        T["ps_c"][:, 0:512], ALU.add), wait=[pb.pe.now()], inc=pb.dve)
        e0 += psz

    pb.dmaop("gpsimd", lambda g: g.dma_start(
        out=wsu[:, :, :], in_=T["shu_i"].rearrange("(i p) f -> p i f", p=128)),
        wait=[pb.pe.now()])
    pb.dmaop("gpsimd", lambda g: g.dma_start(
        out=wsd[:, :, :], in_=T["shd_i"].rearrange("(k p) f -> p k f", p=128)))
    wl = [pb.dmag.now()]
    for ch in range(4):
        for mt in range(16):
            for i in range(NE):
                pb.op("tensor", lambda t, ch=ch, mt=mt, i=i: t.matmul(
                    T["ps_ab"][:, (mt % 4) * 512:(mt % 4) * 512 + 512],
                    wsu[:, i, mt * 128:(mt + 1) * 128],
                    xlnT[:, i, ch * 512:(ch + 1) * 512],
                    start=(i == 0), stop=(i == NE - 1)),
                    wait=(wl + [pb.act.now(), pb.dve.now()]
                          if i == 0 else ()), inc=pb.pe)
            pb.op("scalar", lambda s, mt=mt: s.activation(
                T["rl"][:, (mt % 4) * 512:(mt % 4) * 512 + 512],
                T["ps_ab"][:, (mt % 4) * 512:(mt % 4) * 512 + 512], AF.Relu),
                wait=[pb.pe.now()], inc=pb.act)
            pb.op("vector", lambda v, mt=mt: v.tensor_mul(
                acts[:, mt, :], T["rl"][:, (mt % 4) * 512:(mt % 4) * 512 + 512],
                T["rl"][:, (mt % 4) * 512:(mt % 4) * 512 + 512]),
                wait=[pb.act.now()], inc=pb.dve)
        sh_done = [pb.dve.now()]
        for mt in range(4):
            for n in range(2):
                for k in range(16):
                    pb.op("tensor", lambda t, k=k, mt=mt, n=n: t.matmul(
                        T["ps_c"][:, 0:512], acts[:, k, mt * 128:(mt + 1) * 128],
                        wsd[:, k, n * 512:(n + 1) * 512],
                        start=(k == 0), stop=(k == 15)),
                        wait=(sh_done + [pb.dve.now(), pb.act.now()]
                              if k == 0 else ()), inc=pb.pe)
                pb.op("vector", lambda v, ch=ch, mt=mt, n=n: v.tensor_tensor(
                    outb[:, ch * 4 + mt, n * 512:(n + 1) * 512],
                    outb[:, ch * 4 + mt, n * 512:(n + 1) * 512],
                    T["ps_c"][:, 0:512], ALU.add), wait=[pb.pe.now()], inc=pb.dve)
    moe_done = [pb.dve.now()]

    for j in range(NT):
        pb.dmaop("sync", lambda s, j=j: s.dma_start(
            out=T["xt"][:, :], in_=T["x_in"][j * 128:(j + 1) * 128, :]),
            wait=[pb.dve.now()])
        emit_ln_tile(pb, outb[:, j, :], T["un"][:, :], T["stats"][:, :],
                     moe_done + [pb.dma.now()])
        pb.op("vector", lambda v: v.tensor_add(T["xt"][:, :], T["xt"][:, :],
                                               T["un"][:, :]),
              wait=[pb.dve.now(), pb.dma.now()], inc=pb.dve)
        pb.dmaop("sync", lambda s, j=j: s.dma_start(
            out=T["x_out"][j * 128:(j + 1) * 128, :], in_=T["xt"][:, :]),
            wait=[pb.dve.now()])
    pb.op("sync", None, wait=[pb.dma.now()])


def sbuf_p4(nc, T, ar, ps=None):
    sb = lambda n, shp, dt: T.__setitem__(n, ar.take(shp, dt))
    sb("xlnT4", [128, NE * TCORE], BF16)
    sb("wz4", [128, 6 * 8 * 512 + 6 * 4 * 1024], BF16)
    sb("outb", [128, NT * D], BF16)
    sb("act6", [128, 6 * 4 * 512], BF16)
    sb("rl", [128, 2 * 2048], BF16)
    sb("cbc", [128, 512], BF16)
    sb("crow", [1, 512], BF16)
    sb("rw_sb", [128, 8, 64], BF16)
    sb("rsc", [128, NT, 64], BF16)
    sb("Cr", [128, NT, 64], BF16)
    sb("sfc", [128, 64], BF16)
    sb("msk", [128, 64], BF16)
    sb("sel", [128, 64], BF16)
    sb("gs", [128, 8], F32)
    sb("gm", [128, 8], F32)
    sb("m8", [128, 8], F32)
    sb("cbias_sb", [128, 64], F32)
    sb("ctb", [64, 128], BF16)
    sb("stats", [128, 16], F32)
    sb("un", [128, D], BF16)
    sb("xt", [128, D], F32)
    sb("id_sb", [128, 128], BF16)
    sb("ones_sb", [128, 128], BF16)
    sb("cpool", [128, 8], F32)
    return psum_std(nc, T, split_ab=False, ps=ps)


def decls_p4(nc, T, xio=True):
    def inp(name, shape, dtype=F32):
        T[name] = nc.dram_tensor(name, list(shape), dtype, kind="ExternalInput").ap()

    if xio:
        inp("x_in", [TCORE, D])
        T["x_out"] = nc.dram_tensor("x_out", [TCORE, D], F32,
                                    kind="ExternalOutput").ap()
    inp("rwT_i", [D, 64])
    inp("cbias_i", [128, 64])
    inp("upw_i", [64, D, 512])
    inp("dnw_i", [64, 512, D])
    inp("shu_i", [D, 2048])
    inp("shd_i", [2048, D])
    T["CTd"] = nc.dram_tensor("CTd", [64, TCORE], BF16, kind="Internal")
    decl_shared(nc, T)


def build_p4(limit=None):
    nc = bass.Bass("TRN2", target_bir_lowering=False)
    T = {}
    decls_p4(nc, T)
    sbuf_p4(nc, T, Arena(nc, "G", ARENA_N))

    pb = PB(nc)
    pb.limit = limit
    pb.cpool = T["cpool"][:, :]
    emit_phase4(pb, T)
    with nc.Block() as blk:
        pb.replay(blk)
    return nc


def host_inputs_p4(full, x_stream=None):
    rwT = np.ascontiguousarray(np.asarray(full["L1_ffn_router_w"], np.float32).T)
    cbias = bcast_np(np.asarray(full["L1_ffn_correction_bias"], np.float32))
    upw = np.ascontiguousarray(np.asarray(full["L1_ffn_up_w"], np.float32))
    dnw = np.ascontiguousarray(np.asarray(full["L1_ffn_down_w"], np.float32))
    shu = np.ascontiguousarray(np.asarray(full["L1_ffn_shared_up_w"], np.float32))
    shd = np.ascontiguousarray(np.asarray(full["L1_ffn_shared_down_w"], np.float32))
    maps = []
    for c in range(8):
        b, q = c // 4, c % 4
        t0 = q * TCORE
        maps.append(dict(
            **({"x_in": np.ascontiguousarray(x_stream[b, t0:t0 + TCORE])}
               if x_stream is not None else {}),
            rwT_i=rwT, cbias_i=cbias, upw_i=upw, dnw_i=dnw, shu_i=shu, shd_i=shd))
    return maps


def sbuf_p1(nc, T, ar, ps=None):
    sb = lambda n, shp, dt: T.__setitem__(n, ar.take(shp, dt))
    sb("reads", [128, NT * D], BF16)
    sb("xln", [128, NT * D], BF16)
    sb("xlnT", [128, NE, TCORE + 1], BF16)
    sb("v_sb", [128, NT, D], BF16)
    sb("wt_sb", [128, NE, D], BF16)
    sb("wbuf", [128, NE, D], BF16)
    sb("stats", [128, NT, 16], F32)
    sb("gcol", [128, 16], F32)
    sb("gl", [128, 1], F32)
    sb("al", [128, 1], F32)
    sb("g_upd", [128, NQC], F32)
    sb("g_qb", [128, NQC], F32)
    sb("g_glob", [128, NT], F32)
    sb("g_pw", [128, 4], F32)
    sb("pw_mask", [128, 4], F32)
    sb("m_full", [128, NQC, 128], F32)
    sb("smt", [128, 128], BF16)
    sb("ev1", [128, D], F32)
    sb("xt", [128, D], F32)
    sb("zl", [128, D], BF16)
    sb("halo", [1, D + 32], F32)
    sb("id_sb", [128, 128], BF16)
    sb("cpool", [128, 8], F32)
    sb("idf_sb", [128, 128], F32)
    return psum_std(nc, T, split_ab=True, ps=ps)


def decls_p1(nc, T, xio=True):
    def inp(name, shape, dtype=F32):
        T[name] = nc.dram_tensor(name, list(shape), dtype, kind="ExternalInput").ap()

    if xio:
        T["x_out"] = nc.dram_tensor("x_out", [TCORE, D], F32,
                                    kind="ExternalOutput").ap()
    inp("x_in", [TCORE, D]); inp("xhalo", [1, D])
    inp("WwT", [D, D]); inp("WrT", [D, D])
    inp("dec_sc", [128, 1]); inp("la_sc", [128, 1])
    inp("e_upd", [128, NQC]); inp("e_qb", [128, NQC]); inp("e_glob", [128, NT])
    inp("e_m1", [128, 128]); inp("m_tri", [128, 128])
    inp("pexp", [128, 4]); inp("pmask", [128, 4])
    T["wdram"] = nc.dram_tensor("wdram", [TCORE + 1, D], BF16, kind="Internal")
    T["halod"] = nc.dram_tensor("halod", [1, D], BF16, kind="Internal")
    T["ccA_in"] = nc.dram_tensor("ccA_in", [D, D], BF16, kind="Internal")
    T["ccA_out"] = nc.dram_tensor("ccA_out", [4 * D, D], BF16, kind="Internal")
    decl_shared(nc, T)
    T["identf"] = nc.inline_tensor(np.eye(128, dtype=np.float32), name="identf").ap()


def build_p1(limit=None):
    nc = bass.Bass("TRN2", target_bir_lowering=False)
    T = {}
    decls_p1(nc, T)
    sbuf_p1(nc, T, Arena(nc, "G", ARENA_N))

    pb = PB(nc)
    pb.limit = limit
    emit_phase1(pb, T)
    with nc.Block() as blk:
        pb.replay(blk)
    return nc


def host_inputs_p1(full):
    x = np.asarray(full["L0_attn_out"], np.float32)
    WwT = np.ascontiguousarray(np.asarray(full["L0_attn_W_write"], np.float32).T)
    WrT = np.ascontiguousarray(np.asarray(full["L0_attn_W_read"], np.float32).T)
    dec = np.full((128, 1), np.float32(full["L0_attn_decay"]), np.float32)
    la = np.full((128, 1), np.float32(full["L0_attn_log_alpha"]), np.float32)
    p = np.arange(128, dtype=np.float32)[:, None]
    e_upd = np.concatenate([UB - 1 - (c * 128 + p) for c in range(NQC)], axis=1)
    e_qb = np.concatenate([(c * 128 + p) for c in range(NQC)], axis=1)
    e_glob = np.concatenate([(j * 128 + p) for j in range(NT)], axis=1)
    pk = np.arange(128, dtype=np.float32)
    e_m1 = pk[None, :] - 1.0 - pk[:, None]
    m_tri = (pk[None, :] > pk[:, None]).astype(np.float32)
    maps = []
    for c in range(8):
        b, q = c // 4, c % 4
        t0 = q * TCORE
        xs = np.ascontiguousarray(x[b, t0:t0 + TCORE])
        halo = (np.ascontiguousarray(x[b, t0 - 1:t0]) if q > 0
                else np.zeros((1, D), np.float32))
        pexp = np.zeros((128, 4), np.float32)
        pmask = np.zeros((128, 4), np.float32)
        for j in range(4):
            if j < q:
                pexp[:, j] = TCORE * (q - 1 - j)
                pmask[:, j] = 1.0
        maps.append(dict(x_in=xs, xhalo=halo, WwT=WwT, WrT=WrT, dec_sc=dec, la_sc=la,
                         e_upd=np.ascontiguousarray(e_upd),
                         e_qb=np.ascontiguousarray(e_qb),
                         e_glob=np.ascontiguousarray(e_glob),
                         e_m1=np.ascontiguousarray(e_m1), m_tri=m_tri,
                         pexp=pexp, pmask=pmask))
    return maps


def build_fused(limit=None):
    nc = bass.Bass("TRN2", target_bir_lowering=False)
    ar = Arena(nc, "G", ARENA_N)
    xcur = nc.dram_tensor("xcur", [TCORE, D], F32, kind="Internal")
    shared = {}
    decl_shared(nc, shared)
    pb = PB(nc)
    pb.limit = limit
    ps = None
    phases = ((decls_p1, sbuf_p1, emit_phase1),
              (decls_p2, sbuf_p2, emit_phase2),
              (decls_p3, sbuf_p3, emit_phase3),
              (decls_p4, sbuf_p4, emit_phase4))
    for i, (decls, sbufs, emit) in enumerate(phases):
        T = dict(shared)
        decls(nc, T, xio=False)
        if i > 0:
            T["x_in"] = xcur.ap()
        if i == 3:
            T["x_out"] = nc.dram_tensor("x_out", [TCORE, D], F32,
                                        kind="ExternalOutput").ap()
        else:
            T["x_out"] = xcur.ap()
        ar.reset()
        pb.phase = f"phase{i + 1}"
        ps = sbufs(nc, T, ar, ps)
        if i == 1:
            T["hbuf_v"] = T["wz"].ap()[:, 0:8 * HW].rearrange(
                "p (a b) -> p a b", b=HW)
        pb.cpool = T["cpool"][:, :]
        emit(pb, T)
        if i < 3:
            pb.barrier()
    with nc.Block() as blk:
        pb.replay(blk)
    return nc


def host_inputs_fused(inputs):
    m1 = host_inputs_p1(inputs)
    m2 = host_inputs_p2(inputs, None)
    m3 = host_inputs_p3(inputs, None)
    m4 = host_inputs_p4(inputs, None)
    maps = []
    for c in range(8):
        mp = {}
        for m in (m1[c], m2[c], m3[c], m4[c]):
            mp.update(m)
        maps.append(mp)
    return maps


def kernel(**inputs):
    nc = build_fused()
    maps = host_inputs_fused(inputs)
    res = run_bass_kernel_spmd(nc, maps, core_ids=list(range(8)))
    out = np.zeros((2, 8192, D), np.float32)
    for c in range(8):
        b, q = c // 4, c % 4
        out[b, q * TCORE:(q + 1) * TCORE] = res.results[c]["x_out"]
    return out


def kernel_unfused(**inputs):
    x = np.asarray(inputs["L0_attn_out"], np.float32)
    for builder, hoster in ((build_p1, None),
                            (build_p2, host_inputs_p2),
                            (build_p3, host_inputs_p3),
                            (build_p4, host_inputs_p4)):
        if hoster is None:
            nc = build_p1()
            maps = host_inputs_p1(inputs)
        else:
            nc = builder()
            maps = hoster(inputs, x)
        res = run_bass_kernel_spmd(nc, maps, core_ids=list(range(8)))
        x = np.zeros((2, 8192, D), np.float32)
        for c in range(8):
            b, q = c // 4, c % 4
            x[b, q * TCORE:(q + 1) * TCORE] = res.results[c]["x_out"]
    return x
